# version 10
# baseline (speedup 1.0000x reference)
import numpy as np
import ml_dtypes
from contextlib import ExitStack

import concourse.bass as bass
import concourse.mybir as mybir
from concourse.bass_utils import run_bass_kernel_spmd

F32 = mybir.dt.float32
BF16 = mybir.dt.bfloat16
AF = mybir.ActivationFunctionType
ALU = mybir.AluOpType
AX = mybir.AxisListType

B, T, D = 16, 2048, 1024
NCORES = 8
SEQ_PER_CORE = B // NCORES
NT = T // 128
DCH = D // 128
H, DH = 16, 64
DFF = 4096
EXP_OFF = 12.0
MASKVAL = -1e9

BF = lambda a: np.ascontiguousarray(a).astype(ml_dtypes.bfloat16)


def host_constants():
    d = 64
    inv_freq = 1.0 / (10000.0 ** (np.arange(0, d, 2, dtype=np.float64) / d))
    t = np.arange(T, dtype=np.float64)
    fr = np.outer(t, inv_freq)
    cos = np.cos(fr).T
    sin = np.sin(fr).T
    C = np.empty((64, T), dtype=np.float32)
    S = np.empty((64, T), dtype=np.float32)
    C[0::2] = cos; C[1::2] = cos
    S[0::2] = -sin; S[1::2] = sin
    C2 = np.concatenate([C, C], axis=0)
    S2 = np.concatenate([S, S], axis=0)
    P = np.zeros((128, 128), dtype=np.float32)
    for i in range(0, 128, 2):
        P[i, i + 1] = 1.0
        P[i + 1, i] = 1.0
    M = np.zeros((128, 128), dtype=np.float32)
    for k in range(128):
        M[k, :k] = MASKVAL
    return {"rope_c": C2, "rope_s": S2, "perm128": BF(P).astype(ml_dtypes.bfloat16),
            "cmask": M}


class Sem:
    def __init__(self, ctx, nc, name):
        self.h = ctx.enter_context(nc.semaphore(name))
        self.n = 0

    def inc(self, instr, k=1):
        instr.then_inc(self.h, k)
        self.n += k
        return self.n


class Gate:
    def __init__(self):
        self.seen = {}

    def wait(self, eng, sem, val=None):
        if val is None:
            val = sem.n
        if val <= 0:
            return
        e = getattr(eng, "engine", None)
        key = (str(e), sem.h.name)
        if self.seen.get(key, -1) >= val:
            return
        eng.wait_ge(sem.h, val)
        self.seen[key] = val


def _ap_span(ap):
    try:
        t = ap.tensor.name
    except Exception:
        return None
    off = ap.offset if isinstance(ap.offset, int) else 0
    ext = 0
    try:
        for step, count in ap.ap:
            if isinstance(step, int) and isinstance(count, int):
                ext += abs(step) * (count - 1)
            else:
                ext += 1 << 40
    except Exception:
        ext = 1 << 40
    return (t, off, off + ext + 1)


def _overlap(a, b):
    return a[0] == b[0] and a[1] < b[2] and b[1] < a[2]


class GEng:
    WRITE0 = {"tensor_copy", "tensor_tensor", "tensor_add", "tensor_sub",
              "tensor_mul", "tensor_max", "tensor_scalar", "tensor_single_scalar",
              "tensor_scalar_add", "tensor_scalar_sub", "tensor_scalar_mul",
              "tensor_scalar_max", "tensor_scalar_min", "tensor_relu",
              "tensor_reduce", "reduce_max", "reduce_sum", "scalar_tensor_tensor",
              "activation", "sign", "square", "sqrt", "mul", "copy", "add",
              "reciprocal", "bn_stats", "bn_aggr", "memset", "memzero",
              "select", "transpose", "iota", "affine_select", "stream_shuffle"}

    def __init__(self, eng, state):
        self._e = eng
        self._st = state

    def __getattr__(self, name):
        f = getattr(self._e, name)
        if name not in self.WRITE0:
            return f

        def wrapped(*args, **kw):
            aps = [a for a in list(args) + list(kw.values()) if isinstance(a, bass.AP)]
            spans = [s for s in (_ap_span(a) for a in aps) if s is not None]
            if spans:
                outs = spans[:1]
                ins = spans[1:]
                need = False
                for s in spans[:1]:
                    if any(_overlap(s, p) for p in self._st["w"]) or \
                       any(_overlap(s, p) for p in self._st["r"]):
                        need = True
                for s in ins:
                    if any(_overlap(s, p) for p in self._st["w"]):
                        need = True
                if need:
                    self._e.drain()
                    self._st["w"].clear()
                    self._st["r"].clear()
                self._st["w"].extend(outs)
                self._st["r"].extend(ins)
            return f(*args, **kw)
        return wrapped

    def barrier(self):
        self._e.drain()
        self._st["w"].clear()
        self._st["r"].clear()


class Tok:
    __slots__ = ("incs",)
    def __init__(self):
        self.incs = []

    def then_inc(self, sem_handle, k=1):
        self.incs.append((sem_handle, k))
        return self


class Rec:
    __slots__ = ("sched", "engine")

    def __init__(self, sched, engname):
        self.sched = sched
        self.engine = engname

    def __getattr__(self, name):
        def call(*a, **k):
            tok = Tok()
            self.sched.append((self.engine, name, a, k, tok))
            return tok
        return call


class P:
    def __init__(self, nc, ctx):
        self.nc, self.ctx = nc, ctx
        self.gate = Gate()
        self.est = {k: {"w": [], "r": []} for k in ("vector", "scalar", "gpsimd")}
        self.qX = Sem(ctx, nc, "qX")
        self.qH = Sem(ctx, nc, "qH")
        self.qW0 = Sem(ctx, nc, "qW0")
        self.qW1 = Sem(ctx, nc, "qW1")
        self.qSm = Sem(ctx, nc, "qSm")
        self.qC = Sem(ctx, nc, "qC")
        self.qL = [Sem(ctx, nc, "qL0"), Sem(ctx, nc, "qL1")]
        self.qP = [Sem(ctx, nc, "qP0"), Sem(ctx, nc, "qP1")]
        self.qD = Sem(ctx, nc, "qD")
        self.cPE = Sem(ctx, nc, "cPE")
        self.cACT = Sem(ctx, nc, "cACT")
        self.cDVE = Sem(ctx, nc, "cDVE")
        self.cPOOL = Sem(ctx, nc, "cPOOL")
        self.all_dma = [self.qX, self.qH, self.qW0, self.qW1, self.qSm,
                        self.qC, *self.qL, *self.qP, self.qD]
        self._sems = {}
        self.sched = []
        self.block = ctx.enter_context(nc.Block())

    def sem(self, key):
        if key not in self._sems:
            s = Sem(self.ctx, self.nc, f"q_{key}")
            self._sems[key] = s
            self.all_dma.append(s)
        return self._sems[key]

    def pe(self, fn):
        fn(Rec(self.sched, "tensor"), self.gate)

    def act(self, fn):
        fn(GEng(Rec(self.sched, "scalar"), self.est["scalar"]), self.gate)

    def dve(self, fn):
        fn(GEng(Rec(self.sched, "vector"), self.est["vector"]), self.gate)

    def pool(self, fn):
        fn(GEng(Rec(self.sched, "gpsimd"), self.est["gpsimd"]), self.gate)

    def sp(self, fn):
        fn(Rec(self.sched, "sync"), self.gate)

    def flush(self):
        engmap = {"tensor": self.block.tensor, "scalar": self.block.scalar,
                  "vector": self.block.vector, "gpsimd": self.block.gpsimd,
                  "sync": self.block.sync}
        for engname in ("tensor", "scalar", "vector", "gpsimd", "sync"):
            def run(e, engname=engname):
                for en, name, a, k, tok in self.sched:
                    if en != engname:
                        continue
                    ins = getattr(e, name)(*a, **k)
                    for h, kk in tok.incs:
                        ins.then_inc(h, kk)
            engmap[engname](run)

    @staticmethod
    def fence(geng, sem, k=1):
        geng.barrier()
        sem.inc(geng.nop(), k)


class Builder:

    SUBLAYERS = ["l0a", "l0f", "l0b", "l0f2", "l1a", "l1f", "l1b", "l1f2"]

    def __init__(self, lo, hi):
        self.lo, self.hi = lo, hi
        self.subs = self.SUBLAYERS[lo:hi]
        nc = bass.Bass("TRN2", target_bir_lowering=False)
        self.nc = nc
        self.ctx = ExitStack()
        self.dram_io()
        self.sbuf_alloc()
        self.psum_alloc()
        self.p = P(nc, self.ctx)
        self.emit()
        self.ctx.close()

    def dram_io(self):
        nc = self.nc
        ap = lambda name, shape, dt, kind: nc.dram_tensor(name, shape, dt, kind=kind).ap()
        self.x_in = ap("x", [SEQ_PER_CORE, T, D], F32, "ExternalInput")
        self.y_out = ap("y", [SEQ_PER_CORE, T, D], F32, "ExternalOutput")
        need = set(self.subs)
        W = {}
        def inp(name, shape):
            W[name] = ap(name, shape, F32, "ExternalInput")
        if "l0a" in need:
            inp("L0_attn_a_Wq", [D, D]); inp("L0_attn_a_Wk", [D, D])
            inp("L0_attn_a_Wv", [D, D]); inp("L0_attn_a_Wo", [D, D])
        if "l0f" in need or "l0f2" in need:
            inp("L0_ffn_w_up", [DFF, D]); inp("L0_ffn_w_down", [D, DFF])
        if "l0b" in need:
            inp("L0_attn_b_w_qkv", [3 * D, D]); inp("L0_attn_b_b_qkv", [3 * D])
            inp("L0_attn_b_w_o", [D, D]); inp("L0_attn_b_b_o", [D])
        if "l1a" in need:
            inp("L1_attn_a_Wqa", [D, 64]); inp("L1_attn_a_Wqb", [64, D])
            inp("L1_attn_a_Wkva", [D, 32]); inp("L1_attn_a_Wkvb", [32, 512])
            inp("L1_attn_a_Wout", [D, D]); inp("L1_attn_a_bout", [D])
        if "l1f" in need or "l1f2" in need:
            inp("L1_ffn_weights", [D, D]); inp("L1_ffn_scale", [D]); inp("L1_ffn_bias", [D])
        if "l1b" in need:
            inp("L1_attn_b_norm1_g", [D]); inp("L1_attn_b_norm1_b", [D])
            inp("L1_attn_b_qkv_w", [3 * D, D]); inp("L1_attn_b_proj_w", [D, D])
            inp("L1_attn_b_proj_b", [D])
            inp("L1_attn_b_norm2_g", [D]); inp("L1_attn_b_norm2_b", [D])
            inp("L1_attn_b_fc1_w", [DFF, D]); inp("L1_attn_b_fc1_b", [DFF])
            inp("L1_attn_b_fc2_w", [D, DFF]); inp("L1_attn_b_fc2_b", [D])
        self.W = W
        if "l1a" in need:
            self.rope_c = ap("rope_c", [128, T], F32, "ExternalInput")
            self.rope_s = ap("rope_s", [128, T], F32, "ExternalInput")
            self.perm128 = ap("perm128", [128, 128], BF16, "ExternalInput")
        if "l1a" in need or "l0a" in need:
            self.cmask = ap("cmask", [128, 128], F32, "ExternalInput")
        sc = lambda name, shape, dt: nc.dram_tensor(name, shape, dt).ap()
        self.xhat_d = sc("xhat_d", [T, D], BF16)
        self.rows_d = sc("rows_d", [8, T], F32)
        self.cache = {}
        if "l0b" in need:
            self.cache["qkv0"] = sc("qkv0_bf", [3 * D, D], BF16)
            self.cache["wo0"] = sc("wo0_bf", [D, D], BF16)
        if "l0f" in need or "l0f2" in need:
            self.cache["wup"] = sc("wup_bf", [DFF, D], BF16)
            self.cache["wdown"] = sc("wdown_bf", [D, DFF], BF16)
        if "l1b" in need:
            self.cache["qkv1"] = sc("qkv1_bf", [3 * D, D], BF16)
            self.cache["proj1"] = sc("proj1_bf", [D, D], BF16)
            self.cache["fc1"] = sc("fc1_bf", [DFF, D], BF16)
            self.cache["fc2"] = sc("fc2_bf", [D, DFF], BF16)
            self.cache["beff_d"] = sc("beff_d", [8, 3 * D], F32)
        if "l1f" in need or "l1f2" in need:
            self.cache["l1f"] = sc("l1f_bf", [D, D], BF16)

    def sbuf_alloc(self):
        nc, ctx = self.nc, self.ctx
        t = lambda name, shape, dt: ctx.enter_context(nc.sbuf_tensor(name, shape, dt))
        self.x_res = t("x_res", [128, NT, D], F32)
        self.pool = t("pool", [128, 4096], F32)
        self.stats = t("stats", [128, 128], F32)
        self.stats2 = t("stats2", [128, 64], F32)
        self.bnb = t("bnb", [128, 16], F32)
        self.cpool = t("cpool", [128, 16], F32)
        self.arena = t("arena", [128, 63744], BF16)
        self.cmask_sb = t("cmask_sb", [128, 128], F32)
        self.zb = self.pool[:, 0:1024]
        self.bb = self.pool[:, 1024:2048]
        self.bb2 = self.pool[:, 2048:3072]
        self.z2row = self.pool[0:2, 3072:4096]
        self.grow = self.pool[0:1, 3072:4096]
        A = self.arena
        off = 0
        def take(n):
            nonlocal off
            lo = off; off += n
            return lo
        self.o_xhatf = take(8 * T)
        self.o_osp = take(4 * T)
        self.o_q = take(4 * T)
        self.o_k = take(4 * T)
        self.o_v = take(16 * 520)
        self.o_w = take(8 * 512)
        self.o_w2 = take(8 * 512)
        self.o_misc = take(2048)
        self.o_pt = take(4096)
        assert off <= 63744, off

    def psum_alloc(self):
        nc, ctx = self.nc, self.ctx
        self.psA = ctx.enter_context(nc.psum_tensor("psA", [128, 2048], F32))
        self.psB = ctx.enter_context(nc.psum_tensor("psB", [128, 2048], F32))

    def av(self, base, tile, width=T, rows=128, row0=0):
        return self.arena[row0:row0 + rows, base + tile * width: base + (tile + 1) * width]

    def emit(self):
        import os
        p = self.p
        self.dev_stage = os.environ.get("DEV_STAGE", "")
        self.smalls = self.ctx.enter_context(self.nc.sbuf_tensor("smalls", [128, 64], F32))
        self.wslot_free = {}
        self.cache_qg = 0
        self.onesrow = self.ctx.enter_context(self.nc.sbuf_tensor("onesrow", [1, 128], F32))
        self.onesbf = self.ctx.enter_context(self.nc.sbuf_tensor("onesbf", [1, 64], BF16))
        self.growb = self.arena[0:1, self.o_pt + 2048:self.o_pt + 3072]
        self.init_consts()
        self.prep_weights()
        self.prep_sems = [(s, s.n) for s in (p.qP[0], p.qP[1], p.qC, p.qL[0],
                                             p.qL[1], p.qSm)]
        self.prep_eng = [(p.cDVE, p.cDVE.n), (p.cACT, p.cACT.n)]
        for seq in range(SEQ_PER_CORE):
            self.seq = seq
            self.load_x(seq)
            if self.dev_stage == "copy":
                self.store_y(seq)
                continue
            if self.dev_stage == "ln":
                self.ln_phase("plain")
                self.dump_arena(self.o_xhatf, seq)
                continue
            if self.dev_stage == "wload":
                self.ln_phase("plain")
                p2 = self.p
                wsrc = self.W["L0_attn_a_Wq"]
                def g(gp, gate):
                    p2.sem("w0g").inc(gp.dma_start(
                        out=self.arena[:, self.o_w:self.o_w + 4096]
                        .rearrange("p (c o) -> p c o", c=8),
                        in_=wsrc[:, 0:512].rearrange("(c p) o -> p c o", p=128)), 16)
                self.p.pool(g)
                self.dump_arena(self.o_w, seq, 4096)
                continue
            if self.dev_stage == "vones":
                self.emit_vones()
                self.dump_arena(self.o_v, seq, 8320)
                continue
            if self.dev_stage == "qproj":
                self.ln_phase("plain")
                wsrc = self.W["L0_attn_a_Wq"]
                p2 = self.p
                def g(gp, gate):
                    gate.wait(gp, p2.cPE, self.wslot_free.get(self.o_w, 0))
                    p2.sem("w0g").inc(gp.dma_start(
                        out=self.arena[:, self.o_w:self.o_w + 4096]
                        .rearrange("p (c o) -> p c o", c=8),
                        in_=wsrc[:, 0:512].rearrange("(c p) o -> p c o", p=128)), 16)
                p2.pool(g)
                wb = lambda c: self.arena[:, self.o_w + c * 512:self.o_w + (c + 1) * 512]
                self.proj_feature_major(wb, (p2.sem("w0g"), p2.sem("w0g").n), self.o_q, "l0a", 0, 0)
                self.dump_arena(self.o_q, seq, 8192)
                continue
            if self.dev_stage == "vproj":
                self.ln_phase("plain")
                self.emit_vones()
                wsrc = self.W["L0_attn_a_Wv"]
                p2 = self.p
                def g(gp, gate):
                    gate.wait(gp, p2.cPE, self.wslot_free.get(self.o_w, 0))
                    p2.sem("w0g").inc(gp.dma_start(
                        out=self.arena[:, self.o_w:self.o_w + 4096]
                        .rearrange("p (c o) -> p c o", c=8),
                        in_=wsrc[:, 0:512].rearrange("(c p) o -> p c o", p=128)), 16)
                p2.pool(g)
                wb = lambda c: self.arena[:, self.o_w + c * 512:self.o_w + (c + 1) * 512]
                self.proj_v_token_major(wb, (p2.sem("w0g"), p2.sem("w0g").n), "l0a", 0)
                self.dump_arena(self.o_v, seq, 8320)
                continue
            if self.dev_stage in ("qkv", "qkvk", "qkvv", "heads"):
                kind = {"l0a": "l0a", "l0b": "l0b", "l1a": "l1a"}[self.subs[0]]
                self.sub = self.subs[0]
                causal = kind != "l0b"
                scale = 1.0 / 32.0 if kind == "l0b" else 0.125
                self.ln_phase("plain")
                if kind == "l0b":
                    self.emit_attn_setup_l0b()
                elif kind == "l1a":
                    self.emit_attn_setup_l1a()
                self.emit_vones()
                self.attn_qkv(kind, 0)
                if self.dev_stage == "qkv":
                    self.dump_arena(self.o_q, seq, 8192)
                elif self.dev_stage == "qkvk":
                    self.dump_arena(self.o_k, seq, 8192 if kind != "l1a" else 4096)
                elif self.dev_stage == "qkvv":
                    self.dump_arena(self.o_v, seq, 8320)
                else:
                    self.qkv_ready = (self.p.cACT.n, self.p.cDVE.n)
                    for hl in range(8):
                        self.attn_head(kind, 0, hl, causal, scale)
                    self.dump_arena(self.o_osp, seq, 8192)
                continue
            for sub in self.subs:
                self.sub = sub
                if sub == "l0a":
                    self.emit_attn(dict(kind="l0a", causal=True, scale=0.125))
                elif sub == "l0b":
                    self.emit_attn(dict(kind="l0b", causal=False, scale=1.0 / 32.0))
                elif sub == "l1a":
                    self.emit_attn(dict(kind="l1a", causal=True, scale=0.125))
                elif sub in ("l0f", "l0f2"):
                    self.emit_l0ffn()
                elif sub in ("l1f", "l1f2"):
                    self.emit_l1ffn()
                elif sub == "l1b":
                    self.emit_l1b()
            self.store_y(seq)
        self.p.flush()

    def init_consts(self):
        p = self.p
        cp = self.cpool
        def f(v, g):
            v.memset(cp[:, 0:1], -EXP_OFF)
            v.memset(cp[:, 1:2], 1e-5)
            v.memset(cp[:, 2:3], 1e-8)
            v.memset(cp[:, 3:4], 1e-6)
            v.memset(cp[:, 4:5], 0.0)
            v.memset(cp[:, 5:6], 1.0)
            v.memset(self.onesrow[:, :], 1.0)
            v.memset(self.onesbf[:, :], 1.0)
            v.barrier()
            p.cDVE.inc(v.engine_nop())
        p.dve(f)
        self.c_negoff = cp[:, 0:1]
        self.c_lneps = cp[:, 1:2]
        self.c_rmseps = cp[:, 2:3]
        self.c_attneps = cp[:, 3:4]
        self.c_zero = cp[:, 4:5]
        self.c_ones = cp[:, 5:6]
        self.cmask_ready = 0
        if hasattr(self, "cmask"):
            def s(sy, gate):
                p.sem("mask").inc(sy.dma_start(self.cmask_sb[:, :], self.cmask[:, :]), 16)
            p.sp(s)
            self.cmask_ready = p.sem("mask").n

    def prep_weights(self):
        p, W = self.p, self.W
        plain = []
        if "qkv0" in self.cache:
            plain += [(W["L0_attn_b_w_qkv"], self.cache["qkv0"]),
                      (W["L0_attn_b_w_o"], self.cache["wo0"])]
        if "qkv1" in self.cache:
            plain += [(W["L1_attn_b_qkv_w"], self.cache["qkv1"]),
                      (W["L1_attn_b_proj_w"], self.cache["proj1"]),
                      (W["L1_attn_b_fc1_w"], self.cache["fc1"]),
                      (W["L1_attn_b_fc2_w"], self.cache["fc2"])]
        def g(gp, gate):
            for src, dst in plain:
                p.qC.inc(gp.dma_start(out=dst[:, :], in_=src[:, :]), 16)
        p.pool(g)
        self.cache_qg = p.qC.n
        if "wup" in self.cache:
            self.prep_ternary(W["L0_ffn_w_up"], self.cache["wup"], DFF, D)
            self.prep_ternary(W["L0_ffn_w_down"], self.cache["wdown"], D, DFF)
        if "l1f" in self.cache:
            self.prep_sign(W["L1_ffn_weights"], W["L1_ffn_scale"], self.cache["l1f"])

    def prep_ternary(self, src, dst, rows, cols):
        p = self.p
        ncc = cols // 1024
        n = (rows // 128) * ncc
        ldA = self.pool[:, 0:1024]; ldB = self.pool[:, 1024:2048]
        absb = self.pool[:, 2048:3072]; sgnb = self.pool[:, 3072:4096]
        acc = self.smalls[:, 0:1]; thr = self.smalls[:, 1:2]
        red = self.smalls[:, 2:3]
        outb = self.arena[:, self.o_w:self.o_w + 2048]
        chunk = lambda i: (src[(i // ncc) * 128:(i // ncc) * 128 + 128,
                               (i % ncc) * 1024:(i % ncc) * 1024 + 1024])
        dchunk = lambda i: (dst[(i // ncc) * 128:(i // ncc) * 128 + 128,
                                (i % ncc) * 1024:(i % ncc) * 1024 + 1024])
        ql0 = [p.qL[0].n, p.qL[1].n]
        dv0, ac0, pe0 = p.cDVE.n, p.cACT.n, p.cPE.n
        def g1(gp, gate):
            for i in range(n):
                gate.wait(gp, p.cDVE, dv0 + (i - 1) if i >= 2 else dv0)
                p.qL[i % 2].inc(gp.dma_start(out=(ldA if i % 2 == 0 else ldB), in_=chunk(i)), 16)
        p.pool(g1)
        def v1(v, gate):
            for i in range(n):
                gate.wait(v, p.qL[i % 2], 16 * (ql0[i % 2] // 16 + i // 2 + 1))
                v.tensor_reduce(red, (ldA if i % 2 == 0 else ldB), AX.X, ALU.add,
                                apply_absolute_value=True)
                if i == 0:
                    ins = v.tensor_copy(acc, red)
                else:
                    ins = v.tensor_add(acc, acc, red)
                p.cDVE.inc(ins)
        p.dve(v1)
        def t1(t, gate):
            gate.wait(t, p.cDVE, dv0 + n)
            gate.wait(t, p.cACT, ac0)
            p.cPE.inc(t.matmul(self.psA[0:1, 0:1], self.c_ones, acc,
                               start=True, stop=True))
        p.pe(t1)
        def a1(a, gate):
            gate.wait(a, p.cPE, pe0 + 1)
            p.cACT.inc(a.mul(self.smalls[0:1, 3:4], self.psA[0:1, 0:1],
                             0.5 / float(rows * cols)))
        p.act(a1)
        def t2(t, gate):
            gate.wait(t, p.cACT, ac0 + 1)
            p.cPE.inc(t.matmul(self.psA[0:128, 4:5], self.onesrow[:, :],
                               self.smalls[0:1, 3:4], start=True, stop=True))
        p.pe(t2)
        def v2(v, gate):
            gate.wait(v, p.cPE, pe0 + 2)
            p.cDVE.inc(v.tensor_copy(thr, self.psA[:, 4:5]))
        p.dve(v2)
        ql1 = [p.qL[0].n, p.qL[1].n]
        qp0 = [p.qP[0].n, p.qP[1].n]
        dv1, ac1 = p.cDVE.n, p.cACT.n
        def g2(gp, gate):
            gate.wait(gp, p.cDVE, dv1)
            for i in range(n):
                if i >= 2:
                    gate.wait(gp, p.cACT, ac1 + (i - 1))
                p.qL[i % 2].inc(gp.dma_start(out=(ldA if i % 2 == 0 else ldB), in_=chunk(i)), 16)
        p.pool(g2)
        def a2(a, gate):
            for i in range(n):
                gate.wait(a, p.qL[i % 2], 16 * (ql1[i % 2] // 16 + i // 2 + 1))
                if i >= 1:
                    gate.wait(a, p.cDVE, dv1 + i)
                a.activation(absb, (ldA if i % 2 == 0 else ldB), AF.Abs)
                a.sign(sgnb, (ldA if i % 2 == 0 else ldB))
                p.fence(a, p.cACT)
        p.act(a2)
        def v3(v, gate):
            for i in range(n):
                ob = outb[:, (i % 2) * 1024:(i % 2) * 1024 + 1024]
                gate.wait(v, p.cACT, ac1 + i + 1)
                gate.wait(v, p.qP[i % 2], qp0[i % 2] + 16 * (i // 2) if i >= 2
                          else qp0[i % 2])
                v.tensor_scalar(absb, absb, thr, None, ALU.is_gt)
                p.cDVE.inc(v.tensor_mul(ob, absb, sgnb))
        p.dve(v3)
        def s2(sy, gate):
            for i in range(n):
                ob = outb[:, (i % 2) * 1024:(i % 2) * 1024 + 1024]
                gate.wait(sy, p.cDVE, dv1 + i + 1)
                p.qP[i % 2].inc(sy.dma_start(out=dchunk(i), in_=ob), 16)
        p.sp(s2)

    def prep_sign(self, wsrc, ssrc, dst):
        p = self.p
        ldA = self.pool[:, 0:1024]; ldB = self.pool[:, 1024:2048]
        tmp = self.pool[:, 2048:3072]
        scl = self.smalls[:, 8:16]
        outb = self.arena[:, self.o_w:self.o_w + 2048]
        ql0 = [p.qL[0].n, p.qL[1].n]
        qp0 = [p.qP[0].n, p.qP[1].n]
        qsm0 = p.qSm.n
        dv0, ac0 = p.cDVE.n, p.cACT.n
        def g(gp, gate):
            gate.wait(gp, p.cDVE, dv0)
            p.qSm.inc(gp.dma_start(out=scl, in_=ssrc.rearrange("(c p) -> p c", p=128),
                                   allow_slow_non_contiguous=True), 16)
            for r in range(8):
                gate.wait(gp, p.cDVE, dv0 + 2 * (r - 1) if r >= 2 else dv0)
                p.qL[r % 2].inc(gp.dma_start(out=(ldA if r % 2 == 0 else ldB),
                                             in_=wsrc[r * 128:(r + 1) * 128, :]), 16)
        p.pool(g)
        def v(vv, gate):
            for r in range(8):
                gate.wait(vv, p.qSm, qsm0 + 16)
                gate.wait(vv, p.qL[r % 2], 16 * (ql0[r % 2] // 16 + r // 2 + 1))
                gate.wait(vv, p.qP[r % 2], qp0[r % 2] + 16 * (r // 2) if r >= 2
                          else qp0[r % 2])
                vv.tensor_scalar(tmp, (ldA if r % 2 == 0 else ldB), 0.0, None, ALU.is_ge)
                vv.tensor_scalar(tmp, tmp, 2.0, -1.0, ALU.mult, ALU.add)
                ob = outb[:, (r % 2) * 1024:(r % 2) * 1024 + 1024]
                p.cDVE.inc(vv.tensor_scalar_mul(ob, tmp, scl[:, r:r + 1]), 2)
        p.dve(v)
        def s(sy, gate):
            for r in range(8):
                gate.wait(sy, p.cDVE, dv0 + 2 * (r + 1))
                p.qP[r % 2].inc(sy.dma_start(out=dst[r * 128:(r + 1) * 128, :],
                                             in_=(outb[:, (r % 2) * 1024:(r % 2) * 1024 + 1024])), 16)
        p.sp(s)

    def load_x(self, seq):
        p = self.p
        def f(sy, gate):
            gate.wait(sy, p.cDVE); gate.wait(sy, p.cACT); gate.wait(sy, p.cPE)
            p.qX.inc(sy.dma_start(
                self.x_res[:, :, :],
                self.x_in[seq].rearrange("(n p) e -> p n e", p=128)), 16)
        p.sp(f)

    def dump_arena(self, base, seq, n=16384):
        p = self.p
        def f(gp, gate):
            for s_ in p.all_dma:
                gate.wait(gp, s_)
            gate.wait(gp, p.cACT); gate.wait(gp, p.cDVE); gate.wait(gp, p.cPE)
            p.qD.inc(gp.dma_start(
                out=self.y_out[seq].rearrange("(x p) e -> p x e", p=128)[:, :, 0:n // 16],
                in_=self.arena[:, base:base + n].rearrange("p (x e) -> p x e", x=16)), 16)
            gp.wait_ge(p.qD.h, p.qD.n)
        p.pool(f)

    def store_y(self, seq):
        if getattr(self, "dev_stage", "") not in ("", "copy"):
            return
        p = self.p
        def f(sy, gate):
            gate.wait(sy, p.cDVE); gate.wait(sy, p.cACT)
            p.qX.inc(sy.dma_start(
                self.y_out[seq].rearrange("(n p) e -> p n e", p=128),
                self.x_res[:, :, :]), 16)
            sy.wait_ge(p.qX.h, p.qX.n)
        p.sp(f)

    def slot_guard(self, eng, gate, slots, newsem):
        for s_, n_ in self.prep_sems:
            gate.wait(eng, s_, n_)
        if not hasattr(self, "slot_state"):
            self.slot_state = {}
        for sl in slots:
            st = self.slot_state.get(sl)
            if st is not None:
                cons, lsem = st
                for csem, ccnt in cons:
                    gate.wait(eng, csem, ccnt)
                gate.wait(eng, lsem)
        gate.wait(eng, newsem)

    def slot_loaded(self, slots, newsem):
        if not hasattr(self, "slot_state"):
            self.slot_state = {}
        for sl in slots:
            self.slot_state[sl] = ([], newsem)

    def slot_consumed(self, slots, sem):
        if not hasattr(self, "slot_state"):
            self.slot_state = {}
        for sl in slots:
            st = self.slot_state.get(sl)
            if st is None:
                st = ([], self.p.sem("wnull"))
            self.slot_state[sl] = (st[0] + [(sem, sem.n)], st[1])

    def wait_xhatf(self, eng, gate, first_half):
        sems = self.xhatf_half if first_half else self.xhatf_sems
        for s_, n_ in sems:
            gate.wait(eng, s_, n_)

    def ps_wait(self, eng, gate, tensor_name, banks):
        for b in banks:
            k = (tensor_name, b)
            if k in getattr(self, "bankfree", {}):
                sem, cnt = self.bankfree[k]
                gate.wait(eng, sem, cnt)

    def ps_set(self, tensor_name, banks, sem):
        if not hasattr(self, "bankfree"):
            self.bankfree = {}
        for b in banks:
            self.bankfree[(tensor_name, b)] = (sem, sem.n)

    def ln_phase(self, mode="plain", gmul=None, badd=None):
        p, st = self.p, self.stats
        def v_stats(v, gate):
            for s_ in p.all_dma:
                gate.wait(v, s_)
            for t in range(NT):
                v.bn_stats(self.bnb[:, 0:6], self.x_res[:, t, 0:512])
                v.bn_stats(self.bnb[:, 6:12], self.x_res[:, t, 512:1024])
                v.bn_aggr(st[:, 2 * t:2 * t + 2], self.bnb[:, 0:12])
            p.fence(v, p.cDVE)
        p.dve(v_stats)
        dv_stats = p.cDVE.n
        varv = st[:, 1:33:2]
        meanv = st[:, 0:32:2]
        def a_post(a, gate):
            gate.wait(a, p.cDVE, dv_stats)
            a.activation(st[:, 32:48], varv, AF.Ln, bias=self.c_lneps)
            a.activation(st[:, 48:64], st[:, 32:48], AF.Exp, scale=-0.5)
            if mode == "rms":
                a.activation(st[:, 112:128], varv, AF.Ln)
                a.activation(st[:, 112:128], st[:, 112:128], AF.Exp, scale=0.5)
            p.fence(a, p.cACT)
        p.act(a_post)
        ac_post = p.cACT.n
        def v_post(v, gate):
            gate.wait(v, p.cACT, ac_post)
            v.tensor_mul(st[:, 96:112], meanv, st[:, 48:64])
            v.tensor_scalar_mul(st[:, 96:112], st[:, 96:112], -1.0)
            if mode == "plain":
                v.tensor_copy(st[:, 64:80], st[:, 48:64])
                v.tensor_copy(st[:, 80:96], st[:, 96:112])
            elif mode == "rms":
                v.tensor_mul(st[:, 112:128], st[:, 112:128], st[:, 48:64])
                v.tensor_scalar(st[:, 112:128], st[:, 112:128], self.c_rmseps,
                                None, ALU.add)
                v.reciprocal(st[:, 112:128], st[:, 112:128])
                v.tensor_mul(st[:, 64:80], st[:, 48:64], st[:, 112:128])
                v.tensor_mul(st[:, 80:96], st[:, 96:112], st[:, 112:128])
            elif mode == "l1b":
                v.tensor_mul(st[:, 112:128], varv, st[:, 48:64])
                v.tensor_mul(st[:, 112:128], st[:, 112:128], st[:, 48:64])
            p.fence(v, p.cDVE)
        p.dve(v_post)
        dv_post = p.cDVE.n
        if mode == "l1b":
            def a_l1b(a, gate):
                gate.wait(a, p.cDVE, dv_post)
                a.activation(st[:, 112:128], st[:, 112:128], AF.Ln, bias=self.c_lneps)
                a.activation(st[:, 112:128], st[:, 112:128], AF.Exp, scale=-0.5)
                p.fence(a, p.cACT)
            p.act(a_l1b)
            ac_l1b = p.cACT.n
            def v_l1b(v, gate):
                gate.wait(v, p.cACT, ac_l1b)
                v.tensor_mul(st[:, 64:80], st[:, 48:64], st[:, 112:128])
                v.tensor_mul(st[:, 80:96], st[:, 96:112], st[:, 112:128])
                p.fence(v, p.cDVE)
            p.dve(v_l1b)
            dv_post = p.cDVE.n
        bufs = [self.arena[:, self.o_pt + i * 1024: self.o_pt + (i + 1) * 1024]
                for i in range(2)]
        xs = [p.sem("xs0"), p.sem("xs1"), p.sem("xs2"), p.sem("xs3")]
        xs0 = [s.n for s in xs]
        ssem = lambda t: (t // 8) * 2 + (t % 2)
        sval = lambda t: xs0[ssem(t)] + 16 * (((t % 8) // 2) + 1)
        ac_hat0 = p.cACT.n
        def a_hat(a, gate):
            gate.wait(a, p.cDVE, dv_post)
            for t in range(NT):
                if t >= 2:
                    gate.wait(a, xs[ssem(t - 2)], sval(t - 2))
                if gmul is not None and t >= 2:
                    gate.wait(a, p.cDVE, dv_post + t - 1)
                p.cACT.inc(a.activation(bufs[t % 2], self.x_res[:, t, :], AF.Identity,
                                        bias=st[:, 80 + t:81 + t],
                                        scale=st[:, 64 + t:65 + t]))
        p.act(a_hat)
        def v_g(v, gate):
            for t in range(NT):
                gate.wait(v, p.cACT, ac_hat0 + t + 1)
                if t >= 2:
                    gate.wait(v, xs[ssem(t - 2)], sval(t - 2))
                ins = v.tensor_mul(bufs[t % 2], bufs[t % 2], gmul)
                if badd is not None:
                    ins = v.tensor_add(bufs[t % 2], bufs[t % 2], badd)
                p.cDVE.inc(ins)
        if gmul is not None:
            p.dve(v_g)
        half_holder = {}
        def g_store(gp, gate):
            gate.wait(gp, p.cACT, ac_hat0); gate.wait(gp, p.cDVE, dv_post)
            for t in range(NT):
                if gmul is None:
                    gate.wait(gp, p.cACT, ac_hat0 + t + 1)
                else:
                    gate.wait(gp, p.cDVE, dv_post + t + 1)
                xs[ssem(t)].inc(gp.dma_start(out=self.xhat_d[t * 128:(t + 1) * 128, :],
                                             in_=bufs[t % 2]), 16)
        p.pool(g_store)
        def s_tr(sy, gate):
            gate.wait(sy, xs[0], xs0[0] + 64); gate.wait(sy, xs[1], xs0[1] + 64)
            qh2 = p.sem("xhTA")
            for c in range(DCH):
                qh2.inc(sy.dma_start(self.av(self.o_xhatf, c)[:, 0:1024],
                                     self.xhat_d[0:1024, c * 128:(c + 1) * 128],
                                     transpose=True), 16)
            half_holder["h"] = [(qh2, qh2.n), (xs[0], xs0[0] + 64),
                                (xs[1], xs0[1] + 64)]
            gate.wait(sy, xs[2]); gate.wait(sy, xs[3])
            for c in range(DCH):
                p.qH.inc(sy.dma_start(self.av(self.o_xhatf, c)[:, 1024:2048],
                                      self.xhat_d[1024:2048, c * 128:(c + 1) * 128],
                                      transpose=True), 16)
        p.sp(s_tr)
        self.xhatf_half = half_holder["h"]
        self.xhatf_ready = p.qH.n
        self.xhatf_sems = [(p.qH, p.qH.n), (p.sem("xhTA"), p.sem("xhTA").n)]

    def build_bcast(self, dst_pool_ap, src_row_ap, width, gate_sem=None):
        p = self.p
        nch = (width + 511) // 512
        def t(tt, gate):
            if gate_sem is not None:
                gate.wait(tt, gate_sem[0], gate_sem[1])
            self.ps_wait(tt, gate, "psB", [2, 3])
            for i in range(nch):
                w = min(512, width - i * 512)
                p.cPE.inc(tt.matmul(
                    self.psB[:, 1024 + i * 512:1024 + i * 512 + w],
                    self.onesrow[:, :], src_row_ap[:, i * 512:i * 512 + w],
                    start=True, stop=True))
        p.pe(t)
        pe_n = p.cPE.n
        def a(aa, gate):
            gate.wait(aa, p.cPE, pe_n)
            aa.copy(dst_pool_ap[:, 0:width], self.psB[:, 1024:1024 + width])
            p.fence(aa, p.cACT)
        p.act(a)
        self.ps_set("psB", [2, 3], p.cACT)

    def emit_attn(self, v):
        kind, causal, scale = v["kind"], v["causal"], v["scale"]
        p = self.p
        self.ln_phase("plain")
        if kind == "l0b":
            self.emit_attn_setup_l0b()
        elif kind == "l1a":
            self.emit_attn_setup_l1a()
        self.emit_vones()
        for hf in range(2):
            self.attn_qkv(kind, hf)
            self.qkv_ready = (p.cACT.n, p.cDVE.n)
            for hl in range(8):
                self.attn_head(kind, hf, hl, causal, scale)
            self.attn_outproj(kind, hf)

    def emit_attn_setup_l0b(self):
        p = self.p
        if True:
            def g(gp, gate):
                gate.wait(gp, p.cACT); gate.wait(gp, p.cPE)
                p.qSm.inc(gp.dma_start(out=self.smalls[:, 16:40],
                                       in_=self.W["L0_attn_b_b_qkv"]
                                       .rearrange("(c p) -> p c", p=128),
                                       allow_slow_non_contiguous=True), 16)
                p.qSm.inc(gp.dma_start(out=self.pool[0:1, 3072:4096],
                                       in_=self.W["L0_attn_b_b_qkv"][None, 2048:3072]), 16)
            p.pool(g)
            self.build_bcast(self.bb, self.pool[0:1, 3072:4096], 1024,
                             gate_sem=(p.qSm, p.qSm.n))
            def g2(gp, gate):
                gate.wait(gp, p.cPE)
                p.qSm.inc(gp.dma_start(out=self.pool[0:1, 3072:4096],
                                       in_=self.W["L0_attn_b_b_o"][None, :]), 16)
            p.pool(g2)
            self.build_bcast(self.bb2, self.pool[0:1, 3072:4096], 1024,
                             gate_sem=(p.qSm, p.qSm.n))
    def emit_attn_setup_l1a(self):
        p = self.p
        if True:
            def g(gp, gate):
                gate.wait(gp, p.cACT); gate.wait(gp, p.cPE); gate.wait(gp, p.cDVE)
                p.qSm.inc(gp.dma_start(out=self.pool[0:1, 3072:4096],
                                       in_=self.W["L1_attn_a_bout"][None, :]), 16)
                p.qSm.inc(gp.dma_start(
                    out=self.arena[:, self.o_misc:self.o_misc + 512]
                    .rearrange("p (c o) -> p c o", c=8),
                    in_=self.W["L1_attn_a_Wqa"].rearrange("(c p) o -> p c o", p=128)), 16)
                p.qSm.inc(gp.dma_start(
                    out=self.arena[:, self.o_misc + 512:self.o_misc + 768]
                    .rearrange("p (c o) -> p c o", c=8),
                    in_=self.W["L1_attn_a_Wkva"].rearrange("(c p) o -> p c o", p=128)), 16)
                p.qSm.inc(gp.dma_start(
                    out=self.arena[0:32, self.o_misc + 768:self.o_misc + 1280],
                    in_=self.W["L1_attn_a_Wkvb"][:, :]), 16)
                p.qSm.inc(gp.dma_start(
                    out=self.arena[:, self.o_misc + 1792:self.o_misc + 1920],
                    in_=self.perm128[:, :]), 16)
            p.pool(g)
            self.build_bcast(self.bb, self.pool[0:1, 3072:4096], 1024,
                             gate_sem=(p.qSm, p.qSm.n))
            qg_n = p.qSm.n
            def vv(vvv, gate):
                gate.wait(vvv, p.qSm, qg_n)
                w3 = self.arena[0:32, self.o_misc + 768:self.o_misc + 1280] \
                    .rearrange("p (h d two) -> p h two d", h=4, two=2)
                vvv.tensor_copy(self.arena[0:32, self.o_misc + 1280:self.o_misc + 1536]
                                .rearrange("p (h d) -> p h d", h=4), w3[:, :, 0, :])
                vvv.tensor_copy(self.arena[0:32, self.o_misc + 1536:self.o_misc + 1792]
                                .rearrange("p (h d) -> p h d", h=4), w3[:, :, 1, :])
                p.fence(vvv, p.cDVE)
            p.dve(vv)
            self.l1a_repack_n = p.cDVE.n
    def emit_vones(self):
        p = self.p
        def vones(vvv, gate):
            gate.wait(vvv, p.cPE)
            vvv.memset(self.arena[:, self.o_v:self.o_v + 16 * 520], 1.0)
            p.fence(vvv, p.cDVE)
        p.dve(vones)
        self.vones_n = p.cDVE.n

    def attn_qkv(self, kind, hf):
        p = self.p
        xf = lambda c: self.av(self.o_xhatf, c)
        if kind in ("l0a", "l0b"):
            wnames = {"l0a": ["L0_attn_a_Wq", "L0_attn_a_Wk", "L0_attn_a_Wv"]}
            for pi, proj in enumerate(("q", "k", "v")):
                wslot = self.o_w if pi % 2 == 0 else self.o_w2
                wbuf = lambda c, s=wslot: self.arena[:, s + c * 512: s + (c + 1) * 512]
                pe_before = p.cPE.n
                sl = "0" if wslot == self.o_w else "1"
                qw = p.sem("w" + sl + ("g" if kind == "l0a" else "s"))
                if kind == "l0a":
                    wsrc = self.W[wnames["l0a"][pi]]
                    def g(gp, gate, wsrc=wsrc, wslot=wslot, qw=qw):
                        self.slot_guard(gp, gate, [wslot], qw)
                        qw.inc(gp.dma_start(
                            out=self.arena[:, wslot:wslot + 4096]
                            .rearrange("p (c o) -> p c o", c=8),
                            in_=wsrc[:, hf * 512:(hf + 1) * 512]
                            .rearrange("(c p) o -> p c o", p=128)), 16)
                    p.pool(g)
                    wready = (qw, qw.n)
                else:
                    rowbase = pi * 1024 + hf * 512
                    def s(sy, gate, rowbase=rowbase, wslot=wslot, qw=qw):
                        self.slot_guard(sy, gate, [wslot], qw)
                        gate.wait(sy, p.qC, self.cache_qg)
                        for c in range(8):
                            qw.inc(sy.dma_start(
                                self.arena[:, wslot + c * 512: wslot + (c + 1) * 512],
                                self.cache["qkv0"][rowbase:rowbase + 512,
                                                   c * 128:(c + 1) * 128],
                                transpose=True), 16)
                    p.sp(s)
                    wready = (qw, qw.n)
                self.slot_loaded([wslot], qw)
                if proj in ("q", "k"):
                    dstbase = self.o_q if proj == "q" else self.o_k
                    self.proj_feature_major(wbuf, wready, dstbase, kind, pi, hf)
                    self.slot_consumed([wslot], p.cACT)
                else:
                    self.proj_v_token_major(wbuf, wready, kind, hf)
                    self.slot_consumed([wslot], p.cDVE)
        else:
            self.l1a_qkv(hf)

    def proj_feature_major(self, wbuf, wready, dstbase, kind, pi, hf):
        p = self.p
        xf = lambda c: self.av(self.o_xhatf, c)
        for tc in range(4):
            for oc in range(4):
                bank = (tc * 4 + oc) % 4
                def t(tt, gate, oc=oc, tc=tc, bank=bank):
                    self.wait_xhatf(tt, gate, tc < 2)
                    gate.wait(tt, wready[0], wready[1])
                    self.ps_wait(tt, gate, "psA", [bank])
                    for c in range(8):
                        ins = tt.matmul(self.psA[:, bank * 512:(bank + 1) * 512],
                                        wbuf(c)[:, oc * 128:(oc + 1) * 128],
                                        xf(c)[:, tc * 512:(tc + 1) * 512],
                                        start=(c == 0), stop=(c == 7))
                    p.cPE.inc(ins)
                p.pe(t)
                pe_n = p.cPE.n
                def a(aa, gate, oc=oc, tc=tc, bank=bank, pe_n=pe_n):
                    gate.wait(aa, p.cPE, pe_n)
                    dst = self.arena[:, dstbase + oc * T + tc * 512:
                                     dstbase + oc * T + tc * 512 + 512]
                    if kind == "l0b":
                        bcol = pi * 8 + hf * 4 + oc
                        p.cACT.inc(aa.activation(dst, self.psA[:, bank * 512:(bank + 1) * 512],
                                                 AF.Identity, bias=self.smalls[:, 16 + bcol:17 + bcol]))
                    else:
                        p.cACT.inc(aa.copy(dst, self.psA[:, bank * 512:(bank + 1) * 512]))
                p.act(a)
                self.ps_set("psA", [bank], p.cACT)

    def proj_v_token_major(self, wbuf, wready, kind, hf):
        p = self.p
        xf = lambda c: self.av(self.o_xhatf, c)
        for kt in range(NT):
            bank = kt % 4
            def t(tt, gate, kt=kt, bank=bank):
                self.wait_xhatf(tt, gate, kt < 8)
                gate.wait(tt, wready[0], wready[1])
                gate.wait(tt, p.cDVE, self.vones_n)
                self.ps_wait(tt, gate, "psB", [bank])
                for c in range(8):
                    ins = tt.matmul(self.psB[:, bank * 512:(bank + 1) * 512],
                                    xf(c)[:, kt * 128:(kt + 1) * 128],
                                    wbuf(c),
                                    start=(c == 0), stop=(c == 7))
                p.cPE.inc(ins)
            p.pe(t)
            pe_n = p.cPE.n
            dst = self.arena[:, self.o_v + kt * 520:self.o_v + (kt + 1) * 520] \
                .rearrange("p (h c) -> p h c", h=8)[:, :, 0:64]
            def vv(vvv, gate, kt=kt, bank=bank, pe_n=pe_n, dst=dst):
                gate.wait(vvv, p.cPE, pe_n)
                if kind == "l0b":
                    p.cDVE.inc(vvv.scalar_tensor_tensor(
                        dst, self.psB[:, bank * 512:(bank + 1) * 512], 1.0,
                        self.bb[:, hf * 512:hf * 512 + 512], ALU.mult, ALU.add))
                else:
                    p.cDVE.inc(vvv.tensor_copy(dst, self.psB[:, bank * 512:(bank + 1) * 512]))
            p.dve(vv)
            self.ps_set("psB", [bank], p.cDVE)

    def attn_head(self, kind, hf, hl, causal, scale):
        for qh in range(2):
            self.attn_unit(kind, hf, hl, causal, scale, qh)

    def attn_unit(self, kind, hf, hl, causal, scale, qh):
        p = self.p
        qtile, qrow = hl // 2, (hl % 2) * 64
        if kind == "l1a":
            ktile, krow, vslot = hl // 4, qrow, hl // 4
        else:
            ktile, krow, vslot = hl // 2, (hl % 2) * 64, hl
        Kf = self.arena[krow:krow + 64, self.o_k + ktile * T:self.o_k + (ktile + 1) * T]
        Qf = self.arena[qrow:qrow + 64, self.o_q + qtile * T:self.o_q + (qtile + 1) * T]
        ptile = [self.arena[:, self.o_pt + i * 1024:self.o_pt + (i + 1) * 1024]
                 for i in range(2)]
        pair = hl // 2
        q_lo = qh * 1024
        iters = []
        for kt in range(NT):
            q0 = max(q_lo, kt * 128) if causal else q_lo
            if q0 >= q_lo + 1024:
                continue
            iters.append((kt, q0, q_lo + 1024 - q0))
        ng = len(iters)
        last_kt = iters[-1][0]
        if not hasattr(self, "sslot"):
            self.sslot = 0
        pe0, ac0, dv0 = p.cPE.n, p.cACT.n, p.cDVE.n
        nmm = lambda w: 1 if w <= 512 else 2
        def bank_parts(kt, q0, w):
            out = []
            for b in range(q0 // 512, (q0 + w + 511) // 512):
                s, e = max(q0, b * 512), min(q0 + w, (b + 1) * 512)
                lk = min(NT - 1, 4 * b + 3) if causal else NT - 1
                out.append((s, e, kt == lk))
            return out
        smark = {}
        gmark = {g: ac0 + g + 1 for g in range(ng)}
        cnt = pe0
        sched = [("S", 0)]
        for g in range(ng):
            if g + 1 < ng:
                sched.append(("S", g + 1))
            sched.append(("AV", g))
        slot_of = {}
        for op, g in sched:
            kt, q0, w = iters[g]
            if op == "S":
                slot_of[g] = (self.sslot % 2)
                self.sslot += 1
                cnt += nmm(w)
                smark[g] = cnt
            else:
                for (s, e, stop) in bank_parts(kt, q0, w):
                    if stop:
                        cnt += 1
        pe_av = cnt
        diag = {g: (causal and iters[g][1] == iters[g][0] * 128 and
                    iters[g][0] * 128 >= q_lo) for g in range(ng)}
        dmark = {}
        j = 0
        for g in range(ng):
            if diag[g]:
                j += 1
                dmark[g] = dv0 + j

        def v_mask(v, gate):
            gate.wait(v, p.sem("mask"), self.cmask_ready)
            for g in range(ng):
                if not diag[g]:
                    continue
                c0 = slot_of[g] * 1024
                gate.wait(v, p.cPE, smark[g])
                p.cDVE.inc(v.tensor_add(self.psA[:, c0:c0 + 128],
                                        self.psA[:, c0:c0 + 128],
                                        self.cmask_sb[:, :]))
        if causal and any(diag.values()):
            p.dve(v_mask)

        def t_s(tt, gate):
            self.wait_xhatf(tt, gate, False)
            gate.wait(tt, p.cACT, self.qkv_ready[0])
            gate.wait(tt, p.cDVE, self.qkv_ready[1])
            def emit_S(g):
                kt, q0, w = iters[g]
                par = slot_of[g]
                self.ps_wait(tt, gate, "psA", [par * 2, par * 2 + 1])
                if g >= 2:
                    gate.wait(tt, p.cACT, gmark[g - 2])
                for s0 in range(0, w, 512):
                    sw = min(512, w - s0)
                    p.cPE.inc(tt.matmul(
                        self.psA[:, par * 1024 + s0:par * 1024 + s0 + sw],
                        Kf[:, kt * 128:(kt + 1) * 128],
                        Qf[:, q0 + s0:q0 + s0 + sw], start=True, stop=True))
            def emit_AV(g):
                kt, q0, w = iters[g]
                par = slot_of[g]
                if g == 0:
                    self.ps_wait(tt, gate, "psB", [2 * qh, 2 * qh + 1])
                gate.wait(tt, p.cACT, gmark[g])
                for (s, e, stop) in bank_parts(kt, q0, w):
                    ins = tt.matmul(self.psB[0:65, s:e],
                                    self.arena[:, self.o_v + kt * 520 + vslot * 65:
                                               self.o_v + kt * 520 + vslot * 65 + 65],
                                    ptile[par][:, s - q0:e - q0],
                                    start=(kt == 0), stop=stop)
                    if stop:
                        p.cPE.inc(ins)
            emit_S(0)
            for g in range(ng):
                if g + 1 < ng:
                    emit_S(g + 1)
                emit_AV(g)
        p.pe(t_s)
        assert p.cPE.n == pe_av, (p.cPE.n, pe_av)

        def a_exp(a, gate):
            for g in range(ng):
                kt, q0, w = iters[g]
                par = slot_of[g]
                gate.wait(a, p.cPE, smark[g])
                if g in dmark:
                    gate.wait(a, p.cDVE, dmark[g])
                p.cACT.inc(a.activation(ptile[par][:, 0:w],
                                        self.psA[:, par * 1024:par * 1024 + w],
                                        AF.Exp, bias=self.c_negoff, scale=scale))
        p.act(a_exp)
        self.ps_set("psA", [0, 1, 2, 3], p.cACT)
        def a_r(a, gate):
            gate.wait(a, p.cPE, pe_av)
            a.activation(self.grow[:, 0:1024],
                         self.psB[64:65, q_lo:q_lo + 1024], AF.Ln)
            p.cACT.inc(a.activation(self.growb[:, :], self.grow[:, 0:1024],
                                    AF.Exp, scale=-1.0))
        p.act(a_r)
        ac_n = p.cACT.n
        def t_z(tt, gate, ac_n=ac_n):
            gate.wait(tt, p.cACT, ac_n)
            self.ps_wait(tt, gate, "psA", [2, 3])
            p.cPE.inc(tt.matmul(self.psA[0:64, 1024:1536], self.onesbf[:, :],
                                self.growb[:, 0:512], start=True, stop=True))
            p.cPE.inc(tt.matmul(self.psA[0:64, 1536:2048], self.onesbf[:, :],
                                self.growb[:, 512:1024], start=True, stop=True))
        p.pe(t_z)
        pe_n = p.cPE.n
        dst = self.arena[qrow:qrow + 64,
                         self.o_osp + pair * T + q_lo:self.o_osp + pair * T + q_lo + 1024]
        def v_m(v, gate, dst=dst, pe_n=pe_n):
            gate.wait(v, p.cPE, pe_n)
            v.tensor_copy(self.zb[0:64, :], self.psA[0:64, 1024:2048])
            p.cDVE.inc(v.tensor_mul(dst, self.psB[0:64, q_lo:q_lo + 1024],
                                    self.zb[0:64, :]))
        p.dve(v_m)
        self.ps_set("psA", [2, 3], p.cDVE)
        self.ps_set("psB", [2 * qh, 2 * qh + 1], p.cDVE)

    def attn_outproj(self, kind, hf):
        p = self.p
        wslot = self.o_w
        qwo = p.sem("w0s" if kind == "l0b" else "w0g")
        if kind == "l0b":
            def s(sy, gate):
                self.slot_guard(sy, gate, [wslot], qwo)
                gate.wait(sy, p.qC, self.cache_qg)
                for j in range(4):
                    qwo.inc(sy.dma_start(
                        self.arena[:, wslot + j * 1024:wslot + (j + 1) * 1024],
                        self.cache["wo0"][0:1024, (hf * 4 + j) * 128:
                                          (hf * 4 + j) * 128 + 128],
                        transpose=True), 16)
            p.sp(s)
            self.slot_loaded([wslot], qwo)
            wready = (qwo, qwo.n)
        else:
            wname = "L0_attn_a_Wo" if kind == "l0a" else "L1_attn_a_Wout"
            def g(gp, gate):
                self.slot_guard(gp, gate, [wslot], qwo)
                qwo.inc(gp.dma_start(
                    out=self.arena[:, wslot:wslot + 4096]
                    .rearrange("p (c e) -> p c e", c=4),
                    in_=self.W[wname][hf * 512:(hf + 1) * 512, :]
                    .rearrange("(c p) e -> p c e", p=128)), 16)
            p.pool(g)
            self.slot_loaded([wslot], qwo)
            wready = (qwo, qwo.n)
        self.attn_o_done = p.cDVE.n
        if kind == "l1a" and hf == 1:
            def g2(gp, gate):
                gate.wait(gp, p.cPE); gate.wait(gp, p.cACT)
                p.qSm.inc(gp.dma_start(out=self.pool[0:1, 3072:4096],
                                       in_=self.W["L1_attn_a_bout"][None, :]), 16)
            p.pool(g2)
            self.build_bcast(self.bb, self.pool[0:1, 3072:4096], 1024,
                             gate_sem=(p.qSm, p.qSm.n))
        bb_ready = p.cACT.n
        for tt in range(NT):
            banks = [0, 1] if tt % 2 == 0 else [2, 3]
            cb = banks[0] * 512
            def t(t_, gate, tt=tt, banks=banks, cb=cb):
                gate.wait(t_, wready[0], wready[1])
                gate.wait(t_, p.cDVE, self.attn_o_done)
                self.ps_wait(t_, gate, "psA", banks)
                for eh in range(2):
                    for j in range(4):
                        ins = t_.matmul(
                            self.psA[:, cb + eh * 512:cb + eh * 512 + 512],
                            self.arena[:, self.o_osp + j * T + tt * 128:
                                       self.o_osp + j * T + tt * 128 + 128],
                            self.arena[:, wslot + j * 1024 + eh * 512:
                                       wslot + j * 1024 + eh * 512 + 512],
                            start=(j == 0), stop=(j == 3))
                    p.cPE.inc(ins)
            p.pe(t)
            pe_n = p.cPE.n
            extras = (kind == "l0b" and hf == 0)
            if extras:
                def a_h(aa, gate, tt=tt):
                    gate.wait(aa, p.cDVE)
                    p.cACT.inc(aa.activation(self.zb, self.x_res[:, tt, :], AF.Identity,
                                             bias=self.stats[:, 80 + tt:81 + tt],
                                             scale=self.stats[:, 64 + tt:65 + tt]))
                p.act(a_h)
                ac_n = p.cACT.n
            def v_e(v, gate, tt=tt, cb=cb, pe_n=pe_n, extras=extras):
                gate.wait(v, p.cPE, pe_n)
                if extras:
                    gate.wait(v, p.cACT, ac_n)
                ins = v.tensor_add(self.x_res[:, tt, :], self.x_res[:, tt, :],
                                   self.psA[:, cb:cb + 1024])
                if extras:
                    v.tensor_add(self.x_res[:, tt, :], self.x_res[:, tt, :], self.zb)
                    ins = v.tensor_add(self.x_res[:, tt, :], self.x_res[:, tt, :],
                                       self.bb2)
                if kind == "l1a" and hf == 1:
                    gate.wait(v, p.cACT, bb_ready)
                    ins = v.tensor_add(self.x_res[:, tt, :], self.x_res[:, tt, :],
                                       self.bb)
                p.cDVE.inc(ins)
            p.dve(v_e)
            self.ps_set("psA", banks, p.cDVE)
        self.slot_consumed([wslot], p.cDVE)

    def l1a_qkv(self, hf):
        p = self.p
        xf = lambda c: self.av(self.o_xhatf, c)
        om = self.o_misc
        qlat = self.arena[0:64, self.o_w2:self.o_w2 + 2048]
        kvlat = self.arena[0:32, self.o_w2 + 2048:self.o_w2 + 4096]
        wqa = lambda c: self.arena[:, om + c * 64:om + (c + 1) * 64]
        wkva = lambda c: self.arena[:, om + 512 + c * 32:om + 512 + (c + 1) * 32]
        wkvb_k = self.arena[0:32, om + 1280:om + 1536]
        wkvb_v = self.arena[0:32, om + 1536:om + 1792]
        perm_sb = self.arena[:, om + 1792:om + 1920]
        if hf == 0:
            for which, outap, lat, nlat in (("q", qlat, wqa, 64), ("kv", kvlat, wkva, 32)):
                for tc in range(4):
                    bank = tc % 4
                    qsm_n = p.qSm.n
                    def t_l(tt, gate, outap=outap, lat=lat, nlat=nlat, tc=tc,
                            bank=bank, qsm_n=qsm_n):
                        self.wait_xhatf(tt, gate, tc < 2)
                        gate.wait(tt, p.qSm, qsm_n)
                        self.ps_wait(tt, gate, "psA", [bank])
                        for c in range(8):
                            ins = tt.matmul(self.psA[0:nlat, bank * 512:bank * 512 + 512],
                                            lat(c), xf(c)[:, tc * 512:(tc + 1) * 512],
                                            start=(c == 0), stop=(c == 7))
                        p.cPE.inc(ins)
                    p.pe(t_l)
                    pe_n = p.cPE.n
                    def a_l(aa, gate, outap=outap, nlat=nlat, tc=tc, bank=bank, pe_n=pe_n):
                        gate.wait(aa, p.cPE, pe_n)
                        p.cACT.inc(aa.copy(outap[:, tc * 512:(tc + 1) * 512],
                                           self.psA[0:nlat, bank * 512:bank * 512 + 512]))
                    p.act(a_l)
                    self.ps_set("psA", [bank], p.cACT)
        qwb = p.sem("wqb")
        def g_w(gp, gate):
            self.slot_guard(gp, gate, [self.o_w], qwb)
            qwb.inc(gp.dma_start(out=self.arena[0:64, self.o_w:self.o_w + 512],
                                 in_=self.W["L1_attn_a_Wqb"][:, hf * 512:(hf + 1) * 512]), 16)
        p.pool(g_w)
        self.slot_loaded([self.o_w], qwb)
        wqb_n = qwb.n
        for oc in range(4):
            for tc in range(4):
                bank = (oc * 4 + tc) % 4
                def t_q(tt, gate, oc=oc, tc=tc, bank=bank):
                    gate.wait(tt, qwb, wqb_n)
                    gate.wait(tt, p.cACT)
                    self.ps_wait(tt, gate, "psA", [bank])
                    p.cPE.inc(tt.matmul(self.psA[:, bank * 512:bank * 512 + 512],
                                        self.arena[0:64, self.o_w + oc * 128:
                                                   self.o_w + oc * 128 + 128],
                                        qlat[:, tc * 512:(tc + 1) * 512],
                                        start=True, stop=True))
                p.pe(t_q)
                pe_n = p.cPE.n
                def a_q(aa, gate, oc=oc, tc=tc, bank=bank, pe_n=pe_n):
                    gate.wait(aa, p.cPE, pe_n)
                    dst = self.arena[:, self.o_q + oc * T + tc * 512:
                                     self.o_q + oc * T + tc * 512 + 512]
                    p.cACT.inc(aa.copy(dst, self.psA[:, bank * 512:bank * 512 + 512]))
                p.act(a_q)
                self.ps_set("psA", [bank], p.cACT)
        self.slot_consumed([self.o_w], p.cACT)
        for kvl in range(2):
            kvh = hf * 2 + kvl
            for tc in range(4):
                bank = tc % 4
                def t_k(tt, gate, kvh=kvh, kvl=kvl, tc=tc, bank=bank):
                    gate.wait(tt, p.cDVE, self.l1a_repack_n)
                    self.ps_wait(tt, gate, "psA", [bank])
                    p.cPE.inc(tt.matmul(self.psA[0:64, bank * 512:bank * 512 + 512],
                                        wkvb_k[:, kvh * 64:(kvh + 1) * 64],
                                        kvlat[:, tc * 512:(tc + 1) * 512],
                                        start=True, stop=True))
                p.pe(t_k)
                pe_n = p.cPE.n
                def a_k(aa, gate, kvl=kvl, tc=tc, bank=bank, pe_n=pe_n):
                    gate.wait(aa, p.cPE, pe_n)
                    base = self.o_k + kvl * T
                    aa.copy(self.arena[0:64, base + tc * 512:base + tc * 512 + 512],
                            self.psA[0:64, bank * 512:bank * 512 + 512])
                    p.cACT.inc(aa.copy(
                        self.arena[64:128, base + tc * 512:base + tc * 512 + 512],
                        self.psA[0:64, bank * 512:bank * 512 + 512]))
                p.act(a_k)
                self.ps_set("psA", [bank], p.cACT)
        for kt in range(NT):
            bank = kt % 4
            def t_v(tt, gate, kt=kt, bank=bank):
                gate.wait(tt, p.cDVE, max(self.l1a_repack_n, self.vones_n))
                self.ps_wait(tt, gate, "psB", [bank])
                p.cPE.inc(tt.matmul(self.psB[:, bank * 512:bank * 512 + 128],
                                    kvlat[:, kt * 128:(kt + 1) * 128],
                                    wkvb_v[:, hf * 128:(hf + 1) * 128],
                                    start=True, stop=True))
            p.pe(t_v)
            pe_n = p.cPE.n
            dst = self.arena[:, self.o_v + kt * 520:self.o_v + kt * 520 + 130] \
                .rearrange("p (h c) -> p h c", h=2)[:, :, 0:64]
            def v_v(v, gate, kt=kt, bank=bank, pe_n=pe_n, dst=dst):
                gate.wait(v, p.cPE, pe_n)
                p.cDVE.inc(v.tensor_copy(
                    dst, self.psB[:, bank * 512:bank * 512 + 128]
                    .rearrange("p (h c) -> p h c", h=2)))
            p.dve(v_v)
            self.ps_set("psB", [bank], p.cDVE)
        for ti in range(4):
            self.rope_tile(self.arena[:, self.o_q + ti * T:self.o_q + (ti + 1) * T],
                           perm_sb)
        self.rope_tile(self.arena[:, self.o_k:self.o_k + T], perm_sb)
        self.rope_tile(self.arena[:, self.o_k + T:self.o_k + 2 * T], perm_sb)

    def rope_tile(self, tile_ap, perm_sb):
        p = self.p
        qcs = p.sem("ropecs")
        C = self.pool[:, 1024:2048]
        S = self.pool[:, 2048:3072]
        zbuf = self.pool[:, 0:1024]
        for ch in range(2):
            key = (self.seq, self.sub, getattr(self, "_rope_hf", 0), ch)
            if getattr(self, "_rope_loaded", None) != (self.seq, ch, self.sub):
                dv_n = p.cDVE.n
                ac_n = p.cACT.n
                def s_cs(sy, gate, ch=ch, dv_n=dv_n, ac_n=ac_n):
                    gate.wait(sy, p.cDVE, dv_n)
                    gate.wait(sy, p.cACT, ac_n)
                    gate.wait(sy, qcs)
                    qcs.inc(sy.dma_start(C, self.rope_c[:, ch * 1024:(ch + 1) * 1024]), 16)
                    qcs.inc(sy.dma_start(S, self.rope_s[:, ch * 1024:(ch + 1) * 1024]), 16)
                p.sp(s_cs)
                self._rope_loaded = (self.seq, ch, self.sub)
            qcs_n = qcs.n
            def t_p(tt, gate, ch=ch):
                gate.wait(tt, p.cACT)
                gate.wait(tt, p.qSm)
                self.ps_wait(tt, gate, "psB", [0, 1])
                for j in range(2):
                    p.cPE.inc(tt.matmul(self.psB[:, j * 512:j * 512 + 512],
                                        perm_sb,
                                        tile_ap[:, ch * 1024 + j * 512:
                                                ch * 1024 + j * 512 + 512],
                                        start=True, stop=True))
            p.pe(t_p)
            pe_n = p.cPE.n
            def v_r(v, gate, ch=ch, pe_n=pe_n, qcs_n=qcs_n):
                gate.wait(v, p.cPE, pe_n)
                gate.wait(v, qcs, qcs_n)
                v.tensor_mul(zbuf, tile_ap[:, ch * 1024:(ch + 1) * 1024], C)
                v.tensor_mul(self.psB[:, 0:1024], self.psB[:, 0:1024], S)
                p.cDVE.inc(v.tensor_add(tile_ap[:, ch * 1024:(ch + 1) * 1024],
                                        self.psB[:, 0:1024], zbuf))
            p.dve(v_r)
            self.ps_set("psB", [0, 1], p.cDVE)

    def emit_l0ffn(self):
        p = self.p
        self.ln_phase("rms")
        oH = self.o_osp
        xf = lambda c: self.av(self.o_xhatf, c)
        Ht = lambda fc: self.arena[:, oH + fc * 1024: oH + (fc + 1) * 1024]
        onescell = self.arena[:, self.o_pt + 2048:self.o_pt + 2049]
        def v_ones(v, gate):
            gate.wait(v, p.cPE)
            v.memset(onescell, 1.0)
            p.fence(v, p.cDVE)
        p.dve(v_ones)
        qup = p.sem("wup")
        qdn = [p.sem("wdn0"), p.sem("wdn1")]
        qr2 = p.sem("r2")
        prep_w = [(s, s.n) for s in (p.qP[0], p.qP[1])]
        for sc in range(2):
            t0 = sc * 1024
            for g in range(4):
                acn = p.cACT.n
                def s_w(sy, gate, g=g, acn=acn):
                    self.slot_guard(sy, gate, [self.o_w, self.o_w2], qup)
                    gate.wait(sy, p.cACT, acn)
                    for c in range(8):
                        qup.inc(sy.dma_start(
                            self.arena[:, self.o_w + c * 1024:self.o_w + (c + 1) * 1024],
                            self.cache["wup"][g * 1024:(g + 1) * 1024,
                                              c * 128:(c + 1) * 128],
                            transpose=True), 16)
                p.sp(s_w)
                self.slot_loaded([self.o_w, self.o_w2], qup)
                wn = qup.n
                for fcl in range(8):
                    fc = g * 8 + fcl
                    for th in range(2):
                        bank = (fc * 2 + th) % 4
                        def t_up(tt, gate, fc=fc, fcl=fcl, th=th, bank=bank, wn=wn):
                            self.wait_xhatf(tt, gate, sc == 0)
                            gate.wait(tt, qup, wn)
                            self.ps_wait(tt, gate, "psA", [bank])
                            for c in range(8):
                                ins = tt.matmul(
                                    self.psA[:, bank * 512:(bank + 1) * 512],
                                    self.arena[:, self.o_w + c * 1024 + fcl * 128:
                                               self.o_w + c * 1024 + fcl * 128 + 128],
                                    xf(c)[:, t0 + th * 512:t0 + th * 512 + 512],
                                    start=(c == 0), stop=(c == 7))
                            p.cPE.inc(ins)
                        p.pe(t_up)
                        pe_n = p.cPE.n
                        def a_g(aa, gate, fc=fc, th=th, bank=bank, pe_n=pe_n):
                            gate.wait(aa, p.cPE, pe_n)
                            p.cACT.inc(aa.activation(
                                Ht(fc)[:, th * 512:th * 512 + 512],
                                self.psA[:, bank * 512:(bank + 1) * 512], AF.Gelu))
                        p.act(a_g)
                        self.ps_set("psA", [bank], p.cACT)
                        ac_n = p.cACT.n
                        sqbuf = self.arena[:, self.o_pt + (fc % 2) * 1024:
                                           self.o_pt + (fc % 2) * 1024 + 512]
                        def v_sq(v, gate, fc=fc, th=th, ac_n=ac_n, sqbuf=sqbuf):
                            gate.wait(v, p.cACT, ac_n)
                            gate.wait(v, p.cPE)
                            p.cDVE.inc(v.tensor_mul(sqbuf,
                                                    Ht(fc)[:, th * 512:th * 512 + 512],
                                                    Ht(fc)[:, th * 512:th * 512 + 512]))
                        p.dve(v_sq)
                        dv_n = p.cDVE.n
                        def t_sq(tt, gate, fc=fc, th=th, dv_n=dv_n, sqbuf=sqbuf):
                            gate.wait(tt, p.cDVE, dv_n)
                            if fc == 0 and th == 0:
                                self.ps_wait(tt, gate, "psB", [0])
                            p.cPE.inc(tt.matmul(self.psB[0:1, th * 512:th * 512 + 512],
                                                onescell, sqbuf,
                                                start=(fc == 0), stop=(fc == 31)))
                        p.pe(t_sq)
            self.slot_consumed([self.o_w, self.o_w2], p.cACT)
            pe_n = p.cPE.n
            def a_r2(aa, gate, pe_n=pe_n):
                gate.wait(aa, p.cPE, pe_n)
                aa.activation(self.grow[:, 0:1024], self.psB[0:1, 0:1024], AF.Ln)
                aa.activation(self.grow[:, 0:1024], self.grow[:, 0:1024], AF.Exp,
                              scale=0.5)
                aa.activation(self.grow[:, 0:1024], self.grow[:, 0:1024], AF.Ln,
                              bias=self.cpool[0:1, 2:3], scale=1.0 / 64.0)
                aa.activation(self.grow[:, 0:1024], self.grow[:, 0:1024], AF.Exp,
                              scale=-1.0)
                p.fence(aa, p.cACT)
            p.act(a_r2)
            self.ps_set("psB", [0], p.cACT)
            dv_n = p.cACT.n
            def s_r2(sy, gate, sc=sc, dv_n=dv_n):
                gate.wait(sy, p.cACT, dv_n)
                gate.wait(sy, p.cPE)
                gate.wait(sy, qr2)
                qr2.inc(sy.dma_start(self.rows_d[0:1, t0:t0 + 1024],
                                     self.grow[:, 0:1024]), 16)
                gate.wait(sy, qr2)
                qr2.inc(sy.dma_start(
                    self.smalls[:, 48:56],
                    self.rows_d[0, t0:t0 + 1024].rearrange("(n p) -> p n", p=128),
                    allow_slow_non_contiguous=True), 16)
            p.sp(s_r2)
            r2n = qr2.n
            dnbufs = [self.arena[:, self.o_misc:self.o_misc + 1024],
                      self.arena[:, self.o_misc + 1024:self.o_misc + 2048]]
            for half in range(2):
                pe_before = p.cPE.n
                dmark = {fc: pe_before + fc + 1 for fc in range(32)}
                qdn_base = [qdn[0].n, qdn[1].n]
                def s_dn(sy, gate, half=half):
                    self.slot_guard(sy, gate, [self.o_misc], qdn[0])
                    gate.wait(sy, qdn[1])
                    for fc in range(32):
                        par = fc % 2
                        if fc >= 2:
                            gate.wait(sy, p.cPE, dmark[fc - 2])
                        else:
                            gate.wait(sy, p.cDVE)
                        gate.wait(sy, qdn[par])
                        qdn[par].inc(sy.dma_start(
                            dnbufs[par],
                            self.cache["wdown"][0:1024, fc * 128:(fc + 1) * 128],
                            transpose=True), 16)
                p.sp(s_dn)
                self.slot_loaded([self.o_misc], qdn[0])
                def t_dn(tt, gate, half=half, qdn_base=qdn_base):
                    gate.wait(tt, p.cACT)
                    for fc in range(32):
                        par = fc % 2
                        gate.wait(tt, qdn[par], qdn_base[par] + 16 * (fc // 2 + 1))
                        if fc == 0:
                            self.ps_wait(tt, gate, "psA", [0, 1, 2, 3])
                            self.ps_wait(tt, gate, "psB", [0, 1, 2, 3])
                        for j in range(4):
                            ps = self.psA if j < 2 else self.psB
                            cb = (j % 2) * 1024
                            for eh in range(2):
                                ins = tt.matmul(
                                    ps[:, cb + eh * 512:cb + eh * 512 + 512],
                                    Ht(fc)[:, (half * 4 + j) * 128:
                                           (half * 4 + j) * 128 + 128],
                                    dnbufs[par][:, eh * 512:eh * 512 + 512],
                                    start=(fc == 0), stop=(fc == 31))
                        p.cPE.inc(ins)
                p.pe(t_dn)
                pe_dn = p.cPE.n
                for j in range(4):
                    tq = sc * 8 + half * 4 + j
                    ps = self.psA if j < 2 else self.psB
                    cb = (j % 2) * 1024
                    def v_e(v, gate, j=j, tq=tq, ps=ps, cb=cb):
                        gate.wait(v, p.cPE, pe_dn)
                        gate.wait(v, qr2, r2n)
                        p.cDVE.inc(v.scalar_tensor_tensor(
                            self.x_res[:, tq, :], ps[:, cb:cb + 1024],
                            self.smalls[:, 48 + (tq - sc * 8):49 + (tq - sc * 8)],
                            self.x_res[:, tq, :], ALU.mult, ALU.add))
                    p.dve(v_e)
                self.ps_set("psA", [0, 1, 2, 3], p.cDVE)
                self.ps_set("psB", [0, 1, 2, 3], p.cDVE)
            self.slot_consumed([self.o_misc], p.cDVE)

    def emit_l1ffn(self):
        p = self.p
        self.ln_phase("plain")
        xf = lambda c: self.av(self.o_xhatf, c)
        qwf = p.sem("wl1f")
        qrow = p.sem("rowl1f")
        ac_n, dv_n = p.cACT.n, p.cDVE.n
        def g_b(gp, gate):
            gate.wait(gp, p.cPE); gate.wait(gp, p.cACT, ac_n); gate.wait(gp, p.cDVE, dv_n)
            gate.wait(gp, qrow)
            qrow.inc(gp.dma_start(out=self.pool[0:1, 3072:4096],
                                  in_=self.W["L1_ffn_bias"][None, :]), 16)
        p.pool(g_b)
        self.build_bcast(self.bb, self.pool[0:1, 3072:4096], 1024,
                         gate_sem=(qrow, qrow.n))
        def s_w(sy, gate):
            self.slot_guard(sy, gate, [self.o_w, self.o_w2], qwf)
            for s_, n_ in [(p.qP[0], p.qP[0].n), (p.qP[1], p.qP[1].n)]:
                gate.wait(sy, s_, n_)
            for c in range(8):
                qwf.inc(sy.dma_start(
                    self.arena[:, self.o_w + c * 1024:self.o_w + (c + 1) * 1024],
                    self.cache["l1f"][0:1024, c * 128:(c + 1) * 128],
                    transpose=True), 16)
        p.sp(s_w)
        self.slot_loaded([self.o_w, self.o_w2], qwf)
        wn = qwf.n
        for tt in range(NT):
            banks = [0, 1] if tt % 2 == 0 else [2, 3]
            cb = banks[0] * 512
            def t(t_, gate, tt=tt, banks=banks, cb=cb):
                self.wait_xhatf(t_, gate, tt < 8)
                gate.wait(t_, qwf, wn)
                self.ps_wait(t_, gate, "psA", banks)
                for eh in range(2):
                    for c in range(8):
                        ins = t_.matmul(
                            self.psA[:, cb + eh * 512:cb + eh * 512 + 512],
                            xf(c)[:, tt * 128:(tt + 1) * 128],
                            self.arena[:, self.o_w + c * 1024 + eh * 512:
                                       self.o_w + c * 1024 + eh * 512 + 512],
                            start=(c == 0), stop=(c == 7))
                    p.cPE.inc(ins)
            p.pe(t)
            pe_n = p.cPE.n
            def v_e(v, gate, tt=tt, cb=cb, pe_n=pe_n):
                gate.wait(v, p.cPE, pe_n)
                gate.wait(v, p.cACT)
                v.tensor_add(self.zb, self.psA[:, cb:cb + 1024], self.bb)
                v.tensor_scalar(self.zb, self.zb, 100.0, -100.0, ALU.min, ALU.max)
                p.cDVE.inc(v.tensor_add(self.x_res[:, tt, :], self.x_res[:, tt, :],
                                        self.zb))
            p.dve(v_e)
            self.ps_set("psA", banks, p.cDVE)
        self.slot_consumed([self.o_w, self.o_w2], p.cDVE)

    def emit_l1b(self):
        p = self.p
        qrow = p.sem("rowl1b")
        ac0, dv0 = p.cACT.n, p.cDVE.n
        def g_r(gp, gate):
            gate.wait(gp, p.cPE); gate.wait(gp, p.cACT, ac0); gate.wait(gp, p.cDVE, dv0)
            gate.wait(gp, qrow)
            qrow.inc(gp.dma_start(out=self.pool[0:1, 3072:4096],
                                  in_=self.W["L1_attn_b_norm1_g"][None, :]), 16)
        p.pool(g_r)
        self.build_bcast(self.bb, self.pool[0:1, 3072:4096], 1024,
                         gate_sem=(qrow, qrow.n))
        def g_r2(gp, gate):
            gate.wait(gp, p.cPE); gate.wait(gp, qrow)
            qrow.inc(gp.dma_start(out=self.pool[0:1, 3072:4096],
                                  in_=self.W["L1_attn_b_norm1_b"][None, :]), 16)
        p.pool(g_r2)
        self.build_bcast(self.bb2, self.pool[0:1, 3072:4096], 1024,
                         gate_sem=(qrow, qrow.n))
        self.ln_phase("l1b", gmul=self.bb, badd=self.bb2)
        self.emit_vones()
        for hf in range(2):
            self.l1b_qkv(hf)
            for hl in range(8):
                self.l1b_head(hf, hl)
        self.l1b_proj_mid()
        self.l1b_mlp()

    def l1b_qkv(self, hf):
        p = self.p
        xf = lambda c: self.av(self.o_xhatf, c)
        for pi, proj in enumerate(("q", "k", "v")):
            wslot = self.o_w if pi % 2 == 0 else self.o_w2
            sl = "0" if wslot == self.o_w else "1"
            qw = p.sem("w" + sl + "s")
            rowbase = pi * 1024 + hf * 512
            def s(sy, gate, rowbase=rowbase, wslot=wslot, qw=qw):
                self.slot_guard(sy, gate, [wslot], qw)
                gate.wait(sy, p.qC, self.cache_qg)
                for c in range(8):
                    qw.inc(sy.dma_start(
                        self.arena[:, wslot + c * 512: wslot + (c + 1) * 512],
                        self.cache["qkv1"][rowbase:rowbase + 512,
                                           c * 128:(c + 1) * 128],
                        transpose=True), 16)
            p.sp(s)
            self.slot_loaded([wslot], qw)
            wready = (qw, qw.n)
            wbuf = lambda c, s_=wslot: self.arena[:, s_ + c * 512: s_ + (c + 1) * 512]
            if proj == "q":
                self.proj_feature_major(wbuf, wready, self.o_q, "l1b", pi, hf)
                self.slot_consumed([wslot], p.cACT)
                for ti in range(4):
                    for ch in range(2):
                        ap = self.arena[:, self.o_q + ti * T + ch * 1024:
                                        self.o_q + ti * T + ch * 1024 + 1024]
                        ac_n = p.cACT.n
                        def v_1(v, gate, ap=ap, ac_n=ac_n):
                            gate.wait(v, p.cACT, ac_n)
                            p.cDVE.inc(v.tensor_scalar_min(self.zb, ap, 0.0))
                        p.dve(v_1)
                        dv_n = p.cDVE.n
                        def a_1(aa, gate, dv_n=dv_n):
                            gate.wait(aa, p.cDVE, dv_n)
                            p.cACT.inc(aa.activation(self.zb, self.zb, AF.Exp))
                        p.act(a_1)
                        ac_n2 = p.cACT.n
                        def v_2(v, gate, ap=ap, ac_n2=ac_n2):
                            gate.wait(v, p.cACT, ac_n2)
                            p.cDVE.inc(v.scalar_tensor_tensor(ap, ap, 1.0, self.zb,
                                                              ALU.add, ALU.max))
                        p.dve(v_2)
            else:
                for kt in range(NT):
                    bank = kt % 4
                    def t(tt, gate, kt=kt, bank=bank, wready=wready):
                        self.wait_xhatf(tt, gate, kt < 8)
                        gate.wait(tt, wready[0], wready[1])
                        gate.wait(tt, p.cDVE)
                        self.ps_wait(tt, gate, "psB", [bank])
                        for c in range(8):
                            ins = tt.matmul(self.psB[:, bank * 512:(bank + 1) * 512],
                                            xf(c)[:, kt * 128:(kt + 1) * 128],
                                            wbuf(c), start=(c == 0), stop=(c == 7))
                        p.cPE.inc(ins)
                    p.pe(t)
                    pe_n = p.cPE.n
                    if proj == "k":
                        dst = self.arena[:, self.o_k + kt * 512:self.o_k + (kt + 1) * 512]
                        def v_e(v, gate, kt=kt, bank=bank, pe_n=pe_n, dst=dst):
                            gate.wait(v, p.cPE, pe_n)
                            p.cDVE.inc(v.tensor_scalar_min(
                                self.zb[:, 0:512], self.psB[:, bank * 512:(bank + 1) * 512], 0.0))
                        p.dve(v_e)
                        dv_n = p.cDVE.n
                        def a_e(aa, gate, dv_n=dv_n):
                            gate.wait(aa, p.cDVE, dv_n)
                            p.cACT.inc(aa.activation(self.zb[:, 0:512], self.zb[:, 0:512],
                                                     AF.Exp))
                        p.act(a_e)
                        ac_n = p.cACT.n
                        def v_e2(v, gate, kt=kt, bank=bank, dst=dst, ac_n=ac_n):
                            gate.wait(v, p.cACT, ac_n)
                            p.cDVE.inc(v.scalar_tensor_tensor(
                                dst, self.psB[:, bank * 512:(bank + 1) * 512], 1.0,
                                self.zb[:, 0:512], ALU.add, ALU.max))
                        p.dve(v_e2)
                        self.ps_set("psB", [bank], p.cDVE)
                    else:
                        dst = self.arena[:, self.o_v + kt * 520:self.o_v + (kt + 1) * 520] \
                            .rearrange("p (h c) -> p h c", h=8)[:, :, 0:64]
                        def v_e(v, gate, kt=kt, bank=bank, pe_n=pe_n, dst=dst):
                            gate.wait(v, p.cPE, pe_n)
                            p.cDVE.inc(v.tensor_copy(
                                dst, self.psB[:, bank * 512:(bank + 1) * 512]
                                .rearrange("p (h c) -> p h c", h=8)))
                        p.dve(v_e)
                        self.ps_set("psB", [bank], p.cDVE)
                self.slot_consumed([wslot], p.cDVE)

    def l1b_head(self, hf, hl):
        p = self.p
        qtile, qrow = hl // 2, (hl % 2) * 64
        pair = hl // 2
        kvsb = self.arena[:, self.o_pt + 3072:self.o_pt + 3137]
        def t_kv(tt, gate):
            gate.wait(tt, p.cDVE)
            gate.wait(tt, p.cACT)
            self.ps_wait(tt, gate, "psA", [0])
            for kt in range(NT):
                ins = tt.matmul(self.psA[0:64, 0:65],
                                self.arena[:, self.o_k + kt * 512 + hl * 64:
                                           self.o_k + kt * 512 + hl * 64 + 64],
                                self.arena[:, self.o_v + kt * 520 + hl * 65:
                                           self.o_v + kt * 520 + hl * 65 + 65],
                                start=(kt == 0), stop=(kt == NT - 1))
            p.cPE.inc(ins)
        p.pe(t_kv)
        pe_n = p.cPE.n
        def a_kv(aa, gate, pe_n=pe_n):
            gate.wait(aa, p.cPE, pe_n)
            gate.wait(aa, p.cDVE)
            aa.copy(kvsb[0:64, :], self.psA[0:64, 0:65])
            p.cACT.inc(aa.copy(kvsb[64:128, :], self.psA[0:64, 0:65]))
        p.act(a_kv)
        self.ps_set("psA", [0], p.cACT)
        ac_kv = p.cACT.n
        def t_att(tt, gate):
            gate.wait(tt, p.cACT, ac_kv)
            self.ps_wait(tt, gate, "psB", [0, 1, 2, 3])
            for tc in range(4):
                ins = tt.matmul(self.psB[0:65, tc * 512:(tc + 1) * 512],
                                kvsb[qrow:qrow + 64, :],
                                self.arena[qrow:qrow + 64,
                                           self.o_q + qtile * T + tc * 512:
                                           self.o_q + qtile * T + tc * 512 + 512],
                                start=True, stop=True)
            p.cPE.inc(ins)
        p.pe(t_att)
        pe_att = p.cPE.n
        for qh in range(2):
            def v_r(a, gate, qh=qh):
                gate.wait(a, p.cPE, pe_att)
                a.activation(self.grow[:, 0:1024],
                             self.psB[64:65, qh * 1024:(qh + 1) * 1024], AF.Ln,
                             bias=self.cpool[0:1, 3:4])
                p.cACT.inc(a.activation(self.grow[:, 0:1024], self.grow[:, 0:1024],
                                        AF.Exp, scale=-1.0))
            p.act(v_r)
            dv_n = p.cACT.n
            def t_z(tt, gate, qh=qh, dv_n=dv_n):
                gate.wait(tt, p.cACT, dv_n)
                self.ps_wait(tt, gate, "psA", [2, 3])
                p.cPE.inc(tt.matmul(self.psA[0:64, 1024:1536], self.onesrow[:, 0:64],
                                    self.grow[:, 0:512], start=True, stop=True))
                p.cPE.inc(tt.matmul(self.psA[0:64, 1536:2048], self.onesrow[:, 0:64],
                                    self.grow[:, 512:1024], start=True, stop=True))
            p.pe(t_z)
            pe_n2 = p.cPE.n
            def a_z(aa, gate, pe_n2=pe_n2):
                gate.wait(aa, p.cPE, pe_n2)
                gate.wait(aa, p.cDVE)
                p.cACT.inc(aa.copy(self.zb[0:64, :], self.psA[0:64, 1024:2048]))
            p.act(a_z)
            self.ps_set("psA", [2, 3], p.cACT)
            ac_n = p.cACT.n
            base = (self.o_osp if hf == 0 else self.o_xhatf)
            dst = self.arena[qrow:qrow + 64,
                             base + pair * T + qh * 1024:base + pair * T + qh * 1024 + 1024]
            def v_m(v, gate, qh=qh, dst=dst, ac_n=ac_n):
                gate.wait(v, p.cACT, ac_n)
                p.cDVE.inc(v.tensor_mul(dst, self.psB[0:64, qh * 1024:(qh + 1) * 1024],
                                        self.zb[0:64, :]))
            p.dve(v_m)
        self.ps_set("psB", [0, 1, 2, 3], p.cDVE)

    def l1b_proj_mid(self):
        p = self.p
        st = self.stats
        qw = p.sem("w0s")
        qrow = p.sem("rowl1b")
        ac0 = p.cACT.n
        def g_r(gp, gate):
            gate.wait(gp, p.cPE); gate.wait(gp, p.cACT, ac0); gate.wait(gp, p.cDVE)
            gate.wait(gp, qrow)
            qrow.inc(gp.dma_start(out=self.pool[0:1, 3072:4096],
                                  in_=self.W["L1_attn_b_proj_b"][None, :]), 16)
        p.pool(g_r)
        self.build_bcast(self.bb2, self.pool[0:1, 3072:4096], 1024,
                         gate_sem=(qrow, qrow.n))
        self.g2b = self.arena[:, self.o_v:self.o_v + 1024]
        self.b2b = self.arena[:, self.o_v + 1024:self.o_v + 2048]
        def g_r2(gp, gate):
            gate.wait(gp, p.cPE); gate.wait(gp, qrow)
            qrow.inc(gp.dma_start(out=self.pool[0:1, 3072:4096],
                                  in_=self.W["L1_attn_b_norm2_g"][None, :]), 16)
        p.pool(g_r2)
        self.build_bcast(self.g2b, self.pool[0:1, 3072:4096], 1024,
                         gate_sem=(qrow, qrow.n))
        def g_r3(gp, gate):
            gate.wait(gp, p.cPE); gate.wait(gp, qrow)
            qrow.inc(gp.dma_start(out=self.pool[0:1, 3072:4096],
                                  in_=self.W["L1_attn_b_norm2_b"][None, :]), 16)
        p.pool(g_r3)
        self.build_bcast(self.b2b, self.pool[0:1, 3072:4096], 1024,
                         gate_sem=(qrow, qrow.n))
        def s_w(sy, gate):
            self.slot_guard(sy, gate, [self.o_w, self.o_w2], qw)
            gate.wait(sy, p.qC, self.cache_qg)
            for j in range(8):
                slot = self.o_w if j < 4 else self.o_w2
                qw.inc(sy.dma_start(
                    self.arena[:, slot + (j % 4) * 1024:slot + (j % 4) * 1024 + 1024],
                    self.cache["proj1"][0:1024, j * 128:(j + 1) * 128],
                    transpose=True), 16)
        p.sp(s_w)
        self.slot_loaded([self.o_w, self.o_w2], qw)
        wn = qw.n
        for tt in range(NT):
            banks = [0, 1] if tt % 2 == 0 else [2, 3]
            cb = banks[0] * 512
            def t(t_, gate, tt=tt, banks=banks, cb=cb):
                gate.wait(t_, qw, wn)
                gate.wait(t_, p.cDVE)
                self.ps_wait(t_, gate, "psA", banks)
                for eh in range(2):
                    for j in range(8):
                        base = (self.o_osp if j < 4 else self.o_xhatf)
                        ins = t_.matmul(
                            self.psA[:, cb + eh * 512:cb + eh * 512 + 512],
                            self.arena[:, base + (j % 4) * T + tt * 128:
                                       base + (j % 4) * T + tt * 128 + 128],
                            self.arena[:, (self.o_w if j < 4 else self.o_w2) +
                                       (j % 4) * 1024 + eh * 512:
                                       (self.o_w if j < 4 else self.o_w2) +
                                       (j % 4) * 1024 + eh * 512 + 512],
                            start=(j == 0), stop=(j == 7))
                    p.cPE.inc(ins)
            p.pe(t)
            pe_n = p.cPE.n
            def a_h(aa, gate, tt=tt):
                gate.wait(aa, p.cDVE)
                p.cACT.inc(aa.activation(self.zb, self.x_res[:, tt, :], AF.Identity,
                                         bias=st[:, 96 + tt:97 + tt],
                                         scale=st[:, 48 + tt:49 + tt]))
            p.act(a_h)
            ac_n = p.cACT.n
            def v_e(v, gate, tt=tt, cb=cb, pe_n=pe_n, ac_n=ac_n):
                gate.wait(v, p.cPE, pe_n)
                gate.wait(v, p.cACT, ac_n)
                v.tensor_add(self.zb, self.zb, self.psA[:, cb:cb + 1024])
                v.tensor_add(self.zb, self.zb, self.bb2)
                v.tensor_add(self.x_res[:, tt, :], self.x_res[:, tt, :], self.zb)
                v.bn_stats(self.bnb[:, 0:6], self.zb[:, 0:512])
                v.bn_stats(self.bnb[:, 6:12], self.zb[:, 512:1024])
                p.cDVE.inc(v.bn_aggr(st[:, 32 + 2 * tt % 32:0] if False else
                                     self.stats2[:, 2 * tt:2 * tt + 2],
                                     self.bnb[:, 0:12]))
            p.dve(v_e)
            self.ps_set("psA", banks, p.cDVE)
            dv_n = p.cDVE.n
            def a_h2(aa, gate, tt=tt, dv_n=dv_n):
                gate.wait(aa, p.cDVE, dv_n)
                aa.activation(self.stats2[:, 32 + tt:33 + tt],
                              self.stats2[:, 2 * tt + 1:2 * tt + 2], AF.Ln,
                              bias=self.c_lneps)
                aa.activation(self.stats2[:, 32 + tt:33 + tt],
                              self.stats2[:, 32 + tt:33 + tt], AF.Exp, scale=-0.5)
                p.fence(aa, p.cACT)
            p.act(a_h2)
            ac_n2 = p.cACT.n
            def v_h2(v, gate, tt=tt, ac_n2=ac_n2):
                gate.wait(v, p.cACT, ac_n2)
                v.tensor_mul(self.stats2[:, 48 + tt:49 + tt],
                             self.stats2[:, 2 * tt:2 * tt + 1],
                             self.stats2[:, 32 + tt:33 + tt])
                p.cDVE.inc(v.tensor_scalar_mul(self.stats2[:, 48 + tt:49 + tt],
                                               self.stats2[:, 48 + tt:49 + tt], -1.0))
            p.dve(v_h2)
            dv_n2 = p.cDVE.n
            bufs = [self.arena[:, self.o_pt + i * 1024:self.o_pt + (i + 1) * 1024]
                    for i in range(2)]
            qha = [p.sem("xh0"), p.sem("xh1")]
            def a_h3(aa, gate, tt=tt, dv_n2=dv_n2):
                gate.wait(aa, p.cDVE, dv_n2)
                gate.wait(aa, qha[tt % 2])
                p.cACT.inc(aa.activation(bufs[tt % 2], self.zb, AF.Identity,
                                         bias=self.stats2[:, 48 + tt:49 + tt],
                                         scale=self.stats2[:, 32 + tt:33 + tt]))
            p.act(a_h3)
            ac_n3 = p.cACT.n
            if tt == 0:
                pass
            def v_h3(v, gate, tt=tt, ac_n3=ac_n3):
                gate.wait(v, p.cACT, ac_n3)
                v.tensor_mul(bufs[tt % 2], bufs[tt % 2], self.g2b)
                p.cDVE.inc(v.tensor_add(bufs[tt % 2], bufs[tt % 2], self.b2b))
            p.dve(v_h3)
            dv_n3 = p.cDVE.n
            def s_st(sy, gate, tt=tt, dv_n3=dv_n3):
                gate.wait(sy, p.cDVE, dv_n3)
                qha[tt % 2].inc(sy.dma_start(
                    out=self.xhat_d[tt * 128:(tt + 1) * 128, :], in_=bufs[tt % 2]), 16)
            p.sp(s_st)
        qha = [p.sem("xh0"), p.sem("xh1")]
        def s_tr(sy, gate):
            gate.wait(sy, qha[0]); gate.wait(sy, qha[1])
            for c in range(DCH):
                p.qH.inc(sy.dma_start(
                    self.arena[:, self.o_osp + c * T:self.o_osp + (c + 1) * T]
                    if c < 4 else
                    self.arena[:, self.o_q + (c - 4) * T:self.o_q + (c - 3) * T],
                    self.xhat_d[:, c * 128:(c + 1) * 128], transpose=True), 16)
        p.sp(s_tr)
        self.h2f_sems = [(p.qH, p.qH.n), (p.sem("xh0"), p.sem("xh0").n),
                         (p.sem("xh1"), p.sem("xh1").n)]

    def l1b_mlp(self):
        p = self.p
        h2f = lambda c: (self.arena[:, self.o_osp + c * T:self.o_osp + (c + 1) * T]
                         if c < 4 else
                         self.arena[:, self.o_q + (c - 4) * T:self.o_q + (c - 3) * T])
        oH = self.o_k
        Ht = lambda fc: self.arena[:, oH + fc * 512: oH + (fc + 1) * 512]
        qf1 = p.sem("wfc1")
        qf2 = [p.sem("wfc2a"), p.sem("wfc2b")]
        qrow = p.sem("rowl1b")
        dv0 = p.cDVE.n
        def g_b(gp, gate):
            gate.wait(gp, p.cDVE, dv0); gate.wait(gp, p.cPE); gate.wait(gp, qrow)
            qrow.inc(gp.dma_start(out=self.smalls[:, 32:64],
                                  in_=self.W["L1_attn_b_fc1_b"]
                                  .rearrange("(c p) -> p c", p=128),
                                  allow_slow_non_contiguous=True), 16)
            qrow.inc(gp.dma_start(out=self.pool[0:1, 3072:4096],
                                  in_=self.W["L1_attn_b_fc2_b"][None, :]), 16)
        p.pool(g_b)
        qrow_n = qrow.n
        self.build_bcast(self.bb2, self.pool[0:1, 3072:4096], 1024,
                         gate_sem=(qrow, qrow.n))
        dnbufs = [self.arena[:, self.o_misc:self.o_misc + 1024],
                  self.arena[:, self.o_misc + 1024:self.o_misc + 2048]]
        for sc in range(4):
            t0 = sc * 512
            for g in range(4):
                acn = p.cACT.n
                def s_w(sy, gate, g=g, acn=acn):
                    self.slot_guard(sy, gate, [self.o_w, self.o_w2], qf1)
                    gate.wait(sy, p.qC, self.cache_qg)
                    gate.wait(sy, p.cACT, acn)
                    for c in range(8):
                        qf1.inc(sy.dma_start(
                            self.arena[:, self.o_w + c * 1024:self.o_w + (c + 1) * 1024],
                            self.cache["fc1"][g * 1024:(g + 1) * 1024,
                                              c * 128:(c + 1) * 128],
                            transpose=True), 16)
                p.sp(s_w)
                self.slot_loaded([self.o_w, self.o_w2], qf1)
                wn = qf1.n
                for fcl in range(8):
                    fc = g * 8 + fcl
                    bank = fc % 4
                    def t_up(tt, gate, fc=fc, fcl=fcl, bank=bank, wn=wn, t0=t0):
                        [gate.wait(tt, s_, n_) for s_, n_ in self.h2f_sems]
                        gate.wait(tt, qf1, wn)
                        gate.wait(tt, qrow, qrow_n)
                        self.ps_wait(tt, gate, "psA", [bank])
                        for c in range(8):
                            ins = tt.matmul(
                                self.psA[:, bank * 512:(bank + 1) * 512],
                                self.arena[:, self.o_w + c * 1024 + fcl * 128:
                                           self.o_w + c * 1024 + fcl * 128 + 128],
                                h2f(c)[:, t0:t0 + 512],
                                start=(c == 0), stop=(c == 7))
                        p.cPE.inc(ins)
                    p.pe(t_up)
                    pe_n = p.cPE.n
                    def a_g(aa, gate, fc=fc, bank=bank, pe_n=pe_n):
                        gate.wait(aa, p.cPE, pe_n)
                        p.cACT.inc(aa.activation(Ht(fc),
                                                 self.psA[:, bank * 512:(bank + 1) * 512],
                                                 AF.Gelu,
                                                 bias=self.smalls[:, 32 + fc:33 + fc]))
                    p.act(a_g)
                    self.ps_set("psA", [bank], p.cACT)
            self.slot_consumed([self.o_w, self.o_w2], p.cACT)
            pe_before = p.cPE.n
            dmark = {fc: pe_before + fc + 1 for fc in range(32)}
            qdn_base = [qf2[0].n, qf2[1].n]
            def s_dn(sy, gate, sc=sc):
                self.slot_guard(sy, gate, [self.o_misc], qf2[0])
                gate.wait(sy, qf2[1])
                gate.wait(sy, p.qC, self.cache_qg)
                for fc in range(32):
                    par = fc % 2
                    if fc >= 2:
                        gate.wait(sy, p.cPE, dmark[fc - 2])
                    else:
                        gate.wait(sy, p.cDVE)
                    gate.wait(sy, qf2[par])
                    qf2[par].inc(sy.dma_start(
                        dnbufs[par],
                        self.cache["fc2"][0:1024, fc * 128:(fc + 1) * 128],
                        transpose=True), 16)
            p.sp(s_dn)
            self.slot_loaded([self.o_misc], qf2[0])
            def t_dn(tt, gate, sc=sc, qdn_base=qdn_base):
                gate.wait(tt, p.cACT)
                for fc in range(32):
                    par = fc % 2
                    gate.wait(tt, qf2[par], qdn_base[par] + 16 * (fc // 2 + 1))
                    if fc == 0:
                        self.ps_wait(tt, gate, "psA", [0, 1, 2, 3])
                        self.ps_wait(tt, gate, "psB", [0, 1, 2, 3])
                    for j in range(4):
                        ps = self.psA if j < 2 else self.psB
                        cb = (j % 2) * 1024
                        for eh in range(2):
                            ins = tt.matmul(
                                ps[:, cb + eh * 512:cb + eh * 512 + 512],
                                Ht(fc)[:, j * 128:(j + 1) * 128],
                                dnbufs[par][:, eh * 512:eh * 512 + 512],
                                start=(fc == 0), stop=(fc == 31))
                    p.cPE.inc(ins)
            p.pe(t_dn)
            pe_dn = p.cPE.n
            for j in range(4):
                tq = sc * 4 + j
                ps = self.psA if j < 2 else self.psB
                cb = (j % 2) * 1024
                def v_e(v, gate, j=j, tq=tq, ps=ps, cb=cb, pe_dn=pe_dn):
                    gate.wait(v, p.cPE, pe_dn)
                    gate.wait(v, p.cACT)
                    v.tensor_add(self.x_res[:, tq, :], self.x_res[:, tq, :],
                                 ps[:, cb:cb + 1024])
                    p.cDVE.inc(v.tensor_add(self.x_res[:, tq, :],
                                            self.x_res[:, tq, :], self.bb2))
                p.dve(v_e)
            self.ps_set("psA", [0, 1, 2, 3], p.cDVE)
            self.ps_set("psB", [0, 1, 2, 3], p.cDVE)
            self.slot_consumed([self.o_misc], p.cDVE)


GROUPS = [(0, 8)]


def kernel(**inputs):
    consts = host_constants()
    x = np.ascontiguousarray(inputs["L0_attn_a_x"], dtype=np.float32)
    xs = x.reshape(NCORES, SEQ_PER_CORE, T, D)
    cur = [np.ascontiguousarray(xs[c]) for c in range(NCORES)]
    for (lo, hi) in GROUPS:
        b = Builder(lo, hi)
        in_maps = []
        for c in range(NCORES):
            m = {"x": cur[c]}
            for name in b.W:
                m[name] = np.ascontiguousarray(inputs[name], dtype=np.float32)
            if hasattr(b, "rope_c"):
                m["rope_c"] = consts["rope_c"]
                m["rope_s"] = consts["rope_s"]
                m["perm128"] = consts["perm128"]
            if hasattr(b, "cmask"):
                m["cmask"] = consts["cmask"]
            in_maps.append(m)
        res = run_bass_kernel_spmd(b.nc, in_maps, core_ids=list(range(NCORES)))
        cur = [res.results[c]["y"] for c in range(NCORES)]
    return np.concatenate(cur, axis=0).reshape(B, T, D).astype(np.float32)


# version 11
# speedup vs baseline: 1.2368x; 1.2368x over previous
import numpy as np
import ml_dtypes
from contextlib import ExitStack

import concourse.bass as bass
import concourse.mybir as mybir
from concourse.bass_utils import run_bass_kernel_spmd

F32 = mybir.dt.float32
BF16 = mybir.dt.bfloat16
AF = mybir.ActivationFunctionType
ALU = mybir.AluOpType
AX = mybir.AxisListType

B, T, D = 16, 2048, 1024
NCORES = 8
SEQ_PER_CORE = B // NCORES
NT = T // 128
DCH = D // 128
H, DH = 16, 64
DFF = 4096
EXP_OFF = 12.0
MASKVAL = -1e9

BF = lambda a: np.ascontiguousarray(a).astype(ml_dtypes.bfloat16)


def host_constants():
    d = 64
    inv_freq = 1.0 / (10000.0 ** (np.arange(0, d, 2, dtype=np.float64) / d))
    t = np.arange(T, dtype=np.float64)
    fr = np.outer(t, inv_freq)
    cos = np.cos(fr).T
    sin = np.sin(fr).T
    C = np.empty((64, T), dtype=np.float32)
    S = np.empty((64, T), dtype=np.float32)
    C[0::2] = cos; C[1::2] = cos
    S[0::2] = -sin; S[1::2] = sin
    C2 = np.concatenate([C, C], axis=0)
    S2 = np.concatenate([S, S], axis=0)
    P = np.zeros((128, 128), dtype=np.float32)
    for i in range(0, 128, 2):
        P[i, i + 1] = 1.0
        P[i + 1, i] = 1.0
    M = np.zeros((128, 128), dtype=np.float32)
    for k in range(128):
        M[k, :k] = MASKVAL
    return {"rope_c": C2, "rope_s": S2, "perm128": BF(P).astype(ml_dtypes.bfloat16),
            "cmask": M}


class Sem:
    def __init__(self, ctx, nc, name):
        self.h = ctx.enter_context(nc.semaphore(name))
        self.n = 0

    def inc(self, instr, k=1):
        instr.then_inc(self.h, k)
        self.n += k
        return self.n


class Gate:
    def __init__(self):
        self.seen = {}

    def wait(self, eng, sem, val=None):
        if val is None:
            val = sem.n
        if val <= 0:
            return
        e = getattr(eng, "engine", None)
        key = (str(e), sem.h.name)
        if self.seen.get(key, -1) >= val:
            return
        eng.wait_ge(sem.h, val)
        self.seen[key] = val


def _ap_span(ap):
    try:
        t = ap.tensor.name
    except Exception:
        return None
    off = ap.offset if isinstance(ap.offset, int) else 0
    ext = 0
    try:
        for step, count in ap.ap:
            if isinstance(step, int) and isinstance(count, int):
                ext += abs(step) * (count - 1)
            else:
                ext += 1 << 40
    except Exception:
        ext = 1 << 40
    return (t, off, off + ext + 1)


def _overlap(a, b):
    return a[0] == b[0] and a[1] < b[2] and b[1] < a[2]


class GEng:
    WRITE0 = {"tensor_copy", "tensor_tensor", "tensor_add", "tensor_sub",
              "tensor_mul", "tensor_max", "tensor_scalar", "tensor_single_scalar",
              "tensor_scalar_add", "tensor_scalar_sub", "tensor_scalar_mul",
              "tensor_scalar_max", "tensor_scalar_min", "tensor_relu",
              "tensor_reduce", "reduce_max", "reduce_sum", "scalar_tensor_tensor",
              "activation", "sign", "square", "sqrt", "mul", "copy", "add",
              "reciprocal", "bn_stats", "bn_aggr", "memset", "memzero",
              "select", "transpose", "iota", "affine_select", "stream_shuffle"}

    def __init__(self, eng, state):
        self._e = eng
        self._st = state

    def __getattr__(self, name):
        f = getattr(self._e, name)
        if name not in self.WRITE0:
            return f

        def wrapped(*args, **kw):
            aps = [a for a in list(args) + list(kw.values()) if isinstance(a, bass.AP)]
            spans = [s for s in (_ap_span(a) for a in aps) if s is not None]
            if spans:
                outs = spans[:1]
                ins = spans[1:]
                need = False
                for s in spans[:1]:
                    if any(_overlap(s, p) for p in self._st["w"]) or \
                       any(_overlap(s, p) for p in self._st["r"]):
                        need = True
                for s in ins:
                    if any(_overlap(s, p) for p in self._st["w"]):
                        need = True
                if need:
                    self._e.drain()
                    self._st["w"].clear()
                    self._st["r"].clear()
                self._st["w"].extend(outs)
                self._st["r"].extend(ins)
            return f(*args, **kw)
        return wrapped

    def barrier(self):
        self._e.drain()
        self._st["w"].clear()
        self._st["r"].clear()


class Tok:
    __slots__ = ("incs",)
    def __init__(self):
        self.incs = []

    def then_inc(self, sem_handle, k=1):
        self.incs.append((sem_handle, k))
        return self


class Rec:
    __slots__ = ("sched", "engine")

    def __init__(self, sched, engname):
        self.sched = sched
        self.engine = engname

    def __getattr__(self, name):
        def call(*a, **k):
            tok = Tok()
            self.sched.append((self.engine, name, a, k, tok))
            return tok
        return call


class P:
    def __init__(self, nc, ctx):
        self.nc, self.ctx = nc, ctx
        self.gate = Gate()
        self.est = {k: {"w": [], "r": []} for k in ("vector", "scalar", "gpsimd")}
        self.qX = Sem(ctx, nc, "qX")
        self.qH = Sem(ctx, nc, "qH")
        self.qW0 = Sem(ctx, nc, "qW0")
        self.qW1 = Sem(ctx, nc, "qW1")
        self.qSm = Sem(ctx, nc, "qSm")
        self.qC = Sem(ctx, nc, "qC")
        self.qL = [Sem(ctx, nc, "qL0"), Sem(ctx, nc, "qL1")]
        self.qP = [Sem(ctx, nc, "qP0"), Sem(ctx, nc, "qP1")]
        self.qD = Sem(ctx, nc, "qD")
        self.cPE = Sem(ctx, nc, "cPE")
        self.cACT = Sem(ctx, nc, "cACT")
        self.cDVE = Sem(ctx, nc, "cDVE")
        self.cPOOL = Sem(ctx, nc, "cPOOL")
        self.all_dma = [self.qX, self.qH, self.qW0, self.qW1, self.qSm,
                        self.qC, *self.qL, *self.qP, self.qD]
        self._sems = {}
        self.sched = []
        self.block = ctx.enter_context(nc.Block())

    def sem(self, key):
        if key not in self._sems:
            s = Sem(self.ctx, self.nc, f"q_{key}")
            self._sems[key] = s
            self.all_dma.append(s)
        return self._sems[key]

    def pe(self, fn):
        fn(Rec(self.sched, "tensor"), self.gate)

    def act(self, fn):
        fn(GEng(Rec(self.sched, "scalar"), self.est["scalar"]), self.gate)

    def dve(self, fn):
        fn(GEng(Rec(self.sched, "vector"), self.est["vector"]), self.gate)

    def pool(self, fn):
        fn(GEng(Rec(self.sched, "gpsimd"), self.est["gpsimd"]), self.gate)

    def sp(self, fn):
        fn(Rec(self.sched, "sync"), self.gate)

    def flush(self):
        engmap = {"tensor": self.block.tensor, "scalar": self.block.scalar,
                  "vector": self.block.vector, "gpsimd": self.block.gpsimd,
                  "sync": self.block.sync}
        for engname in ("tensor", "scalar", "vector", "gpsimd", "sync"):
            def run(e, engname=engname):
                for en, name, a, k, tok in self.sched:
                    if en != engname:
                        continue
                    ins = getattr(e, name)(*a, **k)
                    for h, kk in tok.incs:
                        ins.then_inc(h, kk)
            engmap[engname](run)

    @staticmethod
    def fence(geng, sem, k=1):
        geng.barrier()
        sem.inc(geng.nop(), k)


class Builder:

    SUBLAYERS = ["l0a", "l0f", "l0b", "l0f2", "l1a", "l1f", "l1b", "l1f2"]

    def __init__(self, lo, hi):
        self.lo, self.hi = lo, hi
        self.subs = self.SUBLAYERS[lo:hi]
        nc = bass.Bass("TRN2", target_bir_lowering=False)
        self.nc = nc
        self.ctx = ExitStack()
        self.dram_io()
        self.sbuf_alloc()
        self.psum_alloc()
        self.p = P(nc, self.ctx)
        self.emit()
        self.ctx.close()

    def dram_io(self):
        nc = self.nc
        ap = lambda name, shape, dt, kind: nc.dram_tensor(name, shape, dt, kind=kind).ap()
        self.x_in = ap("x", [SEQ_PER_CORE, T, D], F32, "ExternalInput")
        self.y_out = ap("y", [SEQ_PER_CORE, T, D], F32, "ExternalOutput")
        need = set(self.subs)
        W = {}
        def inp(name, shape):
            W[name] = ap(name, shape, F32, "ExternalInput")
        if "l0a" in need:
            inp("L0_attn_a_Wq", [D, D]); inp("L0_attn_a_Wk", [D, D])
            inp("L0_attn_a_Wv", [D, D]); inp("L0_attn_a_Wo", [D, D])
        if "l0f" in need or "l0f2" in need:
            inp("L0_ffn_w_up", [DFF, D]); inp("L0_ffn_w_down", [D, DFF])
        if "l0b" in need:
            inp("L0_attn_b_w_qkv", [3 * D, D]); inp("L0_attn_b_b_qkv", [3 * D])
            inp("L0_attn_b_w_o", [D, D]); inp("L0_attn_b_b_o", [D])
        if "l1a" in need:
            inp("L1_attn_a_Wqa", [D, 64]); inp("L1_attn_a_Wqb", [64, D])
            inp("L1_attn_a_Wkva", [D, 32]); inp("L1_attn_a_Wkvb", [32, 512])
            inp("L1_attn_a_Wout", [D, D]); inp("L1_attn_a_bout", [D])
        if "l1f" in need or "l1f2" in need:
            inp("L1_ffn_weights", [D, D]); inp("L1_ffn_scale", [D]); inp("L1_ffn_bias", [D])
        if "l1b" in need:
            inp("L1_attn_b_norm1_g", [D]); inp("L1_attn_b_norm1_b", [D])
            inp("L1_attn_b_qkv_w", [3 * D, D]); inp("L1_attn_b_proj_w", [D, D])
            inp("L1_attn_b_proj_b", [D])
            inp("L1_attn_b_norm2_g", [D]); inp("L1_attn_b_norm2_b", [D])
            inp("L1_attn_b_fc1_w", [DFF, D]); inp("L1_attn_b_fc1_b", [DFF])
            inp("L1_attn_b_fc2_w", [D, DFF]); inp("L1_attn_b_fc2_b", [D])
        self.W = W
        if "l1a" in need:
            self.rope_c = ap("rope_c", [128, T], F32, "ExternalInput")
            self.rope_s = ap("rope_s", [128, T], F32, "ExternalInput")
            self.perm128 = ap("perm128", [128, 128], BF16, "ExternalInput")
        if "l1a" in need or "l0a" in need:
            self.cmask = ap("cmask", [128, 128], F32, "ExternalInput")
        sc = lambda name, shape, dt: nc.dram_tensor(name, shape, dt).ap()
        self.xhat_d = sc("xhat_d", [T, D], BF16)
        self.rows_d = sc("rows_d", [8, T], F32)
        self.cache = {}
        if "l0b" in need:
            self.cache["qkv0"] = sc("qkv0_bf", [3 * D, D], BF16)
            self.cache["wo0"] = sc("wo0_bf", [D, D], BF16)
        if "l0f" in need or "l0f2" in need:
            self.cache["wup"] = sc("wup_bf", [DFF, D], BF16)
            self.cache["wdown"] = sc("wdown_bf", [D, DFF], BF16)
        if "l1b" in need:
            self.cache["qkv1"] = sc("qkv1_bf", [3 * D, D], BF16)
            self.cache["proj1"] = sc("proj1_bf", [D, D], BF16)
            self.cache["fc1"] = sc("fc1_bf", [DFF, D], BF16)
            self.cache["fc2"] = sc("fc2_bf", [D, DFF], BF16)
            self.cache["beff_d"] = sc("beff_d", [8, 3 * D], F32)
        if "l1f" in need or "l1f2" in need:
            self.cache["l1f"] = sc("l1f_bf", [D, D], BF16)

    def sbuf_alloc(self):
        nc, ctx = self.nc, self.ctx
        t = lambda name, shape, dt: ctx.enter_context(nc.sbuf_tensor(name, shape, dt))
        self.x_res = t("x_res", [128, NT, D], F32)
        self.pool = t("pool", [128, 4096], F32)
        self.stats = t("stats", [128, 128], F32)
        self.stats2 = t("stats2", [128, 64], F32)
        self.bnb = t("bnb", [128, 16], F32)
        self.cpool = t("cpool", [128, 16], F32)
        self.arena = t("arena", [128, 63744], BF16)
        self.cmask_sb = t("cmask_sb", [128, 128], F32)
        self.zb = self.pool[:, 0:1024]
        self.bb = self.pool[:, 1024:2048]
        self.bb2 = self.pool[:, 2048:3072]
        self.z2row = self.pool[0:2, 3072:4096]
        self.grow = self.pool[0:1, 3072:4096]
        A = self.arena
        off = 0
        def take(n):
            nonlocal off
            lo = off; off += n
            return lo
        self.o_xhatf = take(8 * T)
        self.o_osp = take(4 * T)
        self.o_q = take(4 * T)
        self.o_k = take(4 * T)
        self.o_v = take(16 * 520)
        self.o_w = take(8 * 512)
        self.o_w2 = take(8 * 512)
        self.o_misc = take(2048)
        self.o_pt = take(4096)
        assert off <= 63744, off

    def psum_alloc(self):
        nc, ctx = self.nc, self.ctx
        self.psA = ctx.enter_context(nc.psum_tensor("psA", [128, 2048], F32))
        self.psB = ctx.enter_context(nc.psum_tensor("psB", [128, 2048], F32))

    def av(self, base, tile, width=T, rows=128, row0=0):
        return self.arena[row0:row0 + rows, base + tile * width: base + (tile + 1) * width]

    def emit(self):
        import os
        p = self.p
        self.dev_stage = os.environ.get("DEV_STAGE", "")
        self.smalls = self.ctx.enter_context(self.nc.sbuf_tensor("smalls", [128, 64], F32))
        self.wslot_free = {}
        self.cache_qg = 0
        self.onesrow = self.ctx.enter_context(self.nc.sbuf_tensor("onesrow", [1, 128], F32))
        self.onesbf = self.ctx.enter_context(self.nc.sbuf_tensor("onesbf", [1, 64], BF16))
        self.growb = self.arena[0:1, self.o_pt + 2048:self.o_pt + 3072]
        self.init_consts()
        self.prep_weights()
        self.prep_sems = [(s, s.n) for s in (p.qP[0], p.qP[1], p.qC, p.qL[0],
                                             p.qL[1], p.qSm)]
        self.prep_eng = [(p.cDVE, p.cDVE.n), (p.cACT, p.cACT.n)]
        for seq in range(SEQ_PER_CORE):
            self.seq = seq
            self.load_x(seq)
            if self.dev_stage == "copy":
                self.store_y(seq)
                continue
            if self.dev_stage == "ln":
                self.ln_phase("plain")
                self.dump_arena(self.o_xhatf, seq)
                continue
            if self.dev_stage == "wload":
                self.ln_phase("plain")
                p2 = self.p
                wsrc = self.W["L0_attn_a_Wq"]
                def g(gp, gate):
                    p2.sem("w0g").inc(gp.dma_start(
                        out=self.arena[:, self.o_w:self.o_w + 4096]
                        .rearrange("p (c o) -> p c o", c=8),
                        in_=wsrc[:, 0:512].rearrange("(c p) o -> p c o", p=128)), 16)
                self.p.pool(g)
                self.dump_arena(self.o_w, seq, 4096)
                continue
            if self.dev_stage == "vones":
                self.emit_vones()
                self.dump_arena(self.o_v, seq, 8320)
                continue
            if self.dev_stage == "qproj":
                self.ln_phase("plain")
                wsrc = self.W["L0_attn_a_Wq"]
                p2 = self.p
                def g(gp, gate):
                    gate.wait(gp, p2.cPE, self.wslot_free.get(self.o_w, 0))
                    p2.sem("w0g").inc(gp.dma_start(
                        out=self.arena[:, self.o_w:self.o_w + 4096]
                        .rearrange("p (c o) -> p c o", c=8),
                        in_=wsrc[:, 0:512].rearrange("(c p) o -> p c o", p=128)), 16)
                p2.pool(g)
                wb = lambda c: self.arena[:, self.o_w + c * 512:self.o_w + (c + 1) * 512]
                self.proj_feature_major(wb, (p2.sem("w0g"), p2.sem("w0g").n), self.o_q, "l0a", 0, 0)
                self.dump_arena(self.o_q, seq, 8192)
                continue
            if self.dev_stage == "vproj":
                self.ln_phase("plain")
                self.emit_vones()
                wsrc = self.W["L0_attn_a_Wv"]
                p2 = self.p
                def g(gp, gate):
                    gate.wait(gp, p2.cPE, self.wslot_free.get(self.o_w, 0))
                    p2.sem("w0g").inc(gp.dma_start(
                        out=self.arena[:, self.o_w:self.o_w + 4096]
                        .rearrange("p (c o) -> p c o", c=8),
                        in_=wsrc[:, 0:512].rearrange("(c p) o -> p c o", p=128)), 16)
                p2.pool(g)
                wb = lambda c: self.arena[:, self.o_w + c * 512:self.o_w + (c + 1) * 512]
                self.proj_v_token_major(wb, (p2.sem("w0g"), p2.sem("w0g").n), "l0a", 0)
                self.dump_arena(self.o_v, seq, 8320)
                continue
            if self.dev_stage in ("qkv", "qkvk", "qkvv", "heads"):
                kind = {"l0a": "l0a", "l0b": "l0b", "l1a": "l1a"}[self.subs[0]]
                self.sub = self.subs[0]
                causal = kind != "l0b"
                scale = 1.0 / 32.0 if kind == "l0b" else 0.125
                self.ln_phase("plain")
                if kind == "l0b":
                    self.emit_attn_setup_l0b()
                elif kind == "l1a":
                    self.emit_attn_setup_l1a()
                self.emit_vones()
                self.attn_qkv(kind, 0)
                if self.dev_stage == "qkv":
                    self.dump_arena(self.o_q, seq, 8192)
                elif self.dev_stage == "qkvk":
                    self.dump_arena(self.o_k, seq, 8192 if kind != "l1a" else 4096)
                elif self.dev_stage == "qkvv":
                    self.dump_arena(self.o_v, seq, 8320)
                else:
                    self.qkv_ready = (self.p.cACT.n, self.p.cDVE.n)
                    for hl in range(8):
                        self.attn_head(kind, 0, hl, causal, scale)
                    self.dump_arena(self.o_osp, seq, 8192)
                continue
            for sub in self.subs:
                self.sub = sub
                if sub == "l0a":
                    self.emit_attn(dict(kind="l0a", causal=True, scale=0.125))
                elif sub == "l0b":
                    self.emit_attn(dict(kind="l0b", causal=False, scale=1.0 / 32.0))
                elif sub == "l1a":
                    self.emit_attn(dict(kind="l1a", causal=True, scale=0.125))
                elif sub in ("l0f", "l0f2"):
                    self.emit_l0ffn()
                elif sub in ("l1f", "l1f2"):
                    self.emit_l1ffn()
                elif sub == "l1b":
                    self.emit_l1b()
            self.store_y(seq)
        self.p.flush()

    def init_consts(self):
        p = self.p
        cp = self.cpool
        def f(v, g):
            v.memset(cp[:, 0:1], -EXP_OFF)
            v.memset(cp[:, 1:2], 1e-5)
            v.memset(cp[:, 2:3], 1e-8)
            v.memset(cp[:, 3:4], 1e-6)
            v.memset(cp[:, 4:5], 0.0)
            v.memset(cp[:, 5:6], 1.0)
            v.memset(self.onesrow[:, :], 1.0)
            v.memset(self.onesbf[:, :], 1.0)
            v.barrier()
            p.cDVE.inc(v.engine_nop())
        p.dve(f)
        self.c_negoff = cp[:, 0:1]
        self.c_lneps = cp[:, 1:2]
        self.c_rmseps = cp[:, 2:3]
        self.c_attneps = cp[:, 3:4]
        self.c_zero = cp[:, 4:5]
        self.c_ones = cp[:, 5:6]
        self.cmask_ready = 0
        if hasattr(self, "cmask"):
            def s(sy, gate):
                p.sem("mask").inc(sy.dma_start(self.cmask_sb[:, :], self.cmask[:, :]), 16)
            p.sp(s)
            self.cmask_ready = p.sem("mask").n

    def prep_weights(self):
        p, W = self.p, self.W
        plain = []
        if "qkv0" in self.cache:
            plain += [(W["L0_attn_b_w_qkv"], self.cache["qkv0"]),
                      (W["L0_attn_b_w_o"], self.cache["wo0"])]
        if "qkv1" in self.cache:
            plain += [(W["L1_attn_b_qkv_w"], self.cache["qkv1"]),
                      (W["L1_attn_b_proj_w"], self.cache["proj1"]),
                      (W["L1_attn_b_fc1_w"], self.cache["fc1"]),
                      (W["L1_attn_b_fc2_w"], self.cache["fc2"])]
        def g(gp, gate):
            for src, dst in plain:
                p.qC.inc(gp.dma_start(out=dst[:, :], in_=src[:, :]), 16)
        p.pool(g)
        self.cache_qg = p.qC.n
        if "wup" in self.cache:
            self.prep_ternary(W["L0_ffn_w_up"], self.cache["wup"], DFF, D)
            self.prep_ternary(W["L0_ffn_w_down"], self.cache["wdown"], D, DFF)
        if "l1f" in self.cache:
            self.prep_sign(W["L1_ffn_weights"], W["L1_ffn_scale"], self.cache["l1f"])

    def prep_ternary(self, src, dst, rows, cols):
        p = self.p
        ncc = cols // 1024
        n = (rows // 128) * ncc
        ldA = self.pool[:, 0:1024]; ldB = self.pool[:, 1024:2048]
        absb = self.pool[:, 2048:3072]; sgnb = self.pool[:, 3072:4096]
        acc = self.smalls[:, 0:1]; thr = self.smalls[:, 1:2]
        red = self.smalls[:, 2:3]
        outb = self.arena[:, self.o_w:self.o_w + 2048]
        chunk = lambda i: (src[(i // ncc) * 128:(i // ncc) * 128 + 128,
                               (i % ncc) * 1024:(i % ncc) * 1024 + 1024])
        dchunk = lambda i: (dst[(i // ncc) * 128:(i // ncc) * 128 + 128,
                                (i % ncc) * 1024:(i % ncc) * 1024 + 1024])
        ql0 = [p.qL[0].n, p.qL[1].n]
        dv0, ac0, pe0 = p.cDVE.n, p.cACT.n, p.cPE.n
        def g1(gp, gate):
            for i in range(n):
                gate.wait(gp, p.cDVE, dv0 + (i - 1) if i >= 2 else dv0)
                p.qL[i % 2].inc(gp.dma_start(out=(ldA if i % 2 == 0 else ldB), in_=chunk(i)), 16)
        p.pool(g1)
        def v1(v, gate):
            for i in range(n):
                gate.wait(v, p.qL[i % 2], 16 * (ql0[i % 2] // 16 + i // 2 + 1))
                v.tensor_reduce(red, (ldA if i % 2 == 0 else ldB), AX.X, ALU.add,
                                apply_absolute_value=True)
                if i == 0:
                    ins = v.tensor_copy(acc, red)
                else:
                    ins = v.tensor_add(acc, acc, red)
                p.cDVE.inc(ins)
        p.dve(v1)
        def t1(t, gate):
            gate.wait(t, p.cDVE, dv0 + n)
            gate.wait(t, p.cACT, ac0)
            p.cPE.inc(t.matmul(self.psA[0:1, 0:1], self.c_ones, acc,
                               start=True, stop=True))
        p.pe(t1)
        def a1(a, gate):
            gate.wait(a, p.cPE, pe0 + 1)
            p.cACT.inc(a.mul(self.smalls[0:1, 3:4], self.psA[0:1, 0:1],
                             0.5 / float(rows * cols)))
        p.act(a1)
        def t2(t, gate):
            gate.wait(t, p.cACT, ac0 + 1)
            p.cPE.inc(t.matmul(self.psA[0:128, 4:5], self.onesrow[:, :],
                               self.smalls[0:1, 3:4], start=True, stop=True))
        p.pe(t2)
        def v2(v, gate):
            gate.wait(v, p.cPE, pe0 + 2)
            p.cDVE.inc(v.tensor_copy(thr, self.psA[:, 4:5]))
        p.dve(v2)
        ql1 = [p.qL[0].n, p.qL[1].n]
        qp0 = [p.qP[0].n, p.qP[1].n]
        dv1, ac1 = p.cDVE.n, p.cACT.n
        def g2(gp, gate):
            gate.wait(gp, p.cDVE, dv1)
            for i in range(n):
                if i >= 2:
                    gate.wait(gp, p.cACT, ac1 + (i - 1))
                p.qL[i % 2].inc(gp.dma_start(out=(ldA if i % 2 == 0 else ldB), in_=chunk(i)), 16)
        p.pool(g2)
        def a2(a, gate):
            for i in range(n):
                gate.wait(a, p.qL[i % 2], 16 * (ql1[i % 2] // 16 + i // 2 + 1))
                if i >= 1:
                    gate.wait(a, p.cDVE, dv1 + i)
                a.activation(absb, (ldA if i % 2 == 0 else ldB), AF.Abs)
                a.sign(sgnb, (ldA if i % 2 == 0 else ldB))
                p.fence(a, p.cACT)
        p.act(a2)
        def v3(v, gate):
            for i in range(n):
                ob = outb[:, (i % 2) * 1024:(i % 2) * 1024 + 1024]
                gate.wait(v, p.cACT, ac1 + i + 1)
                gate.wait(v, p.qP[i % 2], qp0[i % 2] + 16 * (i // 2) if i >= 2
                          else qp0[i % 2])
                v.tensor_scalar(absb, absb, thr, None, ALU.is_gt)
                p.cDVE.inc(v.tensor_mul(ob, absb, sgnb))
        p.dve(v3)
        def s2(sy, gate):
            for i in range(n):
                ob = outb[:, (i % 2) * 1024:(i % 2) * 1024 + 1024]
                gate.wait(sy, p.cDVE, dv1 + i + 1)
                p.qP[i % 2].inc(sy.dma_start(out=dchunk(i), in_=ob), 16)
        p.sp(s2)

    def prep_sign(self, wsrc, ssrc, dst):
        p = self.p
        ldA = self.pool[:, 0:1024]; ldB = self.pool[:, 1024:2048]
        tmp = self.pool[:, 2048:3072]
        scl = self.smalls[:, 8:16]
        outb = self.arena[:, self.o_w:self.o_w + 2048]
        ql0 = [p.qL[0].n, p.qL[1].n]
        qp0 = [p.qP[0].n, p.qP[1].n]
        qsm0 = p.qSm.n
        dv0, ac0 = p.cDVE.n, p.cACT.n
        def g(gp, gate):
            gate.wait(gp, p.cDVE, dv0)
            p.qSm.inc(gp.dma_start(out=scl, in_=ssrc.rearrange("(c p) -> p c", p=128),
                                   allow_slow_non_contiguous=True), 16)
            for r in range(8):
                gate.wait(gp, p.cDVE, dv0 + 2 * (r - 1) if r >= 2 else dv0)
                p.qL[r % 2].inc(gp.dma_start(out=(ldA if r % 2 == 0 else ldB),
                                             in_=wsrc[r * 128:(r + 1) * 128, :]), 16)
        p.pool(g)
        def v(vv, gate):
            for r in range(8):
                gate.wait(vv, p.qSm, qsm0 + 16)
                gate.wait(vv, p.qL[r % 2], 16 * (ql0[r % 2] // 16 + r // 2 + 1))
                gate.wait(vv, p.qP[r % 2], qp0[r % 2] + 16 * (r // 2) if r >= 2
                          else qp0[r % 2])
                vv.tensor_scalar(tmp, (ldA if r % 2 == 0 else ldB), 0.0, None, ALU.is_ge)
                vv.tensor_scalar(tmp, tmp, 2.0, -1.0, ALU.mult, ALU.add)
                ob = outb[:, (r % 2) * 1024:(r % 2) * 1024 + 1024]
                p.cDVE.inc(vv.tensor_scalar_mul(ob, tmp, scl[:, r:r + 1]), 2)
        p.dve(v)
        def s(sy, gate):
            for r in range(8):
                gate.wait(sy, p.cDVE, dv0 + 2 * (r + 1))
                p.qP[r % 2].inc(sy.dma_start(out=dst[r * 128:(r + 1) * 128, :],
                                             in_=(outb[:, (r % 2) * 1024:(r % 2) * 1024 + 1024])), 16)
        p.sp(s)

    def load_x(self, seq):
        p = self.p
        def f(sy, gate):
            gate.wait(sy, p.cDVE); gate.wait(sy, p.cACT); gate.wait(sy, p.cPE)
            p.qX.inc(sy.dma_start(
                self.x_res[:, :, :],
                self.x_in[seq].rearrange("(n p) e -> p n e", p=128)), 16)
        p.sp(f)

    def dump_arena(self, base, seq, n=16384):
        p = self.p
        def f(gp, gate):
            for s_ in p.all_dma:
                gate.wait(gp, s_)
            gate.wait(gp, p.cACT); gate.wait(gp, p.cDVE); gate.wait(gp, p.cPE)
            p.qD.inc(gp.dma_start(
                out=self.y_out[seq].rearrange("(x p) e -> p x e", p=128)[:, :, 0:n // 16],
                in_=self.arena[:, base:base + n].rearrange("p (x e) -> p x e", x=16)), 16)
            gp.wait_ge(p.qD.h, p.qD.n)
        p.pool(f)

    def store_y(self, seq):
        if getattr(self, "dev_stage", "") not in ("", "copy"):
            return
        p = self.p
        def f(sy, gate):
            gate.wait(sy, p.cDVE); gate.wait(sy, p.cACT)
            p.qX.inc(sy.dma_start(
                self.y_out[seq].rearrange("(n p) e -> p n e", p=128),
                self.x_res[:, :, :]), 16)
            sy.wait_ge(p.qX.h, p.qX.n)
        p.sp(f)

    def slot_guard(self, eng, gate, slots, newsem):
        for s_, n_ in self.prep_sems:
            gate.wait(eng, s_, n_)
        if not hasattr(self, "slot_state"):
            self.slot_state = {}
        for sl in slots:
            st = self.slot_state.get(sl)
            if st is not None:
                cons, lsem = st
                for csem, ccnt in cons:
                    gate.wait(eng, csem, ccnt)
                gate.wait(eng, lsem)
        gate.wait(eng, newsem)

    def slot_loaded(self, slots, newsem):
        if not hasattr(self, "slot_state"):
            self.slot_state = {}
        for sl in slots:
            self.slot_state[sl] = ([], newsem)

    def slot_consumed(self, slots, sem):
        if not hasattr(self, "slot_state"):
            self.slot_state = {}
        for sl in slots:
            st = self.slot_state.get(sl)
            if st is None:
                st = ([], self.p.sem("wnull"))
            self.slot_state[sl] = (st[0] + [(sem, sem.n)], st[1])

    def wait_xhatf(self, eng, gate, first_half):
        sems = self.xhatf_half if first_half else self.xhatf_sems
        for s_, n_ in sems:
            gate.wait(eng, s_, n_)

    def ps_wait(self, eng, gate, tensor_name, banks):
        for b in banks:
            k = (tensor_name, b)
            if k in getattr(self, "bankfree", {}):
                sem, cnt = self.bankfree[k]
                gate.wait(eng, sem, cnt)

    def ps_set(self, tensor_name, banks, sem):
        if not hasattr(self, "bankfree"):
            self.bankfree = {}
        for b in banks:
            self.bankfree[(tensor_name, b)] = (sem, sem.n)

    def ln_phase(self, mode="plain", gmul=None, badd=None):
        p, st = self.p, self.stats
        def v_stats(v, gate):
            for s_ in p.all_dma:
                gate.wait(v, s_)
            for t in range(NT):
                v.bn_stats(self.bnb[:, 0:6], self.x_res[:, t, 0:512])
                v.bn_stats(self.bnb[:, 6:12], self.x_res[:, t, 512:1024])
                v.bn_aggr(st[:, 2 * t:2 * t + 2], self.bnb[:, 0:12])
            p.fence(v, p.cDVE)
        p.dve(v_stats)
        dv_stats = p.cDVE.n
        varv = st[:, 1:33:2]
        meanv = st[:, 0:32:2]
        def a_post(a, gate):
            gate.wait(a, p.cDVE, dv_stats)
            a.activation(st[:, 32:48], varv, AF.Ln, bias=self.c_lneps)
            a.activation(st[:, 48:64], st[:, 32:48], AF.Exp, scale=-0.5)
            if mode == "rms":
                a.activation(st[:, 112:128], varv, AF.Ln)
                a.activation(st[:, 112:128], st[:, 112:128], AF.Exp, scale=0.5)
            p.fence(a, p.cACT)
        p.act(a_post)
        ac_post = p.cACT.n
        def v_post(v, gate):
            gate.wait(v, p.cACT, ac_post)
            v.tensor_mul(st[:, 96:112], meanv, st[:, 48:64])
            v.tensor_scalar_mul(st[:, 96:112], st[:, 96:112], -1.0)
            if mode == "plain":
                v.tensor_copy(st[:, 64:80], st[:, 48:64])
                v.tensor_copy(st[:, 80:96], st[:, 96:112])
            elif mode == "rms":
                v.tensor_mul(st[:, 112:128], st[:, 112:128], st[:, 48:64])
                v.tensor_scalar(st[:, 112:128], st[:, 112:128], self.c_rmseps,
                                None, ALU.add)
                v.reciprocal(st[:, 112:128], st[:, 112:128])
                v.tensor_mul(st[:, 64:80], st[:, 48:64], st[:, 112:128])
                v.tensor_mul(st[:, 80:96], st[:, 96:112], st[:, 112:128])
            elif mode == "l1b":
                v.tensor_mul(st[:, 112:128], varv, st[:, 48:64])
                v.tensor_mul(st[:, 112:128], st[:, 112:128], st[:, 48:64])
            p.fence(v, p.cDVE)
        p.dve(v_post)
        dv_post = p.cDVE.n
        if mode == "l1b":
            def a_l1b(a, gate):
                gate.wait(a, p.cDVE, dv_post)
                a.activation(st[:, 112:128], st[:, 112:128], AF.Ln, bias=self.c_lneps)
                a.activation(st[:, 112:128], st[:, 112:128], AF.Exp, scale=-0.5)
                p.fence(a, p.cACT)
            p.act(a_l1b)
            ac_l1b = p.cACT.n
            def v_l1b(v, gate):
                gate.wait(v, p.cACT, ac_l1b)
                v.tensor_mul(st[:, 64:80], st[:, 48:64], st[:, 112:128])
                v.tensor_mul(st[:, 80:96], st[:, 96:112], st[:, 112:128])
                p.fence(v, p.cDVE)
            p.dve(v_l1b)
            dv_post = p.cDVE.n
        bufs = [self.arena[:, self.o_pt + i * 1024: self.o_pt + (i + 1) * 1024]
                for i in range(2)]
        xs = [p.sem("xs0"), p.sem("xs1"), p.sem("xs2"), p.sem("xs3")]
        xs0 = [s.n for s in xs]
        ssem = lambda t: (t // 8) * 2 + (t % 2)
        sval = lambda t: xs0[ssem(t)] + 16 * (((t % 8) // 2) + 1)
        ac_hat0 = p.cACT.n
        def a_hat(a, gate):
            gate.wait(a, p.cDVE, dv_post)
            for t in range(NT):
                if t >= 2:
                    gate.wait(a, xs[ssem(t - 2)], sval(t - 2))
                if gmul is not None and t >= 2:
                    gate.wait(a, p.cDVE, dv_post + t - 1)
                p.cACT.inc(a.activation(bufs[t % 2], self.x_res[:, t, :], AF.Identity,
                                        bias=st[:, 80 + t:81 + t],
                                        scale=st[:, 64 + t:65 + t]))
        p.act(a_hat)
        def v_g(v, gate):
            for t in range(NT):
                gate.wait(v, p.cACT, ac_hat0 + t + 1)
                if t >= 2:
                    gate.wait(v, xs[ssem(t - 2)], sval(t - 2))
                ins = v.tensor_mul(bufs[t % 2], bufs[t % 2], gmul)
                if badd is not None:
                    ins = v.tensor_add(bufs[t % 2], bufs[t % 2], badd)
                p.cDVE.inc(ins)
        if gmul is not None:
            p.dve(v_g)
        half_holder = {}
        def g_store(gp, gate):
            gate.wait(gp, p.cACT, ac_hat0); gate.wait(gp, p.cDVE, dv_post)
            for t in range(NT):
                if gmul is None:
                    gate.wait(gp, p.cACT, ac_hat0 + t + 1)
                else:
                    gate.wait(gp, p.cDVE, dv_post + t + 1)
                xs[ssem(t)].inc(gp.dma_start(out=self.xhat_d[t * 128:(t + 1) * 128, :],
                                             in_=bufs[t % 2]), 16)
        p.pool(g_store)
        def s_tr(sy, gate):
            gate.wait(sy, xs[0], xs0[0] + 64); gate.wait(sy, xs[1], xs0[1] + 64)
            qh2 = p.sem("xhTA")
            for c in range(DCH):
                qh2.inc(sy.dma_start(self.av(self.o_xhatf, c)[:, 0:1024],
                                     self.xhat_d[0:1024, c * 128:(c + 1) * 128],
                                     transpose=True), 16)
            half_holder["h"] = [(qh2, qh2.n), (xs[0], xs0[0] + 64),
                                (xs[1], xs0[1] + 64)]
            gate.wait(sy, xs[2]); gate.wait(sy, xs[3])
            for c in range(DCH):
                p.qH.inc(sy.dma_start(self.av(self.o_xhatf, c)[:, 1024:2048],
                                      self.xhat_d[1024:2048, c * 128:(c + 1) * 128],
                                      transpose=True), 16)
        p.sp(s_tr)
        self.xhatf_half = half_holder["h"]
        self.xhatf_ready = p.qH.n
        self.xhatf_sems = [(p.qH, p.qH.n), (p.sem("xhTA"), p.sem("xhTA").n)]

    def build_bcast(self, dst_pool_ap, src_row_ap, width, gate_sem=None):
        p = self.p
        nch = (width + 511) // 512
        def t(tt, gate):
            if gate_sem is not None:
                gate.wait(tt, gate_sem[0], gate_sem[1])
            self.ps_wait(tt, gate, "psB", [2, 3])
            for i in range(nch):
                w = min(512, width - i * 512)
                p.cPE.inc(tt.matmul(
                    self.psB[:, 1024 + i * 512:1024 + i * 512 + w],
                    self.onesrow[:, :], src_row_ap[:, i * 512:i * 512 + w],
                    start=True, stop=True))
        p.pe(t)
        pe_n = p.cPE.n
        def a(aa, gate):
            gate.wait(aa, p.cPE, pe_n)
            aa.copy(dst_pool_ap[:, 0:width], self.psB[:, 1024:1024 + width])
            p.fence(aa, p.cACT)
        p.act(a)
        self.ps_set("psB", [2, 3], p.cACT)

    def emit_attn(self, v):
        kind, causal, scale = v["kind"], v["causal"], v["scale"]
        p = self.p
        self.ln_phase("plain")
        if kind == "l0b":
            self.emit_attn_setup_l0b()
        elif kind == "l1a":
            self.emit_attn_setup_l1a()
        self.emit_vones()
        for hf in range(2):
            self.attn_qkv(kind, hf)
            self.qkv_ready = (p.cACT.n, p.cDVE.n)
            for hl in range(8):
                self.attn_head(kind, hf, hl, causal, scale)
            self.attn_outproj(kind, hf)

    def emit_attn_setup_l0b(self):
        p = self.p
        if True:
            def g(gp, gate):
                gate.wait(gp, p.cACT); gate.wait(gp, p.cPE)
                p.qSm.inc(gp.dma_start(out=self.smalls[:, 16:40],
                                       in_=self.W["L0_attn_b_b_qkv"]
                                       .rearrange("(c p) -> p c", p=128),
                                       allow_slow_non_contiguous=True), 16)
                p.qSm.inc(gp.dma_start(out=self.pool[0:1, 3072:4096],
                                       in_=self.W["L0_attn_b_b_qkv"][None, 2048:3072]), 16)
            p.pool(g)
            self.build_bcast(self.bb, self.pool[0:1, 3072:4096], 1024,
                             gate_sem=(p.qSm, p.qSm.n))
            def g2(gp, gate):
                gate.wait(gp, p.cPE)
                p.qSm.inc(gp.dma_start(out=self.pool[0:1, 3072:4096],
                                       in_=self.W["L0_attn_b_b_o"][None, :]), 16)
            p.pool(g2)
            self.build_bcast(self.bb2, self.pool[0:1, 3072:4096], 1024,
                             gate_sem=(p.qSm, p.qSm.n))
    def emit_attn_setup_l1a(self):
        p = self.p
        if True:
            def g(gp, gate):
                gate.wait(gp, p.cACT); gate.wait(gp, p.cPE); gate.wait(gp, p.cDVE)
                p.qSm.inc(gp.dma_start(out=self.pool[0:1, 3072:4096],
                                       in_=self.W["L1_attn_a_bout"][None, :]), 16)
                p.qSm.inc(gp.dma_start(
                    out=self.arena[:, self.o_misc:self.o_misc + 512]
                    .rearrange("p (c o) -> p c o", c=8),
                    in_=self.W["L1_attn_a_Wqa"].rearrange("(c p) o -> p c o", p=128)), 16)
                p.qSm.inc(gp.dma_start(
                    out=self.arena[:, self.o_misc + 512:self.o_misc + 768]
                    .rearrange("p (c o) -> p c o", c=8),
                    in_=self.W["L1_attn_a_Wkva"].rearrange("(c p) o -> p c o", p=128)), 16)
                p.qSm.inc(gp.dma_start(
                    out=self.arena[0:32, self.o_misc + 768:self.o_misc + 1280],
                    in_=self.W["L1_attn_a_Wkvb"][:, :]), 16)
                p.qSm.inc(gp.dma_start(
                    out=self.arena[:, self.o_misc + 1792:self.o_misc + 1920],
                    in_=self.perm128[:, :]), 16)
            p.pool(g)
            self.build_bcast(self.bb, self.pool[0:1, 3072:4096], 1024,
                             gate_sem=(p.qSm, p.qSm.n))
            qg_n = p.qSm.n
            def vv(vvv, gate):
                gate.wait(vvv, p.qSm, qg_n)
                w3 = self.arena[0:32, self.o_misc + 768:self.o_misc + 1280] \
                    .rearrange("p (h d two) -> p h two d", h=4, two=2)
                vvv.tensor_copy(self.arena[0:32, self.o_misc + 1280:self.o_misc + 1536]
                                .rearrange("p (h d) -> p h d", h=4), w3[:, :, 0, :])
                vvv.tensor_copy(self.arena[0:32, self.o_misc + 1536:self.o_misc + 1792]
                                .rearrange("p (h d) -> p h d", h=4), w3[:, :, 1, :])
                p.fence(vvv, p.cDVE)
            p.dve(vv)
            self.l1a_repack_n = p.cDVE.n
    def emit_vones(self):
        p = self.p
        def vones(vvv, gate):
            gate.wait(vvv, p.cPE)
            vvv.memset(self.arena[:, self.o_v:self.o_v + 16 * 520], 1.0)
            p.fence(vvv, p.cDVE)
        p.dve(vones)
        self.vones_n = p.cDVE.n

    def attn_qkv(self, kind, hf):
        p = self.p
        xf = lambda c: self.av(self.o_xhatf, c)
        if kind in ("l0a", "l0b"):
            wnames = {"l0a": ["L0_attn_a_Wq", "L0_attn_a_Wk", "L0_attn_a_Wv"]}
            for pi, proj in enumerate(("q", "k", "v")):
                wslot = self.o_w if pi % 2 == 0 else self.o_w2
                wbuf = lambda c, s=wslot: self.arena[:, s + c * 512: s + (c + 1) * 512]
                pe_before = p.cPE.n
                sl = "0" if wslot == self.o_w else "1"
                qw = p.sem("w" + sl + ("g" if kind == "l0a" else "s"))
                if kind == "l0a":
                    wsrc = self.W[wnames["l0a"][pi]]
                    def g(gp, gate, wsrc=wsrc, wslot=wslot, qw=qw):
                        self.slot_guard(gp, gate, [wslot], qw)
                        qw.inc(gp.dma_start(
                            out=self.arena[:, wslot:wslot + 4096]
                            .rearrange("p (c o) -> p c o", c=8),
                            in_=wsrc[:, hf * 512:(hf + 1) * 512]
                            .rearrange("(c p) o -> p c o", p=128)), 16)
                    p.pool(g)
                    wready = (qw, qw.n)
                else:
                    rowbase = pi * 1024 + hf * 512
                    def s(sy, gate, rowbase=rowbase, wslot=wslot, qw=qw):
                        self.slot_guard(sy, gate, [wslot], qw)
                        gate.wait(sy, p.qC, self.cache_qg)
                        for c in range(8):
                            qw.inc(sy.dma_start(
                                self.arena[:, wslot + c * 512: wslot + (c + 1) * 512],
                                self.cache["qkv0"][rowbase:rowbase + 512,
                                                   c * 128:(c + 1) * 128],
                                transpose=True), 16)
                    p.sp(s)
                    wready = (qw, qw.n)
                self.slot_loaded([wslot], qw)
                if proj in ("q", "k"):
                    dstbase = self.o_q if proj == "q" else self.o_k
                    self.proj_feature_major(wbuf, wready, dstbase, kind, pi, hf)
                    self.slot_consumed([wslot], p.cACT)
                else:
                    self.proj_v_token_major(wbuf, wready, kind, hf)
                    self.slot_consumed([wslot], p.cDVE)
        else:
            self.l1a_qkv(hf)

    def proj_feature_major(self, wbuf, wready, dstbase, kind, pi, hf):
        p = self.p
        xf = lambda c: self.av(self.o_xhatf, c)
        for tc in range(4):
            for oc in range(4):
                bank = (tc * 4 + oc) % 4
                def t(tt, gate, oc=oc, tc=tc, bank=bank):
                    self.wait_xhatf(tt, gate, tc < 2)
                    gate.wait(tt, wready[0], wready[1])
                    self.ps_wait(tt, gate, "psA", [bank])
                    for c in range(8):
                        ins = tt.matmul(self.psA[:, bank * 512:(bank + 1) * 512],
                                        wbuf(c)[:, oc * 128:(oc + 1) * 128],
                                        xf(c)[:, tc * 512:(tc + 1) * 512],
                                        start=(c == 0), stop=(c == 7))
                    p.cPE.inc(ins)
                p.pe(t)
                pe_n = p.cPE.n
                def a(aa, gate, oc=oc, tc=tc, bank=bank, pe_n=pe_n):
                    gate.wait(aa, p.cPE, pe_n)
                    dst = self.arena[:, dstbase + oc * T + tc * 512:
                                     dstbase + oc * T + tc * 512 + 512]
                    if kind == "l0b":
                        bcol = pi * 8 + hf * 4 + oc
                        p.cACT.inc(aa.activation(dst, self.psA[:, bank * 512:(bank + 1) * 512],
                                                 AF.Identity, bias=self.smalls[:, 16 + bcol:17 + bcol]))
                    else:
                        p.cACT.inc(aa.copy(dst, self.psA[:, bank * 512:(bank + 1) * 512]))
                p.act(a)
                self.ps_set("psA", [bank], p.cACT)

    def proj_v_token_major(self, wbuf, wready, kind, hf):
        p = self.p
        xf = lambda c: self.av(self.o_xhatf, c)
        for kt in range(NT):
            bank = kt % 4
            def t(tt, gate, kt=kt, bank=bank):
                self.wait_xhatf(tt, gate, kt < 8)
                gate.wait(tt, wready[0], wready[1])
                gate.wait(tt, p.cDVE, self.vones_n)
                self.ps_wait(tt, gate, "psB", [bank])
                for c in range(8):
                    ins = tt.matmul(self.psB[:, bank * 512:(bank + 1) * 512],
                                    xf(c)[:, kt * 128:(kt + 1) * 128],
                                    wbuf(c),
                                    start=(c == 0), stop=(c == 7))
                p.cPE.inc(ins)
            p.pe(t)
            pe_n = p.cPE.n
            dst = self.arena[:, self.o_v + kt * 520:self.o_v + (kt + 1) * 520] \
                .rearrange("p (h c) -> p h c", h=8)[:, :, 0:64]
            def vv(vvv, gate, kt=kt, bank=bank, pe_n=pe_n, dst=dst):
                gate.wait(vvv, p.cPE, pe_n)
                if kind == "l0b":
                    p.cDVE.inc(vvv.scalar_tensor_tensor(
                        dst, self.psB[:, bank * 512:(bank + 1) * 512], 1.0,
                        self.bb[:, hf * 512:hf * 512 + 512], ALU.mult, ALU.add))
                else:
                    p.cDVE.inc(vvv.tensor_copy(dst, self.psB[:, bank * 512:(bank + 1) * 512]))
            p.dve(vv)
            self.ps_set("psB", [bank], p.cDVE)

    def attn_head(self, kind, hf, hl, causal, scale):
        for qh in range(2):
            self.attn_unit(kind, hf, hl, causal, scale, qh)

    def attn_unit(self, kind, hf, hl, causal, scale, qh):
        p = self.p
        qtile, qrow = hl // 2, (hl % 2) * 64
        if kind == "l1a":
            ktile, krow, vslot = hl // 4, qrow, hl // 4
        else:
            ktile, krow, vslot = hl // 2, (hl % 2) * 64, hl
        Kf = self.arena[krow:krow + 64, self.o_k + ktile * T:self.o_k + (ktile + 1) * T]
        Qf = self.arena[qrow:qrow + 64, self.o_q + qtile * T:self.o_q + (qtile + 1) * T]
        ptile = [self.arena[:, self.o_pt + i * 1024:self.o_pt + (i + 1) * 1024]
                 for i in range(2)]
        pair = hl // 2
        q_lo = qh * 1024
        iters = []
        for kt in range(NT):
            q0 = max(q_lo, kt * 128) if causal else q_lo
            if q0 >= q_lo + 1024:
                continue
            iters.append((kt, q0, q_lo + 1024 - q0))
        ng = len(iters)
        last_kt = iters[-1][0]
        if not hasattr(self, "sslot"):
            self.sslot = 0
        pe0, ac0, dv0 = p.cPE.n, p.cACT.n, p.cDVE.n
        nmm = lambda w: 1 if w <= 512 else 2
        def bank_parts(kt, q0, w):
            out = []
            for b in range(q0 // 512, (q0 + w + 511) // 512):
                s, e = max(q0, b * 512), min(q0 + w, (b + 1) * 512)
                lk = min(NT - 1, 4 * b + 3) if causal else NT - 1
                out.append((s, e, kt == lk))
            return out
        smark = {}
        gmark = {g: ac0 + g + 1 for g in range(ng)}
        cnt = pe0
        sched = [("S", 0)]
        for g in range(ng):
            if g + 1 < ng:
                sched.append(("S", g + 1))
            sched.append(("AV", g))
        slot_of = {}
        for op, g in sched:
            kt, q0, w = iters[g]
            if op == "S":
                slot_of[g] = (self.sslot % 2)
                self.sslot += 1
                cnt += nmm(w)
                smark[g] = cnt
            else:
                for (s, e, stop) in bank_parts(kt, q0, w):
                    if stop:
                        cnt += 1
        pe_av = cnt
        diag = {g: (causal and iters[g][1] == iters[g][0] * 128 and
                    iters[g][0] * 128 >= q_lo) for g in range(ng)}
        dmark = {}
        j = 0
        for g in range(ng):
            if diag[g]:
                j += 1
                dmark[g] = dv0 + j

        def v_mask(v, gate):
            gate.wait(v, p.sem("mask"), self.cmask_ready)
            for g in range(ng):
                if not diag[g]:
                    continue
                c0 = slot_of[g] * 1024
                gate.wait(v, p.cPE, smark[g])
                p.cDVE.inc(v.tensor_add(self.psA[:, c0:c0 + 128],
                                        self.psA[:, c0:c0 + 128],
                                        self.cmask_sb[:, :]))
        if causal and any(diag.values()):
            p.dve(v_mask)

        def t_s(tt, gate):
            self.wait_xhatf(tt, gate, False)
            gate.wait(tt, p.cACT, self.qkv_ready[0])
            gate.wait(tt, p.cDVE, self.qkv_ready[1])
            def emit_S(g):
                kt, q0, w = iters[g]
                par = slot_of[g]
                self.ps_wait(tt, gate, "psA", [par * 2, par * 2 + 1])
                if g >= 2:
                    gate.wait(tt, p.cACT, gmark[g - 2])
                for s0 in range(0, w, 512):
                    sw = min(512, w - s0)
                    p.cPE.inc(tt.matmul(
                        self.psA[:, par * 1024 + s0:par * 1024 + s0 + sw],
                        Kf[:, kt * 128:(kt + 1) * 128],
                        Qf[:, q0 + s0:q0 + s0 + sw], start=True, stop=True))
            def emit_AV(g):
                kt, q0, w = iters[g]
                par = slot_of[g]
                if g == 0:
                    self.ps_wait(tt, gate, "psB", [2 * qh, 2 * qh + 1])
                gate.wait(tt, p.cACT, gmark[g])
                for (s, e, stop) in bank_parts(kt, q0, w):
                    ins = tt.matmul(self.psB[0:65, s:e],
                                    self.arena[:, self.o_v + kt * 520 + vslot * 65:
                                               self.o_v + kt * 520 + vslot * 65 + 65],
                                    ptile[par][:, s - q0:e - q0],
                                    start=(kt == 0), stop=stop)
                    if stop:
                        p.cPE.inc(ins)
            emit_S(0)
            for g in range(ng):
                if g + 1 < ng:
                    emit_S(g + 1)
                emit_AV(g)
        p.pe(t_s)
        assert p.cPE.n == pe_av, (p.cPE.n, pe_av)

        def a_exp(a, gate):
            for g in range(ng):
                kt, q0, w = iters[g]
                par = slot_of[g]
                gate.wait(a, p.cPE, smark[g])
                if g in dmark:
                    gate.wait(a, p.cDVE, dmark[g])
                p.cACT.inc(a.activation(ptile[par][:, 0:w],
                                        self.psA[:, par * 1024:par * 1024 + w],
                                        AF.Exp, bias=self.c_negoff, scale=scale))
        p.act(a_exp)
        self.ps_set("psA", [0, 1, 2, 3], p.cACT)
        def a_r(a, gate):
            gate.wait(a, p.cPE, pe_av)
            a.activation(self.grow[:, 0:1024],
                         self.psB[64:65, q_lo:q_lo + 1024], AF.Ln)
            p.cACT.inc(a.activation(self.growb[:, :], self.grow[:, 0:1024],
                                    AF.Exp, scale=-1.0))
        p.act(a_r)
        ac_n = p.cACT.n
        def t_z(tt, gate, ac_n=ac_n):
            gate.wait(tt, p.cACT, ac_n)
            self.ps_wait(tt, gate, "psA", [2, 3])
            p.cPE.inc(tt.matmul(self.psA[0:64, 1024:1536], self.onesbf[:, :],
                                self.growb[:, 0:512], start=True, stop=True))
            p.cPE.inc(tt.matmul(self.psA[0:64, 1536:2048], self.onesbf[:, :],
                                self.growb[:, 512:1024], start=True, stop=True))
        p.pe(t_z)
        pe_n = p.cPE.n
        dst = self.arena[qrow:qrow + 64,
                         self.o_osp + pair * T + q_lo:self.o_osp + pair * T + q_lo + 1024]
        def v_m(v, gate, dst=dst, pe_n=pe_n):
            gate.wait(v, p.cPE, pe_n)
            v.tensor_copy(self.zb[0:64, :], self.psA[0:64, 1024:2048])
            p.cDVE.inc(v.tensor_mul(dst, self.psB[0:64, q_lo:q_lo + 1024],
                                    self.zb[0:64, :]))
        p.dve(v_m)
        self.ps_set("psA", [2, 3], p.cDVE)
        self.ps_set("psB", [2 * qh, 2 * qh + 1], p.cDVE)

    def attn_outproj(self, kind, hf):
        p = self.p
        wslot = self.o_w
        qwo = p.sem("w0s" if kind == "l0b" else "w0g")
        if kind == "l0b":
            def s(sy, gate):
                self.slot_guard(sy, gate, [wslot], qwo)
                gate.wait(sy, p.qC, self.cache_qg)
                for j in range(4):
                    qwo.inc(sy.dma_start(
                        self.arena[:, wslot + j * 1024:wslot + (j + 1) * 1024],
                        self.cache["wo0"][0:1024, (hf * 4 + j) * 128:
                                          (hf * 4 + j) * 128 + 128],
                        transpose=True), 16)
            p.sp(s)
            self.slot_loaded([wslot], qwo)
            wready = (qwo, qwo.n)
        else:
            wname = "L0_attn_a_Wo" if kind == "l0a" else "L1_attn_a_Wout"
            def g(gp, gate):
                self.slot_guard(gp, gate, [wslot], qwo)
                qwo.inc(gp.dma_start(
                    out=self.arena[:, wslot:wslot + 4096]
                    .rearrange("p (c e) -> p c e", c=4),
                    in_=self.W[wname][hf * 512:(hf + 1) * 512, :]
                    .rearrange("(c p) e -> p c e", p=128)), 16)
            p.pool(g)
            self.slot_loaded([wslot], qwo)
            wready = (qwo, qwo.n)
        self.attn_o_done = p.cDVE.n
        if kind == "l1a" and hf == 1:
            def g2(gp, gate):
                gate.wait(gp, p.cPE); gate.wait(gp, p.cACT)
                p.qSm.inc(gp.dma_start(out=self.pool[0:1, 3072:4096],
                                       in_=self.W["L1_attn_a_bout"][None, :]), 16)
            p.pool(g2)
            self.build_bcast(self.bb, self.pool[0:1, 3072:4096], 1024,
                             gate_sem=(p.qSm, p.qSm.n))
        bb_ready = p.cACT.n
        for tt in range(NT):
            banks = [0, 1] if tt % 2 == 0 else [2, 3]
            cb = banks[0] * 512
            def t(t_, gate, tt=tt, banks=banks, cb=cb):
                gate.wait(t_, wready[0], wready[1])
                gate.wait(t_, p.cDVE, self.attn_o_done)
                self.ps_wait(t_, gate, "psA", banks)
                for eh in range(2):
                    for j in range(4):
                        ins = t_.matmul(
                            self.psA[:, cb + eh * 512:cb + eh * 512 + 512],
                            self.arena[:, self.o_osp + j * T + tt * 128:
                                       self.o_osp + j * T + tt * 128 + 128],
                            self.arena[:, wslot + j * 1024 + eh * 512:
                                       wslot + j * 1024 + eh * 512 + 512],
                            start=(j == 0), stop=(j == 3))
                    p.cPE.inc(ins)
            p.pe(t)
            pe_n = p.cPE.n
            extras = (kind == "l0b" and hf == 0)
            if extras:
                def a_h(aa, gate, tt=tt):
                    gate.wait(aa, p.cDVE)
                    p.cACT.inc(aa.activation(self.zb, self.x_res[:, tt, :], AF.Identity,
                                             bias=self.stats[:, 80 + tt:81 + tt],
                                             scale=self.stats[:, 64 + tt:65 + tt]))
                p.act(a_h)
                ac_n = p.cACT.n
            def v_e(v, gate, tt=tt, cb=cb, pe_n=pe_n, extras=extras):
                gate.wait(v, p.cPE, pe_n)
                if extras:
                    gate.wait(v, p.cACT, ac_n)
                ins = v.tensor_add(self.x_res[:, tt, :], self.x_res[:, tt, :],
                                   self.psA[:, cb:cb + 1024])
                if extras:
                    v.tensor_add(self.x_res[:, tt, :], self.x_res[:, tt, :], self.zb)
                    ins = v.tensor_add(self.x_res[:, tt, :], self.x_res[:, tt, :],
                                       self.bb2)
                if kind == "l1a" and hf == 1:
                    gate.wait(v, p.cACT, bb_ready)
                    ins = v.tensor_add(self.x_res[:, tt, :], self.x_res[:, tt, :],
                                       self.bb)
                p.cDVE.inc(ins)
            p.dve(v_e)
            self.ps_set("psA", banks, p.cDVE)
        self.slot_consumed([wslot], p.cDVE)

    def l1a_qkv(self, hf):
        p = self.p
        xf = lambda c: self.av(self.o_xhatf, c)
        om = self.o_misc
        qlat = self.arena[0:64, self.o_w2:self.o_w2 + 2048]
        kvlat = self.arena[0:32, self.o_w2 + 2048:self.o_w2 + 4096]
        wqa = lambda c: self.arena[:, om + c * 64:om + (c + 1) * 64]
        wkva = lambda c: self.arena[:, om + 512 + c * 32:om + 512 + (c + 1) * 32]
        wkvb_k = self.arena[0:32, om + 1280:om + 1536]
        wkvb_v = self.arena[0:32, om + 1536:om + 1792]
        perm_sb = self.arena[:, om + 1792:om + 1920]
        if hf == 0:
            for which, outap, lat, nlat in (("q", qlat, wqa, 64), ("kv", kvlat, wkva, 32)):
                for tc in range(4):
                    bank = tc % 4
                    qsm_n = p.qSm.n
                    def t_l(tt, gate, outap=outap, lat=lat, nlat=nlat, tc=tc,
                            bank=bank, qsm_n=qsm_n):
                        self.wait_xhatf(tt, gate, tc < 2)
                        gate.wait(tt, p.qSm, qsm_n)
                        self.ps_wait(tt, gate, "psA", [bank])
                        for c in range(8):
                            ins = tt.matmul(self.psA[0:nlat, bank * 512:bank * 512 + 512],
                                            lat(c), xf(c)[:, tc * 512:(tc + 1) * 512],
                                            start=(c == 0), stop=(c == 7))
                        p.cPE.inc(ins)
                    p.pe(t_l)
                    pe_n = p.cPE.n
                    def a_l(aa, gate, outap=outap, nlat=nlat, tc=tc, bank=bank, pe_n=pe_n):
                        gate.wait(aa, p.cPE, pe_n)
                        p.cACT.inc(aa.copy(outap[:, tc * 512:(tc + 1) * 512],
                                           self.psA[0:nlat, bank * 512:bank * 512 + 512]))
                    p.act(a_l)
                    self.ps_set("psA", [bank], p.cACT)
        qwb = p.sem("wqb")
        def g_w(gp, gate):
            self.slot_guard(gp, gate, [self.o_w], qwb)
            qwb.inc(gp.dma_start(out=self.arena[0:64, self.o_w:self.o_w + 512],
                                 in_=self.W["L1_attn_a_Wqb"][:, hf * 512:(hf + 1) * 512]), 16)
        p.pool(g_w)
        self.slot_loaded([self.o_w], qwb)
        wqb_n = qwb.n
        for oc in range(4):
            for tc in range(4):
                bank = (oc * 4 + tc) % 4
                def t_q(tt, gate, oc=oc, tc=tc, bank=bank):
                    gate.wait(tt, qwb, wqb_n)
                    gate.wait(tt, p.cACT)
                    self.ps_wait(tt, gate, "psA", [bank])
                    p.cPE.inc(tt.matmul(self.psA[:, bank * 512:bank * 512 + 512],
                                        self.arena[0:64, self.o_w + oc * 128:
                                                   self.o_w + oc * 128 + 128],
                                        qlat[:, tc * 512:(tc + 1) * 512],
                                        start=True, stop=True))
                p.pe(t_q)
                pe_n = p.cPE.n
                def a_q(aa, gate, oc=oc, tc=tc, bank=bank, pe_n=pe_n):
                    gate.wait(aa, p.cPE, pe_n)
                    dst = self.arena[:, self.o_q + oc * T + tc * 512:
                                     self.o_q + oc * T + tc * 512 + 512]
                    p.cACT.inc(aa.copy(dst, self.psA[:, bank * 512:bank * 512 + 512]))
                p.act(a_q)
                self.ps_set("psA", [bank], p.cACT)
        self.slot_consumed([self.o_w], p.cACT)
        for kvl in range(2):
            kvh = hf * 2 + kvl
            for tc in range(4):
                bank = tc % 4
                def t_k(tt, gate, kvh=kvh, kvl=kvl, tc=tc, bank=bank):
                    gate.wait(tt, p.cDVE, self.l1a_repack_n)
                    self.ps_wait(tt, gate, "psA", [bank])
                    p.cPE.inc(tt.matmul(self.psA[0:64, bank * 512:bank * 512 + 512],
                                        wkvb_k[:, kvh * 64:(kvh + 1) * 64],
                                        kvlat[:, tc * 512:(tc + 1) * 512],
                                        start=True, stop=True))
                p.pe(t_k)
                pe_n = p.cPE.n
                def a_k(aa, gate, kvl=kvl, tc=tc, bank=bank, pe_n=pe_n):
                    gate.wait(aa, p.cPE, pe_n)
                    base = self.o_k + kvl * T
                    aa.copy(self.arena[0:64, base + tc * 512:base + tc * 512 + 512],
                            self.psA[0:64, bank * 512:bank * 512 + 512])
                    p.cACT.inc(aa.copy(
                        self.arena[64:128, base + tc * 512:base + tc * 512 + 512],
                        self.psA[0:64, bank * 512:bank * 512 + 512]))
                p.act(a_k)
                self.ps_set("psA", [bank], p.cACT)
        for kt in range(NT):
            bank = kt % 4
            def t_v(tt, gate, kt=kt, bank=bank):
                gate.wait(tt, p.cDVE, max(self.l1a_repack_n, self.vones_n))
                self.ps_wait(tt, gate, "psB", [bank])
                p.cPE.inc(tt.matmul(self.psB[:, bank * 512:bank * 512 + 128],
                                    kvlat[:, kt * 128:(kt + 1) * 128],
                                    wkvb_v[:, hf * 128:(hf + 1) * 128],
                                    start=True, stop=True))
            p.pe(t_v)
            pe_n = p.cPE.n
            dst = self.arena[:, self.o_v + kt * 520:self.o_v + kt * 520 + 130] \
                .rearrange("p (h c) -> p h c", h=2)[:, :, 0:64]
            def v_v(v, gate, kt=kt, bank=bank, pe_n=pe_n, dst=dst):
                gate.wait(v, p.cPE, pe_n)
                p.cDVE.inc(v.tensor_copy(
                    dst, self.psB[:, bank * 512:bank * 512 + 128]
                    .rearrange("p (h c) -> p h c", h=2)))
            p.dve(v_v)
            self.ps_set("psB", [bank], p.cDVE)
        for ti in range(4):
            self.rope_tile(self.arena[:, self.o_q + ti * T:self.o_q + (ti + 1) * T],
                           perm_sb)
        self.rope_tile(self.arena[:, self.o_k:self.o_k + T], perm_sb)
        self.rope_tile(self.arena[:, self.o_k + T:self.o_k + 2 * T], perm_sb)

    def rope_tile(self, tile_ap, perm_sb):
        p = self.p
        qcs = p.sem("ropecs")
        C = self.pool[:, 1024:2048]
        S = self.pool[:, 2048:3072]
        zbuf = self.pool[:, 0:1024]
        for ch in range(2):
            key = (self.seq, self.sub, getattr(self, "_rope_hf", 0), ch)
            if getattr(self, "_rope_loaded", None) != (self.seq, ch, self.sub):
                dv_n = p.cDVE.n
                ac_n = p.cACT.n
                def s_cs(sy, gate, ch=ch, dv_n=dv_n, ac_n=ac_n):
                    gate.wait(sy, p.cDVE, dv_n)
                    gate.wait(sy, p.cACT, ac_n)
                    gate.wait(sy, qcs)
                    qcs.inc(sy.dma_start(C, self.rope_c[:, ch * 1024:(ch + 1) * 1024]), 16)
                    qcs.inc(sy.dma_start(S, self.rope_s[:, ch * 1024:(ch + 1) * 1024]), 16)
                p.sp(s_cs)
                self._rope_loaded = (self.seq, ch, self.sub)
            qcs_n = qcs.n
            def t_p(tt, gate, ch=ch):
                gate.wait(tt, p.cACT)
                gate.wait(tt, p.qSm)
                self.ps_wait(tt, gate, "psB", [0, 1])
                for j in range(2):
                    p.cPE.inc(tt.matmul(self.psB[:, j * 512:j * 512 + 512],
                                        perm_sb,
                                        tile_ap[:, ch * 1024 + j * 512:
                                                ch * 1024 + j * 512 + 512],
                                        start=True, stop=True))
            p.pe(t_p)
            pe_n = p.cPE.n
            def v_r(v, gate, ch=ch, pe_n=pe_n, qcs_n=qcs_n):
                gate.wait(v, p.cPE, pe_n)
                gate.wait(v, qcs, qcs_n)
                v.tensor_mul(zbuf, tile_ap[:, ch * 1024:(ch + 1) * 1024], C)
                v.tensor_mul(self.psB[:, 0:1024], self.psB[:, 0:1024], S)
                p.cDVE.inc(v.tensor_add(tile_ap[:, ch * 1024:(ch + 1) * 1024],
                                        self.psB[:, 0:1024], zbuf))
            p.dve(v_r)
            self.ps_set("psB", [0, 1], p.cDVE)

    def emit_l0ffn(self):
        p = self.p
        self.ln_phase("rms")
        oH = self.o_osp
        xf = lambda c: self.av(self.o_xhatf, c)
        Ht = lambda fc: self.arena[:, oH + fc * 1024: oH + (fc + 1) * 1024]
        onescell = self.arena[:, self.o_pt + 2048:self.o_pt + 2049]
        def v_ones(v, gate):
            gate.wait(v, p.cPE)
            v.memset(onescell, 1.0)
            p.fence(v, p.cDVE)
        p.dve(v_ones)
        qup = p.sem("wup")
        qdn = [p.sem("wdn0"), p.sem("wdn1")]
        qr2 = p.sem("r2")
        prep_w = [(s, s.n) for s in (p.qP[0], p.qP[1])]
        for sc in range(2):
            t0 = sc * 1024
            pending_sq = None
            for g in range(4):
                acn = p.cACT.n
                def s_w(sy, gate, g=g, acn=acn):
                    self.slot_guard(sy, gate, [self.o_w, self.o_w2], qup)
                    gate.wait(sy, p.cACT, acn)
                    for c in range(8):
                        qup.inc(sy.dma_start(
                            self.arena[:, self.o_w + c * 1024:self.o_w + (c + 1) * 1024],
                            self.cache["wup"][g * 1024:(g + 1) * 1024,
                                              c * 128:(c + 1) * 128],
                            transpose=True), 16)
                p.sp(s_w)
                self.slot_loaded([self.o_w, self.o_w2], qup)
                wn = qup.n
                for fcl in range(8):
                    fc = g * 8 + fcl
                    for th in range(2):
                        bank = (fc * 2 + th) % 4
                        def t_up(tt, gate, fc=fc, fcl=fcl, th=th, bank=bank, wn=wn):
                            self.wait_xhatf(tt, gate, sc == 0)
                            gate.wait(tt, qup, wn)
                            self.ps_wait(tt, gate, "psA", [bank])
                            for c in range(8):
                                ins = tt.matmul(
                                    self.psA[:, bank * 512:(bank + 1) * 512],
                                    self.arena[:, self.o_w + c * 1024 + fcl * 128:
                                               self.o_w + c * 1024 + fcl * 128 + 128],
                                    xf(c)[:, t0 + th * 512:t0 + th * 512 + 512],
                                    start=(c == 0), stop=(c == 7))
                            p.cPE.inc(ins)
                        p.pe(t_up)
                        if pending_sq is not None:
                            p.pe(pending_sq)
                            pending_sq = None
                        pe_n = p.cPE.n
                        def a_g(aa, gate, fc=fc, th=th, bank=bank, pe_n=pe_n):
                            gate.wait(aa, p.cPE, pe_n)
                            p.cACT.inc(aa.activation(
                                Ht(fc)[:, th * 512:th * 512 + 512],
                                self.psA[:, bank * 512:(bank + 1) * 512], AF.Gelu))
                        p.act(a_g)
                        self.ps_set("psA", [bank], p.cACT)
                        ac_n = p.cACT.n
                        sqbuf = self.arena[:, self.o_pt + (fc % 2) * 1024:
                                           self.o_pt + (fc % 2) * 1024 + 512]
                        def v_sq(v, gate, fc=fc, th=th, ac_n=ac_n, sqbuf=sqbuf):
                            gate.wait(v, p.cACT, ac_n)
                            gate.wait(v, p.cPE)
                            p.cDVE.inc(v.tensor_mul(sqbuf,
                                                    Ht(fc)[:, th * 512:th * 512 + 512],
                                                    Ht(fc)[:, th * 512:th * 512 + 512]))
                        p.dve(v_sq)
                        dv_n = p.cDVE.n
                        def t_sq(tt, gate, fc=fc, th=th, dv_n=dv_n, sqbuf=sqbuf):
                            gate.wait(tt, p.cDVE, dv_n)
                            if fc == 0 and th == 0:
                                self.ps_wait(tt, gate, "psB", [0])
                            p.cPE.inc(tt.matmul(self.psB[0:1, th * 512:th * 512 + 512],
                                                onescell, sqbuf,
                                                start=(fc == 0), stop=(fc == 31)))
                        pending_sq = t_sq
            p.pe(pending_sq)
            pending_sq = None
            self.slot_consumed([self.o_w, self.o_w2], p.cACT)
            pe_n = p.cPE.n
            def a_r2(aa, gate, pe_n=pe_n):
                gate.wait(aa, p.cPE, pe_n)
                aa.activation(self.grow[:, 0:1024], self.psB[0:1, 0:1024], AF.Ln)
                aa.activation(self.grow[:, 0:1024], self.grow[:, 0:1024], AF.Exp,
                              scale=0.5)
                aa.activation(self.grow[:, 0:1024], self.grow[:, 0:1024], AF.Ln,
                              bias=self.cpool[0:1, 2:3], scale=1.0 / 64.0)
                aa.activation(self.grow[:, 0:1024], self.grow[:, 0:1024], AF.Exp,
                              scale=-1.0)
                p.fence(aa, p.cACT)
            p.act(a_r2)
            self.ps_set("psB", [0], p.cACT)
            dv_n = p.cACT.n
            def s_r2(sy, gate, sc=sc, dv_n=dv_n):
                gate.wait(sy, p.cACT, dv_n)
                gate.wait(sy, p.cPE)
                gate.wait(sy, qr2)
                qr2.inc(sy.dma_start(self.rows_d[0:1, t0:t0 + 1024],
                                     self.grow[:, 0:1024]), 16)
                gate.wait(sy, qr2)
                qr2.inc(sy.dma_start(
                    self.smalls[:, 48:56],
                    self.rows_d[0, t0:t0 + 1024].rearrange("(n p) -> p n", p=128),
                    allow_slow_non_contiguous=True), 16)
            p.sp(s_r2)
            r2n = qr2.n
            dnbufs = [self.arena[:, self.o_misc:self.o_misc + 1024],
                      self.arena[:, self.o_misc + 1024:self.o_misc + 2048]]
            for half in range(2):
                pe_before = p.cPE.n
                dmark = {fc: pe_before + fc + 1 for fc in range(32)}
                qdn_base = [qdn[0].n, qdn[1].n]
                def s_dn(sy, gate, half=half):
                    self.slot_guard(sy, gate, [self.o_misc], qdn[0])
                    gate.wait(sy, qdn[1])
                    for fc in range(32):
                        par = fc % 2
                        if fc >= 2:
                            gate.wait(sy, p.cPE, dmark[fc - 2])
                        else:
                            gate.wait(sy, p.cDVE)
                        gate.wait(sy, qdn[par])
                        qdn[par].inc(sy.dma_start(
                            dnbufs[par],
                            self.cache["wdown"][0:1024, fc * 128:(fc + 1) * 128],
                            transpose=True), 16)
                p.sp(s_dn)
                self.slot_loaded([self.o_misc], qdn[0])
                def t_dn(tt, gate, half=half, qdn_base=qdn_base):
                    gate.wait(tt, p.cACT)
                    for fc in range(32):
                        par = fc % 2
                        gate.wait(tt, qdn[par], qdn_base[par] + 16 * (fc // 2 + 1))
                        if fc == 0:
                            self.ps_wait(tt, gate, "psA", [0, 1, 2, 3])
                            self.ps_wait(tt, gate, "psB", [0, 1, 2, 3])
                        for j in range(4):
                            ps = self.psA if j < 2 else self.psB
                            cb = (j % 2) * 1024
                            for eh in range(2):
                                ins = tt.matmul(
                                    ps[:, cb + eh * 512:cb + eh * 512 + 512],
                                    Ht(fc)[:, (half * 4 + j) * 128:
                                           (half * 4 + j) * 128 + 128],
                                    dnbufs[par][:, eh * 512:eh * 512 + 512],
                                    start=(fc == 0), stop=(fc == 31))
                        p.cPE.inc(ins)
                p.pe(t_dn)
                pe_dn = p.cPE.n
                for j in range(4):
                    tq = sc * 8 + half * 4 + j
                    ps = self.psA if j < 2 else self.psB
                    cb = (j % 2) * 1024
                    def v_e(v, gate, j=j, tq=tq, ps=ps, cb=cb):
                        gate.wait(v, p.cPE, pe_dn)
                        gate.wait(v, qr2, r2n)
                        p.cDVE.inc(v.scalar_tensor_tensor(
                            self.x_res[:, tq, :], ps[:, cb:cb + 1024],
                            self.smalls[:, 48 + (tq - sc * 8):49 + (tq - sc * 8)],
                            self.x_res[:, tq, :], ALU.mult, ALU.add))
                    p.dve(v_e)
                self.ps_set("psA", [0, 1, 2, 3], p.cDVE)
                self.ps_set("psB", [0, 1, 2, 3], p.cDVE)
            self.slot_consumed([self.o_misc], p.cDVE)

    def emit_l1ffn(self):
        p = self.p
        self.ln_phase("plain")
        xf = lambda c: self.av(self.o_xhatf, c)
        qwf = p.sem("wl1f")
        qrow = p.sem("rowl1f")
        ac_n, dv_n = p.cACT.n, p.cDVE.n
        def g_b(gp, gate):
            gate.wait(gp, p.cPE); gate.wait(gp, p.cACT, ac_n); gate.wait(gp, p.cDVE, dv_n)
            gate.wait(gp, qrow)
            qrow.inc(gp.dma_start(out=self.pool[0:1, 3072:4096],
                                  in_=self.W["L1_ffn_bias"][None, :]), 16)
        p.pool(g_b)
        self.build_bcast(self.bb, self.pool[0:1, 3072:4096], 1024,
                         gate_sem=(qrow, qrow.n))
        def s_w(sy, gate):
            self.slot_guard(sy, gate, [self.o_w, self.o_w2], qwf)
            for s_, n_ in [(p.qP[0], p.qP[0].n), (p.qP[1], p.qP[1].n)]:
                gate.wait(sy, s_, n_)
            for c in range(8):
                qwf.inc(sy.dma_start(
                    self.arena[:, self.o_w + c * 1024:self.o_w + (c + 1) * 1024],
                    self.cache["l1f"][0:1024, c * 128:(c + 1) * 128],
                    transpose=True), 16)
        p.sp(s_w)
        self.slot_loaded([self.o_w, self.o_w2], qwf)
        wn = qwf.n
        for tt in range(NT):
            banks = [0, 1] if tt % 2 == 0 else [2, 3]
            cb = banks[0] * 512
            def t(t_, gate, tt=tt, banks=banks, cb=cb):
                self.wait_xhatf(t_, gate, tt < 8)
                gate.wait(t_, qwf, wn)
                self.ps_wait(t_, gate, "psA", banks)
                for eh in range(2):
                    for c in range(8):
                        ins = t_.matmul(
                            self.psA[:, cb + eh * 512:cb + eh * 512 + 512],
                            xf(c)[:, tt * 128:(tt + 1) * 128],
                            self.arena[:, self.o_w + c * 1024 + eh * 512:
                                       self.o_w + c * 1024 + eh * 512 + 512],
                            start=(c == 0), stop=(c == 7))
                    p.cPE.inc(ins)
            p.pe(t)
            pe_n = p.cPE.n
            def v_e(v, gate, tt=tt, cb=cb, pe_n=pe_n):
                gate.wait(v, p.cPE, pe_n)
                gate.wait(v, p.cACT)
                v.tensor_add(self.zb, self.psA[:, cb:cb + 1024], self.bb)
                v.tensor_scalar(self.zb, self.zb, 100.0, -100.0, ALU.min, ALU.max)
                p.cDVE.inc(v.tensor_add(self.x_res[:, tt, :], self.x_res[:, tt, :],
                                        self.zb))
            p.dve(v_e)
            self.ps_set("psA", banks, p.cDVE)
        self.slot_consumed([self.o_w, self.o_w2], p.cDVE)

    def emit_l1b(self):
        p = self.p
        qrow = p.sem("rowl1b")
        ac0, dv0 = p.cACT.n, p.cDVE.n
        def g_r(gp, gate):
            gate.wait(gp, p.cPE); gate.wait(gp, p.cACT, ac0); gate.wait(gp, p.cDVE, dv0)
            gate.wait(gp, qrow)
            qrow.inc(gp.dma_start(out=self.pool[0:1, 3072:4096],
                                  in_=self.W["L1_attn_b_norm1_g"][None, :]), 16)
        p.pool(g_r)
        self.build_bcast(self.bb, self.pool[0:1, 3072:4096], 1024,
                         gate_sem=(qrow, qrow.n))
        def g_r2(gp, gate):
            gate.wait(gp, p.cPE); gate.wait(gp, qrow)
            qrow.inc(gp.dma_start(out=self.pool[0:1, 3072:4096],
                                  in_=self.W["L1_attn_b_norm1_b"][None, :]), 16)
        p.pool(g_r2)
        self.build_bcast(self.bb2, self.pool[0:1, 3072:4096], 1024,
                         gate_sem=(qrow, qrow.n))
        self.ln_phase("l1b", gmul=self.bb, badd=self.bb2)
        self.emit_vones()
        for hf in range(2):
            self.l1b_qkv(hf)
            for hl in range(8):
                self.l1b_head(hf, hl)
        self.l1b_proj_mid()
        self.l1b_mlp()

    def l1b_qkv(self, hf):
        p = self.p
        xf = lambda c: self.av(self.o_xhatf, c)
        for pi, proj in enumerate(("q", "k", "v")):
            wslot = self.o_w if pi % 2 == 0 else self.o_w2
            sl = "0" if wslot == self.o_w else "1"
            qw = p.sem("w" + sl + "s")
            rowbase = pi * 1024 + hf * 512
            def s(sy, gate, rowbase=rowbase, wslot=wslot, qw=qw):
                self.slot_guard(sy, gate, [wslot], qw)
                gate.wait(sy, p.qC, self.cache_qg)
                for c in range(8):
                    qw.inc(sy.dma_start(
                        self.arena[:, wslot + c * 512: wslot + (c + 1) * 512],
                        self.cache["qkv1"][rowbase:rowbase + 512,
                                           c * 128:(c + 1) * 128],
                        transpose=True), 16)
            p.sp(s)
            self.slot_loaded([wslot], qw)
            wready = (qw, qw.n)
            wbuf = lambda c, s_=wslot: self.arena[:, s_ + c * 512: s_ + (c + 1) * 512]
            if proj == "q":
                self.proj_feature_major(wbuf, wready, self.o_q, "l1b", pi, hf)
                self.slot_consumed([wslot], p.cACT)
                for ti in range(4):
                    for ch in range(2):
                        ap = self.arena[:, self.o_q + ti * T + ch * 1024:
                                        self.o_q + ti * T + ch * 1024 + 1024]
                        ac_n = p.cACT.n
                        def v_1(v, gate, ap=ap, ac_n=ac_n):
                            gate.wait(v, p.cACT, ac_n)
                            p.cDVE.inc(v.tensor_scalar_min(self.zb, ap, 0.0))
                        p.dve(v_1)
                        dv_n = p.cDVE.n
                        def a_1(aa, gate, dv_n=dv_n):
                            gate.wait(aa, p.cDVE, dv_n)
                            p.cACT.inc(aa.activation(self.zb, self.zb, AF.Exp))
                        p.act(a_1)
                        ac_n2 = p.cACT.n
                        def v_2(v, gate, ap=ap, ac_n2=ac_n2):
                            gate.wait(v, p.cACT, ac_n2)
                            p.cDVE.inc(v.scalar_tensor_tensor(ap, ap, 1.0, self.zb,
                                                              ALU.add, ALU.max))
                        p.dve(v_2)
            else:
                for kt in range(NT):
                    bank = kt % 4
                    def t(tt, gate, kt=kt, bank=bank, wready=wready):
                        self.wait_xhatf(tt, gate, kt < 8)
                        gate.wait(tt, wready[0], wready[1])
                        gate.wait(tt, p.cDVE)
                        self.ps_wait(tt, gate, "psB", [bank])
                        for c in range(8):
                            ins = tt.matmul(self.psB[:, bank * 512:(bank + 1) * 512],
                                            xf(c)[:, kt * 128:(kt + 1) * 128],
                                            wbuf(c), start=(c == 0), stop=(c == 7))
                        p.cPE.inc(ins)
                    p.pe(t)
                    pe_n = p.cPE.n
                    if proj == "k":
                        dst = self.arena[:, self.o_k + kt * 512:self.o_k + (kt + 1) * 512]
                        def v_e(v, gate, kt=kt, bank=bank, pe_n=pe_n, dst=dst):
                            gate.wait(v, p.cPE, pe_n)
                            p.cDVE.inc(v.tensor_scalar_min(
                                self.zb[:, 0:512], self.psB[:, bank * 512:(bank + 1) * 512], 0.0))
                        p.dve(v_e)
                        dv_n = p.cDVE.n
                        def a_e(aa, gate, dv_n=dv_n):
                            gate.wait(aa, p.cDVE, dv_n)
                            p.cACT.inc(aa.activation(self.zb[:, 0:512], self.zb[:, 0:512],
                                                     AF.Exp))
                        p.act(a_e)
                        ac_n = p.cACT.n
                        def v_e2(v, gate, kt=kt, bank=bank, dst=dst, ac_n=ac_n):
                            gate.wait(v, p.cACT, ac_n)
                            p.cDVE.inc(v.scalar_tensor_tensor(
                                dst, self.psB[:, bank * 512:(bank + 1) * 512], 1.0,
                                self.zb[:, 0:512], ALU.add, ALU.max))
                        p.dve(v_e2)
                        self.ps_set("psB", [bank], p.cDVE)
                    else:
                        dst = self.arena[:, self.o_v + kt * 520:self.o_v + (kt + 1) * 520] \
                            .rearrange("p (h c) -> p h c", h=8)[:, :, 0:64]
                        def v_e(v, gate, kt=kt, bank=bank, pe_n=pe_n, dst=dst):
                            gate.wait(v, p.cPE, pe_n)
                            p.cDVE.inc(v.tensor_copy(
                                dst, self.psB[:, bank * 512:(bank + 1) * 512]
                                .rearrange("p (h c) -> p h c", h=8)))
                        p.dve(v_e)
                        self.ps_set("psB", [bank], p.cDVE)
                self.slot_consumed([wslot], p.cDVE)

    def l1b_head(self, hf, hl):
        p = self.p
        qtile, qrow = hl // 2, (hl % 2) * 64
        pair = hl // 2
        kvsb = self.arena[:, self.o_pt + 3072:self.o_pt + 3137]
        def t_kv(tt, gate):
            gate.wait(tt, p.cDVE)
            gate.wait(tt, p.cACT)
            self.ps_wait(tt, gate, "psA", [0])
            for kt in range(NT):
                ins = tt.matmul(self.psA[0:64, 0:65],
                                self.arena[:, self.o_k + kt * 512 + hl * 64:
                                           self.o_k + kt * 512 + hl * 64 + 64],
                                self.arena[:, self.o_v + kt * 520 + hl * 65:
                                           self.o_v + kt * 520 + hl * 65 + 65],
                                start=(kt == 0), stop=(kt == NT - 1))
            p.cPE.inc(ins)
        p.pe(t_kv)
        pe_n = p.cPE.n
        def a_kv(aa, gate, pe_n=pe_n):
            gate.wait(aa, p.cPE, pe_n)
            gate.wait(aa, p.cDVE)
            aa.copy(kvsb[0:64, :], self.psA[0:64, 0:65])
            p.cACT.inc(aa.copy(kvsb[64:128, :], self.psA[0:64, 0:65]))
        p.act(a_kv)
        self.ps_set("psA", [0], p.cACT)
        ac_kv = p.cACT.n
        def t_att(tt, gate):
            gate.wait(tt, p.cACT, ac_kv)
            self.ps_wait(tt, gate, "psB", [0, 1, 2, 3])
            for tc in range(4):
                ins = tt.matmul(self.psB[0:65, tc * 512:(tc + 1) * 512],
                                kvsb[qrow:qrow + 64, :],
                                self.arena[qrow:qrow + 64,
                                           self.o_q + qtile * T + tc * 512:
                                           self.o_q + qtile * T + tc * 512 + 512],
                                start=True, stop=True)
            p.cPE.inc(ins)
        p.pe(t_att)
        pe_att = p.cPE.n
        for qh in range(2):
            pe_prev = p.cPE.n
            def v_r(a, gate, qh=qh, pe_prev=pe_prev):
                gate.wait(a, p.cPE, max(pe_att, pe_prev))
                a.activation(self.grow[:, 0:1024],
                             self.psB[64:65, qh * 1024:(qh + 1) * 1024], AF.Ln,
                             bias=self.cpool[0:1, 3:4])
                p.cACT.inc(a.activation(self.growb[:, :], self.grow[:, 0:1024],
                                        AF.Exp, scale=-1.0))
            p.act(v_r)
            dv_n = p.cACT.n
            def t_z(tt, gate, qh=qh, dv_n=dv_n):
                gate.wait(tt, p.cACT, dv_n)
                self.ps_wait(tt, gate, "psA", [2, 3])
                p.cPE.inc(tt.matmul(self.psA[0:64, 1024:1536], self.onesbf[:, :],
                                    self.growb[:, 0:512], start=True, stop=True))
                p.cPE.inc(tt.matmul(self.psA[0:64, 1536:2048], self.onesbf[:, :],
                                    self.growb[:, 512:1024], start=True, stop=True))
            p.pe(t_z)
            pe_n2 = p.cPE.n
            base = (self.o_osp if hf == 0 else self.o_xhatf)
            dst = self.arena[qrow:qrow + 64,
                             base + pair * T + qh * 1024:base + pair * T + qh * 1024 + 1024]
            def v_m(v, gate, qh=qh, dst=dst, pe_n2=pe_n2):
                gate.wait(v, p.cPE, pe_n2)
                v.tensor_copy(self.zb[0:64, :], self.psA[0:64, 1024:2048])
                p.cDVE.inc(v.tensor_mul(dst, self.psB[0:64, qh * 1024:(qh + 1) * 1024],
                                        self.zb[0:64, :]))
            p.dve(v_m)
            self.ps_set("psA", [2, 3], p.cDVE)
        self.ps_set("psB", [0, 1, 2, 3], p.cDVE)

    def l1b_proj_mid(self):
        p = self.p
        st = self.stats
        qw = p.sem("w0s")
        qrow = p.sem("rowl1b")
        ac0 = p.cACT.n
        def g_r(gp, gate):
            gate.wait(gp, p.cPE); gate.wait(gp, p.cACT, ac0); gate.wait(gp, p.cDVE)
            gate.wait(gp, qrow)
            qrow.inc(gp.dma_start(out=self.pool[0:1, 3072:4096],
                                  in_=self.W["L1_attn_b_proj_b"][None, :]), 16)
        p.pool(g_r)
        self.build_bcast(self.bb2, self.pool[0:1, 3072:4096], 1024,
                         gate_sem=(qrow, qrow.n))
        self.g2b = self.arena[:, self.o_v:self.o_v + 1024]
        self.b2b = self.arena[:, self.o_v + 1024:self.o_v + 2048]
        def g_r2(gp, gate):
            gate.wait(gp, p.cPE); gate.wait(gp, qrow)
            qrow.inc(gp.dma_start(out=self.pool[0:1, 3072:4096],
                                  in_=self.W["L1_attn_b_norm2_g"][None, :]), 16)
        p.pool(g_r2)
        self.build_bcast(self.g2b, self.pool[0:1, 3072:4096], 1024,
                         gate_sem=(qrow, qrow.n))
        def g_r3(gp, gate):
            gate.wait(gp, p.cPE); gate.wait(gp, qrow)
            qrow.inc(gp.dma_start(out=self.pool[0:1, 3072:4096],
                                  in_=self.W["L1_attn_b_norm2_b"][None, :]), 16)
        p.pool(g_r3)
        self.build_bcast(self.b2b, self.pool[0:1, 3072:4096], 1024,
                         gate_sem=(qrow, qrow.n))
        def s_w(sy, gate):
            self.slot_guard(sy, gate, [self.o_w, self.o_w2], qw)
            gate.wait(sy, p.qC, self.cache_qg)
            for j in range(8):
                slot = self.o_w if j < 4 else self.o_w2
                qw.inc(sy.dma_start(
                    self.arena[:, slot + (j % 4) * 1024:slot + (j % 4) * 1024 + 1024],
                    self.cache["proj1"][0:1024, j * 128:(j + 1) * 128],
                    transpose=True), 16)
        p.sp(s_w)
        self.slot_loaded([self.o_w, self.o_w2], qw)
        wn = qw.n
        for tt in range(NT):
            banks = [0, 1] if tt % 2 == 0 else [2, 3]
            cb = banks[0] * 512
            def t(t_, gate, tt=tt, banks=banks, cb=cb):
                gate.wait(t_, qw, wn)
                gate.wait(t_, p.cDVE)
                self.ps_wait(t_, gate, "psA", banks)
                for eh in range(2):
                    for j in range(8):
                        base = (self.o_osp if j < 4 else self.o_xhatf)
                        ins = t_.matmul(
                            self.psA[:, cb + eh * 512:cb + eh * 512 + 512],
                            self.arena[:, base + (j % 4) * T + tt * 128:
                                       base + (j % 4) * T + tt * 128 + 128],
                            self.arena[:, (self.o_w if j < 4 else self.o_w2) +
                                       (j % 4) * 1024 + eh * 512:
                                       (self.o_w if j < 4 else self.o_w2) +
                                       (j % 4) * 1024 + eh * 512 + 512],
                            start=(j == 0), stop=(j == 7))
                    p.cPE.inc(ins)
            p.pe(t)
            pe_n = p.cPE.n
            def a_h(aa, gate, tt=tt):
                gate.wait(aa, p.cDVE)
                p.cACT.inc(aa.activation(self.zb, self.x_res[:, tt, :], AF.Identity,
                                         bias=st[:, 96 + tt:97 + tt],
                                         scale=st[:, 48 + tt:49 + tt]))
            p.act(a_h)
            ac_n = p.cACT.n
            def v_e(v, gate, tt=tt, cb=cb, pe_n=pe_n, ac_n=ac_n):
                gate.wait(v, p.cPE, pe_n)
                gate.wait(v, p.cACT, ac_n)
                v.tensor_add(self.zb, self.zb, self.psA[:, cb:cb + 1024])
                v.tensor_add(self.zb, self.zb, self.bb2)
                v.tensor_add(self.x_res[:, tt, :], self.x_res[:, tt, :], self.zb)
                v.bn_stats(self.bnb[:, 0:6], self.zb[:, 0:512])
                v.bn_stats(self.bnb[:, 6:12], self.zb[:, 512:1024])
                p.cDVE.inc(v.bn_aggr(st[:, 32 + 2 * tt % 32:0] if False else
                                     self.stats2[:, 2 * tt:2 * tt + 2],
                                     self.bnb[:, 0:12]))
            p.dve(v_e)
            self.ps_set("psA", banks, p.cDVE)
            dv_n = p.cDVE.n
            def a_h2(aa, gate, tt=tt, dv_n=dv_n):
                gate.wait(aa, p.cDVE, dv_n)
                aa.activation(self.stats2[:, 32 + tt:33 + tt],
                              self.stats2[:, 2 * tt + 1:2 * tt + 2], AF.Ln,
                              bias=self.c_lneps)
                aa.activation(self.stats2[:, 32 + tt:33 + tt],
                              self.stats2[:, 32 + tt:33 + tt], AF.Exp, scale=-0.5)
                p.fence(aa, p.cACT)
            p.act(a_h2)
            ac_n2 = p.cACT.n
            def v_h2(v, gate, tt=tt, ac_n2=ac_n2):
                gate.wait(v, p.cACT, ac_n2)
                v.tensor_mul(self.stats2[:, 48 + tt:49 + tt],
                             self.stats2[:, 2 * tt:2 * tt + 1],
                             self.stats2[:, 32 + tt:33 + tt])
                p.cDVE.inc(v.tensor_scalar_mul(self.stats2[:, 48 + tt:49 + tt],
                                               self.stats2[:, 48 + tt:49 + tt], -1.0))
            p.dve(v_h2)
            dv_n2 = p.cDVE.n
            bufs = [self.arena[:, self.o_pt + i * 1024:self.o_pt + (i + 1) * 1024]
                    for i in range(2)]
            qha = [p.sem("xh0"), p.sem("xh1")]
            def a_h3(aa, gate, tt=tt, dv_n2=dv_n2):
                gate.wait(aa, p.cDVE, dv_n2)
                gate.wait(aa, qha[tt % 2])
                p.cACT.inc(aa.activation(bufs[tt % 2], self.zb, AF.Identity,
                                         bias=self.stats2[:, 48 + tt:49 + tt],
                                         scale=self.stats2[:, 32 + tt:33 + tt]))
            p.act(a_h3)
            ac_n3 = p.cACT.n
            if tt == 0:
                pass
            def v_h3(v, gate, tt=tt, ac_n3=ac_n3):
                gate.wait(v, p.cACT, ac_n3)
                v.tensor_mul(bufs[tt % 2], bufs[tt % 2], self.g2b)
                p.cDVE.inc(v.tensor_add(bufs[tt % 2], bufs[tt % 2], self.b2b))
            p.dve(v_h3)
            dv_n3 = p.cDVE.n
            def s_st(sy, gate, tt=tt, dv_n3=dv_n3):
                gate.wait(sy, p.cDVE, dv_n3)
                qha[tt % 2].inc(sy.dma_start(
                    out=self.xhat_d[tt * 128:(tt + 1) * 128, :], in_=bufs[tt % 2]), 16)
            p.sp(s_st)
        qha = [p.sem("xh0"), p.sem("xh1")]
        def s_tr(sy, gate):
            gate.wait(sy, qha[0]); gate.wait(sy, qha[1])
            for c in range(DCH):
                p.qH.inc(sy.dma_start(
                    self.arena[:, self.o_osp + c * T:self.o_osp + (c + 1) * T]
                    if c < 4 else
                    self.arena[:, self.o_q + (c - 4) * T:self.o_q + (c - 3) * T],
                    self.xhat_d[:, c * 128:(c + 1) * 128], transpose=True), 16)
        p.sp(s_tr)
        self.h2f_sems = [(p.qH, p.qH.n), (p.sem("xh0"), p.sem("xh0").n),
                         (p.sem("xh1"), p.sem("xh1").n)]

    def l1b_mlp(self):
        p = self.p
        h2f = lambda c: (self.arena[:, self.o_osp + c * T:self.o_osp + (c + 1) * T]
                         if c < 4 else
                         self.arena[:, self.o_q + (c - 4) * T:self.o_q + (c - 3) * T])
        oH = self.o_k
        Ht = lambda fc: self.arena[:, oH + fc * 512: oH + (fc + 1) * 512]
        qf1 = p.sem("wfc1")
        qf2 = [p.sem("wfc2a"), p.sem("wfc2b")]
        qrow = p.sem("rowl1b")
        dv0 = p.cDVE.n
        def g_b(gp, gate):
            gate.wait(gp, p.cDVE, dv0); gate.wait(gp, p.cPE); gate.wait(gp, qrow)
            qrow.inc(gp.dma_start(out=self.smalls[:, 32:64],
                                  in_=self.W["L1_attn_b_fc1_b"]
                                  .rearrange("(c p) -> p c", p=128),
                                  allow_slow_non_contiguous=True), 16)
            qrow.inc(gp.dma_start(out=self.pool[0:1, 3072:4096],
                                  in_=self.W["L1_attn_b_fc2_b"][None, :]), 16)
        p.pool(g_b)
        qrow_n = qrow.n
        self.build_bcast(self.bb2, self.pool[0:1, 3072:4096], 1024,
                         gate_sem=(qrow, qrow.n))
        dnbufs = [self.arena[:, self.o_misc:self.o_misc + 1024],
                  self.arena[:, self.o_misc + 1024:self.o_misc + 2048]]
        for sc in range(4):
            t0 = sc * 512
            for g in range(4):
                acn = p.cACT.n
                def s_w(sy, gate, g=g, acn=acn):
                    self.slot_guard(sy, gate, [self.o_w, self.o_w2], qf1)
                    gate.wait(sy, p.qC, self.cache_qg)
                    gate.wait(sy, p.cACT, acn)
                    for c in range(8):
                        qf1.inc(sy.dma_start(
                            self.arena[:, self.o_w + c * 1024:self.o_w + (c + 1) * 1024],
                            self.cache["fc1"][g * 1024:(g + 1) * 1024,
                                              c * 128:(c + 1) * 128],
                            transpose=True), 16)
                p.sp(s_w)
                self.slot_loaded([self.o_w, self.o_w2], qf1)
                wn = qf1.n
                for fcl in range(8):
                    fc = g * 8 + fcl
                    bank = fc % 4
                    def t_up(tt, gate, fc=fc, fcl=fcl, bank=bank, wn=wn, t0=t0):
                        [gate.wait(tt, s_, n_) for s_, n_ in self.h2f_sems]
                        gate.wait(tt, qf1, wn)
                        gate.wait(tt, qrow, qrow_n)
                        self.ps_wait(tt, gate, "psA", [bank])
                        for c in range(8):
                            ins = tt.matmul(
                                self.psA[:, bank * 512:(bank + 1) * 512],
                                self.arena[:, self.o_w + c * 1024 + fcl * 128:
                                           self.o_w + c * 1024 + fcl * 128 + 128],
                                h2f(c)[:, t0:t0 + 512],
                                start=(c == 0), stop=(c == 7))
                        p.cPE.inc(ins)
                    p.pe(t_up)
                    pe_n = p.cPE.n
                    def a_g(aa, gate, fc=fc, bank=bank, pe_n=pe_n):
                        gate.wait(aa, p.cPE, pe_n)
                        p.cACT.inc(aa.activation(Ht(fc),
                                                 self.psA[:, bank * 512:(bank + 1) * 512],
                                                 AF.Gelu,
                                                 bias=self.smalls[:, 32 + fc:33 + fc]))
                    p.act(a_g)
                    self.ps_set("psA", [bank], p.cACT)
            self.slot_consumed([self.o_w, self.o_w2], p.cACT)
            pe_before = p.cPE.n
            dmark = {fc: pe_before + fc + 1 for fc in range(32)}
            qdn_base = [qf2[0].n, qf2[1].n]
            def s_dn(sy, gate, sc=sc):
                self.slot_guard(sy, gate, [self.o_misc], qf2[0])
                gate.wait(sy, qf2[1])
                gate.wait(sy, p.qC, self.cache_qg)
                for fc in range(32):
                    par = fc % 2
                    if fc >= 2:
                        gate.wait(sy, p.cPE, dmark[fc - 2])
                    else:
                        gate.wait(sy, p.cDVE)
                    gate.wait(sy, qf2[par])
                    qf2[par].inc(sy.dma_start(
                        dnbufs[par],
                        self.cache["fc2"][0:1024, fc * 128:(fc + 1) * 128],
                        transpose=True), 16)
            p.sp(s_dn)
            self.slot_loaded([self.o_misc], qf2[0])
            def t_dn(tt, gate, sc=sc, qdn_base=qdn_base):
                gate.wait(tt, p.cACT)
                for fc in range(32):
                    par = fc % 2
                    gate.wait(tt, qf2[par], qdn_base[par] + 16 * (fc // 2 + 1))
                    if fc == 0:
                        self.ps_wait(tt, gate, "psA", [0, 1, 2, 3])
                        self.ps_wait(tt, gate, "psB", [0, 1, 2, 3])
                    for j in range(4):
                        ps = self.psA if j < 2 else self.psB
                        cb = (j % 2) * 1024
                        for eh in range(2):
                            ins = tt.matmul(
                                ps[:, cb + eh * 512:cb + eh * 512 + 512],
                                Ht(fc)[:, j * 128:(j + 1) * 128],
                                dnbufs[par][:, eh * 512:eh * 512 + 512],
                                start=(fc == 0), stop=(fc == 31))
                    p.cPE.inc(ins)
            p.pe(t_dn)
            pe_dn = p.cPE.n
            for j in range(4):
                tq = sc * 4 + j
                ps = self.psA if j < 2 else self.psB
                cb = (j % 2) * 1024
                def v_e(v, gate, j=j, tq=tq, ps=ps, cb=cb, pe_dn=pe_dn):
                    gate.wait(v, p.cPE, pe_dn)
                    gate.wait(v, p.cACT)
                    v.tensor_add(self.x_res[:, tq, :], self.x_res[:, tq, :],
                                 ps[:, cb:cb + 1024])
                    p.cDVE.inc(v.tensor_add(self.x_res[:, tq, :],
                                            self.x_res[:, tq, :], self.bb2))
                p.dve(v_e)
            self.ps_set("psA", [0, 1, 2, 3], p.cDVE)
            self.ps_set("psB", [0, 1, 2, 3], p.cDVE)
            self.slot_consumed([self.o_misc], p.cDVE)


GROUPS = [(0, 8)]


def kernel(**inputs):
    consts = host_constants()
    x = np.ascontiguousarray(inputs["L0_attn_a_x"], dtype=np.float32)
    xs = x.reshape(NCORES, SEQ_PER_CORE, T, D)
    cur = [np.ascontiguousarray(xs[c]) for c in range(NCORES)]
    for (lo, hi) in GROUPS:
        b = Builder(lo, hi)
        in_maps = []
        for c in range(NCORES):
            m = {"x": cur[c]}
            for name in b.W:
                m[name] = np.ascontiguousarray(inputs[name], dtype=np.float32)
            if hasattr(b, "rope_c"):
                m["rope_c"] = consts["rope_c"]
                m["rope_s"] = consts["rope_s"]
                m["perm128"] = consts["perm128"]
            if hasattr(b, "cmask"):
                m["cmask"] = consts["cmask"]
            in_maps.append(m)
        res = run_bass_kernel_spmd(b.nc, in_maps, core_ids=list(range(NCORES)))
        cur = [res.results[c]["y"] for c in range(NCORES)]
    return np.concatenate(cur, axis=0).reshape(B, T, D).astype(np.float32)
